# version 12
# baseline (speedup 1.0000x reference)
import numpy as np
import ml_dtypes
from contextlib import ExitStack
import concourse.bass as bass
import concourse.mybir as mybir
from concourse.bass_utils import run_bass_kernel_spmd

F32 = mybir.dt.float32
BF16 = mybir.dt.bfloat16
AF = mybir.ActivationFunctionType
OP = mybir.AluOpType
AX = mybir.AxisListType
BF = ml_dtypes.bfloat16

ENG = ("sync", "scalar", "vector", "tensor", "gpsimd")
T = 2048
D = 1024
KCH = 8
EPS = 1e-5
SP = 2064


class PB:
    def __init__(self, nc):
        self.nc = nc
        self.items = {e: [] for e in ENG}
        self.cnt = {e: 0 for e in ENG}
        self.lw = {}
        self.rd = {}
        self.waited = {e: {o: 0 for o in ENG} for e in ENG}
        self.last_dma = {e: 0 for e in ENG}

    def emit(self, eng, fn, r=(), w=(), dma=False, risky=False):
        need = {}
        def add(tok):
            if tok is None:
                return
            te, tv, asy = tok
            if te == eng and not asy and eng == "tensor":
                return
            if self.waited[eng][te] >= tv:
                return
            if need.get(te, 0) < tv:
                need[te] = tv
        for reg in r:
            add(self.lw.get(reg))
        for reg in w:
            add(self.lw.get(reg))
            for t in self.rd.get(reg, ()):
                add(t)
        if dma and self.last_dma[eng] > 0 and self.waited[eng][eng] < self.last_dma[eng]:
            if need.get(eng, 0) < self.last_dma[eng]:
                need[eng] = self.last_dma[eng]
        inc = 16 if dma else 1
        self.cnt[eng] += inc
        tok = (eng, self.cnt[eng], dma or risky)
        if dma:
            self.last_dma[eng] = self.cnt[eng]
        for reg in w:
            self.lw[reg] = tok
            self.rd[reg] = []
        for reg in r:
            self.rd.setdefault(reg, []).append(tok)
        for te, tv in need.items():
            if self.waited[eng][te] < tv:
                self.waited[eng][te] = tv
        self.items[eng].append((need, fn, inc))

    def barrier(self):
        snap = {e: self.cnt[e] for e in ENG}
        for e in ENG:
            need = {o: snap[o] for o in ENG
                    if snap[o] > 0 and self.waited[e][o] < snap[o]}
            self.cnt[e] += 1
            self.items[e].append((need, lambda h: h.nop(), 1))
            for o, v in need.items():
                self.waited[e][o] = v

    def realize(self, blk, sems):
        self.barrier()
        for e in ENG:
            items = self.items[e]
            def mk(eng_name, items):
                def body(h):
                    for need, fn, inc in items:
                        for te in sorted(need):
                            h.wait_ge(sems[te], need[te])
                        ins = fn(h)
                        ins.then_inc(sems[eng_name], inc)
                return body
            getattr(blk, e)(mk(e, items))


class Ctx:
    def __init__(self, nc, pb, st):
        self.nc = nc
        self.pb = pb
        self.st = st
        self.uid = [0]

    def name(self, base):
        self.uid[0] += 1
        return f"{base}_{self.uid[0]}"

    def sb(self, base, shape, dtype, stack=None, side=None):
        kw = {} if side is None else {"side": side}
        return (stack or self.st).enter_context(
            self.nc.sbuf_tensor(self.name(base), shape, dtype, **kw))

    def ps(self, base, shape, stack=None, dtype=F32):
        return (stack or self.st).enter_context(
            self.nc.psum_tensor(self.name(base), shape, dtype))


def make_consts(cx):
    pb = cx.pb
    cx.ones_b = cx.sb("ones_b", [1, 512], BF16)
    cx.ones_f = cx.sb("ones_f", [1, 512], F32)
    cx.onec_b = cx.sb("onec_b", [128, 1], BF16)
    cx.cc = cx.sb("cc", [128, 8], F32)
    cx.wpool = [cx.sb("wpool", [128, 1040], F32) for i in range(2)]
    cx.wpool_i = 0
    cx.cr = cx.sb("cr", [1, 8], F32)
    cx.cc_map = {}
    cx.cr_map = {}
    pb.emit("vector", lambda h: h.memset(cx.ones_b[:], 1.0), w=["ones_b"])
    pb.emit("vector", lambda h: h.memset(cx.ones_f[:], 1.0), w=["ones_f"])
    pb.emit("vector", lambda h: h.memset(cx.onec_b[:], 1.0), w=["onec_b"])


def const_col(cx, val, rows=128):
    if val not in cx.cc_map:
        i = len(cx.cc_map)
        assert i < 8
        cx.cc_map[val] = i
        cx.pb.emit("vector", lambda h, i=i, val=val: h.memset(cx.cc[:, i:i + 1], float(val)), w=["cc"])
    i = cx.cc_map[val]
    return cx.cc[0:rows, i:i + 1]


def const_row(cx, val):
    if val not in cx.cr_map:
        i = len(cx.cr_map)
        assert i < 8
        cx.cr_map[val] = i
        cx.pb.emit("vector", lambda h, i=i, val=val: h.memset(cx.cr[0:1, i:i + 1], float(val)), w=["cr"])
    i = cx.cr_map[val]
    return cx.cr[0:1, i:i + 1]


class XStream:
    def __init__(self, cx, dram_ap, stack, nbuf=2, tag="xs"):
        self.cx = cx
        self.dram = dram_ap
        self.bufs = [cx.sb(tag, [128, T], F32, stack) for i in range(nbuf)]
        self.tag = tag
        self.nbuf = nbuf
        self.n = 0

    def fetch(self, c):
        i = self.n % self.nbuf
        self.n += 1
        buf = self.bufs[i]
        reg = f"{self.tag}{i}"
        self.cx.pb.emit("scalar", lambda h, buf=buf, c=c: h.dma_start(
            out=buf[:], in_=self.dram[128 * c:128 * (c + 1), :]), w=[reg], dma=True)
        return buf, reg


def emit_ln_stats(cx, fetch, rstd_row, a_row, rows_reg, ps_stack, bf16_src, off=0):
    pb = cx.pb
    offf = off if callable(off) else (lambda c: off)
    R, A = rstd_row, a_row
    rr, ra = rows_reg + "_r", rows_reg + "_a"
    srow = cx.sb("lnsrow", [1, T], F32, ps_stack)
    sreg = "lnsrow"
    pb.emit("vector", lambda h: h.memset(R[:], 0.0), w=[rr])
    pb.emit("vector", lambda h: h.memset(A[:], 0.0), w=[ra])
    ps1v = [cx.ps("ps_st1", [1, 512], ps_stack) for i in range(2)]
    ps2v = [cx.ps("ps_st2", [1, 512], ps_stack) for i in range(2)]
    castv = [cx.sb("lncast", [128, 512], BF16, ps_stack) for i in range(2)]
    sqv = [cx.sb("lnsq", [128, 512], BF16, ps_stack) for i in range(2)]
    for c in range(KCH):
        tile, reg = fetch(c)
        for nch in range(4):
            lo = nch * 512
            pp = (4 * c + nch) % 2
            ps1, p1r = ps1v[pp], f"ps_st1{pp}"
            ps2, p2r = ps2v[pp], f"ps_st2{pp}"
            cast, creg_ = castv[pp], f"lncast{pp}"
            sq, sqreg = sqv[pp], f"lnsq{pp}"
            src_ap = tile[:, offf(c) + lo:offf(c) + lo + 512]
            if bf16_src:
                xcb, xreg = src_ap, reg
            else:
                pb.emit("vector", lambda h, a=src_ap, cast=cast: h.tensor_copy(out=cast[:], in_=a),
                        r=[reg], w=[creg_])
                xcb, xreg = cast[:], creg_
            pb.emit("scalar", lambda h, a=xcb, sq=sq: h.activation(out=sq[:], in_=a, func=AF.Square),
                    r=[xreg], w=[sqreg])
            pb.emit("tensor", lambda h, a=xcb, ps1=ps1: h.matmul(ps1[:], cx.onec_b[:], a,
                                                        start=True, stop=True, skip_group_check=True),
                    r=[xreg, "onec_b"], w=[p1r])
            pb.emit("tensor", lambda h, ps2=ps2, sq=sq: h.matmul(ps2[:], cx.onec_b[:], sq[:],
                                                 start=True, stop=True, skip_group_check=True),
                    r=[sqreg, "onec_b"], w=[p2r])
            pb.emit("vector", lambda h, lo=lo, ps1=ps1: h.tensor_tensor(
                out=R[0:1, lo:lo + 512], in0=R[0:1, lo:lo + 512], in1=ps1[:], op=OP.add),
                r=[rr, p1r], w=[rr])
            pb.emit("vector", lambda h, lo=lo, ps2=ps2: h.tensor_tensor(
                out=A[0:1, lo:lo + 512], in0=A[0:1, lo:lo + 512], in1=ps2[:], op=OP.add),
                r=[ra, p2r], w=[ra])
    pb.emit("vector", lambda h: h.tensor_scalar(out=R[:], in0=R[:], scalar1=1.0 / D,
                                                scalar2=None, op0=OP.mult), r=[rr], w=[rr])
    pb.emit("vector", lambda h: h.tensor_tensor(out=srow[:], in0=R[:], in1=R[:], op=OP.mult),
            r=[rr], w=[sreg])
    pb.emit("vector", lambda h: h.tensor_scalar(out=A[:], in0=A[:], scalar1=1.0 / D,
                                                scalar2=None, op0=OP.mult), r=[ra], w=[ra])
    pb.emit("vector", lambda h: h.tensor_tensor(out=A[:], in0=A[:], in1=srow[:], op=OP.subtract),
            r=[ra, sreg], w=[ra])
    epsap = const_row(cx, EPS)
    pb.emit("scalar", lambda h, epsap=epsap: h.activation(out=A[:], in_=A[:], func=AF.Ln, bias=epsap),
            r=[ra, "cr"], w=[ra])
    pb.emit("scalar", lambda h: h.activation(out=A[:], in_=A[:], func=AF.Exp, scale=-0.5),
            r=[ra], w=[ra])
    pb.emit("vector", lambda h: h.tensor_tensor(out=srow[:], in0=R[:], in1=A[:], op=OP.mult),
            r=[rr, ra], w=[sreg])
    pb.emit("vector", lambda h: h.tensor_copy(out=R[:], in_=A[:]), r=[ra], w=[rr])
    pb.emit("vector", lambda h: h.tensor_scalar(out=A[:], in0=srow[:], scalar1=-1.0,
                                                scalar2=None, op0=OP.mult),
            r=[sreg], w=[ra])


def emit_bcast_rows(cx, rstd_row, a_row, rows_reg, rb_sb, ab_sb, ps_stack):
    pb = cx.pb
    bcv = [cx.ps("ps_bcr", [128, 512], ps_stack) for i in range(2)]
    for ri, (row, sb_t, reg) in enumerate([(rstd_row, rb_sb, rows_reg + "rb"),
                                           (a_row, ab_sb, rows_reg + "ab")]):
        for nch in range(4):
            lo = nch * 512
            pp = (4 * ri + nch) % 2
            bc, bcr = bcv[pp], f"ps_bcr{pp}"
            rap = row[0:1, lo:lo + 512]
            pb.emit("tensor", lambda h, rap=rap, bc=bc: h.matmul(bc[:], cx.ones_f[0:1, 0:128], rap,
                                                          start=True, stop=True, skip_group_check=True),
                    r=["ones_f", rows_reg + "_r", rows_reg + "_a"], w=[bcr])
            pb.emit("scalar", lambda h, sb_t=sb_t, lo=lo, bc=bc: h.activation(
                out=sb_t[:, lo:lo + 512], in_=bc[:], func=AF.Copy),
                r=[bcr], w=[reg])


def emit_ln_full(cx, fetch, ps_stack, bf16_src, off=0, f32_out=None):
    rstd = cx.sb("rstd", [1, T], F32, ps_stack)
    arow = cx.sb("arow", [1, T], F32, ps_stack)
    rb_sb = cx.sb("rb_sb", [128, T], BF16, ps_stack)
    ab_sb = cx.sb("ab_sb", [128, T], BF16, ps_stack)
    tag = cx.name("ln")
    emit_ln_stats(cx, fetch, rstd, arow, tag, ps_stack, bf16_src, off)
    emit_bcast_rows(cx, rstd, arow, tag, rb_sb, ab_sb, ps_stack)
    if f32_out is not None:
        rbf, abf = f32_out
        pb = cx.pb
        bc = cx.ps("ps_bcf", [128, 512], ps_stack)
        for row, sb_t, reg in [(rstd, rbf, "rbf"), (arow, abf, "abf")]:
            for nch in range(4):
                lo = nch * 512
                rap = row[0:1, lo:lo + 512]
                pb.emit("tensor", lambda h, rap=rap: h.matmul(
                    bc[:], cx.ones_f[0:1, 0:128], rap, start=True, stop=True, skip_group_check=True),
                    r=["ones_f", tag + "_r", tag + "_a"], w=["ps_bcf"])
                pb.emit("scalar", lambda h, sb_t=sb_t, lo=lo: h.activation(
                    out=sb_t[:, lo:lo + 512], in_=bc[:], func=AF.Copy),
                    r=["ps_bcf"], w=[reg])
    return rb_sb, ab_sb, (tag + "rb", tag + "ab")


def emit_ln_apply(cx, fetch, out_tiles, out_reg, rb_sb, ab_sb, off_out=0, off_src=0,
                  residual_store=None, breg=("rb_sb", "ab_sb")):
    pb = cx.pb
    for c in range(KCH):
        tile, reg = fetch(c)
        oap_t = out_tiles[c]
        for nch in range(4):
            lo = nch * 512
            sap = tile[:, off_src + lo:off_src + lo + 512]
            oap = oap_t[:, off_out + lo:off_out + lo + 512]
            pb.emit("vector", lambda h, sap=sap, oap=oap, lo=lo: h.tensor_tensor(
                out=oap, in0=sap, in1=rb_sb[:, lo:lo + 512], op=OP.mult),
                r=[reg, breg[0]], w=[out_reg(c)])
            pb.emit("vector", lambda h, oap=oap, lo=lo: h.tensor_tensor(
                out=oap, in0=oap, in1=ab_sb[:, lo:lo + 512], op=OP.add),
                r=[out_reg(c), breg[1]], w=[out_reg(c)])
        if residual_store is not None:
            fetchx, store = residual_store
            xtile, xreg = fetchx(c)
            pb.emit("vector", lambda h, xtile=xtile, oap_t=oap_t: h.tensor_tensor(
                out=xtile[:], in0=xtile[:], in1=oap_t[:, off_out:off_out + T], op=OP.add),
                r=[xreg, out_reg(c)], w=[xreg])
            store(c, xtile, xreg)


def emit_load_weight(cx, dram_ap, dest, dest_reg, nch, cols, stage, stage_reg, dcols=None):
    pb = cx.pb
    dcols = dcols or cols
    grp = max(1, min(nch, 1040 // cols))
    c = 0
    while c < nch:
        g = min(grp, nch - c)
        i = cx.wpool_i % 2
        cx.wpool_i += 1
        st_t = cx.wpool[i]
        streg = f"wpool{i}"
        dap = dram_ap[128 * c:128 * (c + g), 0:cols].rearrange("(c p) n -> p c n", p=128)
        sap = st_t.ap()[:, 0:g * cols].rearrange("p (c n) -> p c n", n=cols)
        q = "sync" if i == 0 else "gpsimd"
        pb.emit(q, lambda h, dap=dap, sap=sap: h.dma_start(out=sap, in_=dap),
                w=[streg], dma=True)
        if dcols == cols:
            oap = dest[:, dcols * c:dcols * c + g * cols]
            pb.emit("vector", lambda h, st_t=st_t, oap=oap, g=g, cols=cols: h.tensor_copy(
                out=oap, in_=st_t[:, 0:g * cols]),
                r=[streg], w=[dest_reg])
        else:
            for j in range(g):
                oap = dest[:, dcols * (c + j):dcols * (c + j) + cols]
                pb.emit("vector", lambda h, st_t=st_t, oap=oap, j=j, cols=cols: h.tensor_copy(
                    out=oap, in_=st_t[:, j * cols:(j + 1) * cols]),
                    r=[streg], w=[dest_reg])
        c += g


def emit_load_row(cx, dram_ap, dest_ap, dest_reg, cols, stage, stage_reg):
    pb = cx.pb
    pb.emit("sync", lambda h: h.dma_start(out=stage[0:1, 0:cols], in_=dram_ap),
            w=[stage_reg], dma=True)
    pb.emit("vector", lambda h: h.tensor_copy(out=dest_ap, in_=stage[0:1, 0:cols]),
            r=[stage_reg], w=[dest_reg])


def dbg_dump(cx, dram, name, ap, regs):
    if name in dram:
        cx.pb.emit("sync", lambda h: h.dma_start(out=dram[name], in_=ap), r=list(regs), dma=True)


def emit_l0_attn(cx, dram):
    pb, nc = cx.pb, cx.nc
    C0 = 16.0

    ao = [cx.sb("ao", [128, T], BF16) for c in range(KCH)]

    with ExitStack() as mid:
        v = [cx.sb("vt", [128 if t < 16 else 16, 1040], BF16, mid) for t in range(17)]
        qf = [cx.sb("qf", [128, T], BF16, mid) for m in range(4)]
        kf = [cx.sb("kf", [128, SP], BF16, mid) for m in range(4)]

        with ExitStack() as xnst:
            xn = [cx.sb("xn", [128, SP], BF16, xnst) for c in range(KCH)]
            for c in range(KCH):
                pb.emit("vector", lambda h, c=c: h.memset(xn[c][:, T:SP], 0.0), w=[f"xn{c}"])

            with ExitStack() as ph0:
                xs = XStream(cx, dram["x_in"], ph0, tag="xs0")
                rb_sb, ab_sb, breg = emit_ln_full(cx, xs.fetch, ph0, bf16_src=False)
                emit_ln_apply(cx, xs.fetch, xn, lambda c: f"xn{c}", rb_sb, ab_sb, breg=breg)
                pb.barrier()

            with ExitStack() as ph1:
                _w0 = cx.sb("wstage", [128, 1040], F32, ph1)
                wst = [_w0, _w0]
                sig = cx.sb("sig", [128, SP], BF16, ph1)
                brows = cx.sb("brows", [1, 4 * 512 + 2 * 1040], BF16, ph1)
                boffs = {}
                off = 0
                for nm, cols in [("bq", 512), ("bqc", 512), ("bk", 512), ("bkc", 512),
                                 ("bv", 1040), ("bvc", 1040)]:
                    emit_load_row(cx, dram[nm], brows[0:1, off:off + cols], "brows", cols,
                                  wst[0], "wstage0")
                    boffs[nm] = off
                    off += cols
                ppAv = [cx.ps("ppA", [128, 512], ph1) for i in range(2)]
                ppBv = [cx.ps("ppB", [128, 512], ph1) for i in range(2)]
                sigv = [sig, cx.sb("sig2", [128, SP], BF16, ph1)]

                def qk_proj(wg_nm, bg_nm, wv_nm, bv_nm, outs, outreg, q_cols):
                    with ExitStack() as ws:
                        wg = cx.sb("wg", [128, 8 * 512], BF16, ws)
                        wv_ = cx.sb("wv_", [128, 8 * 512], BF16, ws)
                        emit_load_weight(cx, dram[wg_nm], wg, "wg_r", KCH, 512, wst[0], "wstage0")
                        emit_load_weight(cx, dram[wv_nm], wv_, "wv_r", KCH, 512, wst[1], "wstage1")
                        it = 0
                        for mt in range(4):
                            for lo in range(0, q_cols, 512):
                                nn = min(512, q_cols - lo)
                                pp = it % 2
                                it += 1
                                ppA, par_ = ppAv[pp], f"ppA{pp}"
                                ppB, pbr_ = ppBv[pp], f"ppB{pp}"
                                sg_, sgr_ = sigv[pp], f"sig{pp}"
                                for c in range(KCH):
                                    lh = wg[:, 512 * c + 128 * mt:512 * c + 128 * mt + 128]
                                    rh = xn[c][:, lo:lo + nn]
                                    pb.emit("tensor", lambda h, lh=lh, rh=rh, st=(c == 0), nn=nn, ppA=ppA: h.matmul(
                                        ppA[:, 0:nn], lh, rh, start=st, stop=False, skip_group_check=True),
                                        r=["wg_r", f"xn{c}"], w=[par_])
                                bh = brows[0:1, boffs[bg_nm] + 128 * mt: boffs[bg_nm] + 128 * mt + 128]
                                pb.emit("tensor", lambda h, bh=bh, nn=nn, ppA=ppA: h.matmul(
                                    ppA[:, 0:nn], bh, cx.ones_b[0:1, 0:nn], start=False, stop=True,
                                    skip_group_check=True), r=["brows", "ones_b"], w=[par_])
                                pb.emit("scalar", lambda h, lo=lo, nn=nn, ppA=ppA, sg_=sg_: h.activation(
                                    out=sg_[:, lo:lo + nn], in_=ppA[:, 0:nn], func=AF.Sigmoid),
                                    r=[par_], w=[sgr_])
                                for c in range(KCH):
                                    lh = wv_[:, 512 * c + 128 * mt:512 * c + 128 * mt + 128]
                                    rh = xn[c][:, lo:lo + nn]
                                    pb.emit("tensor", lambda h, lh=lh, rh=rh, st=(c == 0), nn=nn, ppB=ppB: h.matmul(
                                        ppB[:, 0:nn], lh, rh, start=st, stop=False, skip_group_check=True),
                                        r=["wv_r", f"xn{c}"], w=[pbr_])
                                bh = brows[0:1, boffs[bv_nm] + 128 * mt: boffs[bv_nm] + 128 * mt + 128]
                                pb.emit("tensor", lambda h, bh=bh, nn=nn, ppB=ppB: h.matmul(
                                    ppB[:, 0:nn], bh, cx.ones_b[0:1, 0:nn], start=False, stop=True,
                                    skip_group_check=True), r=["brows", "ones_b"], w=[pbr_])
                                oap = outs[mt][:, lo:lo + nn]
                                pb.emit("vector", lambda h, oap=oap, lo=lo, nn=nn, mt=mt, ppB=ppB, sg_=sg_: h.tensor_tensor(
                                    out=oap, in0=ppB[:, 0:nn], in1=sg_[:, lo:lo + nn], op=OP.mult),
                                    r=[pbr_, sgr_], w=[f"{outreg}{mt}"])
                        pb.barrier()

                qk_proj("Wqc", "bqc", "Wq", "bq", qf, "qf", T)
                qk_proj("Wkc", "bkc", "Wk", "bk", kf, "kf", SP)

                with ExitStack() as ws:
                    wvc = cx.sb("wvc", [128, 8 * 1040], BF16, ws)
                    wv = cx.sb("wv", [128, 8 * 1040], BF16, ws)
                    emit_load_weight(cx, dram["Wvc"], wvc, "wvc_r", KCH, 1040, wst[0], "wstage0")
                    emit_load_weight(cx, dram["Wv"], wv, "wv2_r", KCH, 1040, wst[1], "wstage1")
                    it2 = 0
                    for tt in range(17):
                        rows = 128 if tt < 16 else 16
                        for lo in range(0, 1040, 512):
                            nn = min(512, 1040 - lo)
                            pp = it2 % 2
                            it2 += 1
                            ppA, par_ = ppAv[pp], f"ppA{pp}"
                            ppB, pbr_ = ppBv[pp], f"ppB{pp}"
                            sg_, sgr_ = sigv[pp], f"sig{pp}"
                            for c in range(KCH):
                                lh = xn[c][:, 128 * tt:128 * tt + rows]
                                rh = wvc[:, 1040 * c + lo:1040 * c + lo + nn]
                                pb.emit("tensor", lambda h, lh=lh, rh=rh, st=(c == 0), rows=rows, nn=nn, ppA=ppA: h.matmul(
                                    ppA[0:rows, 0:nn], lh, rh, start=st, stop=False, skip_group_check=True),
                                    r=["wvc_r", f"xn{c}"], w=[par_])
                            bh = brows[0:1, boffs["bvc"] + lo:boffs["bvc"] + lo + nn]
                            pb.emit("tensor", lambda h, bh=bh, rows=rows, nn=nn, ppA=ppA: h.matmul(
                                ppA[0:rows, 0:nn], cx.ones_b[0:1, 0:rows], bh, start=False, stop=True,
                                skip_group_check=True), r=["brows", "ones_b"], w=[par_])
                            pb.emit("scalar", lambda h, rows=rows, lo=lo, nn=nn, ppA=ppA, sg_=sg_: h.activation(
                                out=sg_[0:rows, lo:lo + nn], in_=ppA[0:rows, 0:nn], func=AF.Sigmoid),
                                r=[par_], w=[sgr_])
                            for c in range(KCH):
                                lh = xn[c][:, 128 * tt:128 * tt + rows]
                                rh = wv[:, 1040 * c + lo:1040 * c + lo + nn]
                                pb.emit("tensor", lambda h, lh=lh, rh=rh, st=(c == 0), rows=rows, nn=nn, ppB=ppB: h.matmul(
                                    ppB[0:rows, 0:nn], lh, rh, start=st, stop=False, skip_group_check=True),
                                    r=["wv2_r", f"xn{c}"], w=[pbr_])
                            bh = brows[0:1, boffs["bv"] + lo:boffs["bv"] + lo + nn]
                            pb.emit("tensor", lambda h, bh=bh, rows=rows, nn=nn, ppB=ppB: h.matmul(
                                ppB[0:rows, 0:nn], cx.ones_b[0:1, 0:rows], bh, start=False, stop=True,
                                skip_group_check=True), r=["brows", "ones_b"], w=[pbr_])
                            pb.emit("vector", lambda h, tt=tt, rows=rows, lo=lo, nn=nn, ppB=ppB, sg_=sg_: h.tensor_tensor(
                                out=v[tt][0:rows, lo:lo + nn], in0=ppB[0:rows, 0:nn],
                                in1=sg_[0:rows, lo:lo + nn], op=OP.mult),
                                r=[pbr_, sgr_], w=[f"v{tt}"])
                        def ones_fn(h, tt=tt, rows=rows):
                            vv = v[tt].ap().rearrange("p (g w) -> p g w", w=65)[0:rows, :, 64:65]
                            return h.memset(vv, 1.0)
                        pb.emit("vector", ones_fn, r=[f"v{tt}"], w=[f"v{tt}"])
                    pb.barrier()
                pb.barrier()
        pb.barrier()

        with ExitStack() as ph2:
            ps_s = [cx.ps("ps_s", [128, 1024], ph2) for i in range(2)]
            ps_ctx2 = [cx.ps("ps_ctx", [65, 512], ph2) for i in range(2)]
            ps_bc = cx.ps("ps_cb", [64, 512], ph2)
            probs = [cx.sb("probs", [128, 1024], BF16, ph2) for i in range(2)]
            ctxb2 = [cx.sb("ctxb", [64, 512], BF16, ph2) for i in range(2)]
            rrow2 = [cx.sb("rrow", [1, 512], F32, ph2) for i in range(2)]

            rounds = []
            for r_ in range(9):
                pair = []
                for kt in (2 * r_, 2 * r_ + 1):
                    kn = min(128, SP - 128 * kt)
                    if kn > 0:
                        pair.append((kt, kn))
                rounds.append(pair)
            n_pv = sum(len(p) for p in rounds)

            for h_i in range(16):
                mt, g = h_i // 4, h_i % 4
                qrows = slice(32 * g, 32 * g + 16)
                tp = (32 * g, 0)
                for qc in range(4):
                    par = (4 * h_i + qc) % 2
                    ps_ctx, creg = ps_ctx2[par], f"ps_ctx{par}"
                    ctxb, cbreg = ctxb2[par], f"ctxb{par}"
                    rrow, rreg = rrow2[par], f"rrow{par}"
                    qlo = 512 * qc
                    pv_i = 0
                    def emit_qk_round(r_):
                        pair = rounds[r_]
                        pbuf = r_ % 2
                        sreg = f"ps_s{pbuf}"
                        for gi, (kt, kn) in enumerate(pair):
                            lh = kf[mt][qrows, 128 * kt:128 * kt + kn]
                            rh = qf[mt][qrows, qlo:qlo + 512]
                            pb.emit("tensor", lambda h, lh=lh, rh=rh, gi=gi, kn=kn, pbuf=pbuf, tp=tp: h.matmul(
                                ps_s[pbuf][0:kn, 512 * gi:512 * gi + 512], lh, rh,
                                start=True, stop=True, skip_group_check=True, tile_position=tp),
                                r=[f"kf{mt}", f"qf{mt}"], w=[sreg])
                    emit_qk_round(0)
                    for r_, pair in enumerate(rounds):
                        pbuf = r_ % 2
                        sreg, preg = f"ps_s{pbuf}", f"probs{pbuf}"
                        if r_ + 1 < len(rounds):
                            emit_qk_round(r_ + 1)
                        wid = 512 * len(pair)
                        kmax = pair[0][1]
                        negc = const_col(cx, -C0, rows=kmax)
                        pb.emit("scalar", lambda h, pbuf=pbuf, wid=wid, kmax=kmax, negc=negc: h.activation(
                            out=probs[pbuf][0:kmax, 0:wid], in_=ps_s[pbuf][0:kmax, 0:wid],
                            func=AF.Exp, bias=negc, scale=0.125),
                            r=[sreg, "cc"], w=[preg])
                        for gi, (kt, kn) in enumerate(pair):
                            pv_i += 1
                            lh = v[kt][0:kn, 65 * h_i:65 * h_i + 65]
                            rh = probs[pbuf][0:kn, 512 * gi:512 * gi + 512]
                            pb.emit("tensor", lambda h, lh=lh, rh=rh, st=(pv_i == 1), sp=(pv_i == n_pv), ps_ctx=ps_ctx: h.matmul(
                                ps_ctx[:], lh, rh, start=st, stop=sp, skip_group_check=True),
                                r=[f"v{kt}", preg], w=[creg])
                    pb.emit("vector", lambda h, rrow=rrow, ps_ctx=ps_ctx: h.tensor_scalar(
                        out=rrow[:], in0=ps_ctx[64:65, 0:512], scalar1=1e-30, scalar2=None, op0=OP.max),
                        r=[creg], w=[rreg], risky=True)
                    pb.emit("vector", lambda h, rrow=rrow: h.reciprocal(out=rrow[:], in_=rrow[:]),
                            r=[rreg], w=[rreg], risky=True)
                    pb.emit("tensor", lambda h, rrow=rrow: h.matmul(ps_bc[:], cx.ones_f[0:1, 0:64], rrow[:],
                                                         start=True, stop=True, skip_group_check=True),
                            r=["ones_f", rreg], w=["ps_cb"])
                    pb.emit("scalar", lambda h, ctxb=ctxb, ps_ctx=ps_ctx: h.activation(out=ctxb[:], in_=ps_ctx[0:64, 0:512],
                                                             func=AF.Copy),
                            r=[creg], w=[cbreg])
                    dt_, dr_ = h_i // 2, 64 * (h_i % 2)
                    pb.emit("vector", lambda h, dt_=dt_, dr_=dr_, qlo=qlo, ctxb=ctxb: h.tensor_tensor(
                        out=ao[dt_][dr_:dr_ + 64, qlo:qlo + 512], in0=ctxb[:], in1=ps_bc[:], op=OP.mult),
                        r=[cbreg, "ps_cb"], w=[f"ao{dt_}"])
            pb.barrier()
    pb.barrier()

    with ExitStack() as ph3:
        rb_sb, ab_sb, breg = emit_ln_full(cx, lambda c: (ao[c], f"ao{c}"), ph3, bf16_src=True)
        u = [cx.sb("u", [128, T], BF16, ph3) for c in range(KCH)]
        xs3 = XStream(cx, dram["x_in"], ph3, tag="xs3")
        def store(c, xtile, xreg):
            pb.emit("scalar", lambda h, c=c, xtile=xtile: h.dma_start(
                out=dram["x_out"][128 * c:128 * (c + 1), :], in_=xtile[:]),
                r=[xreg], dma=True)
        emit_ln_apply(cx, lambda c: (ao[c], f"ao{c}"), u, lambda c: f"u{c}", rb_sb, ab_sb,
                      residual_store=(xs3.fetch, store), breg=breg)
        pb.barrier()


def build_l0_attn_prog():
    nc = bass.Bass("TRN2", target_bir_lowering=False)
    dram = {}
    dram["x_in"] = nc.dram_tensor("x_in", [D, T], F32, kind="ExternalInput").ap()
    for nm, sh in [("Wq", [D, 512]), ("Wqc", [D, 512]), ("Wk", [D, 512]), ("Wkc", [D, 512]),
                   ("Wv", [D, 1040]), ("Wvc", [D, 1040])]:
        dram[nm] = nc.dram_tensor(nm, sh, F32, kind="ExternalInput").ap()
    for nm, n in [("bq", 512), ("bqc", 512), ("bk", 512), ("bkc", 512), ("bv", 1040), ("bvc", 1040)]:
        dram[nm] = nc.dram_tensor(nm, [1, n], F32, kind="ExternalInput").ap()
    dram["x_out"] = nc.dram_tensor("x_out", [D, T], F32, kind="ExternalOutput").ap()

    with ExitStack() as st:
        sems = {e: st.enter_context(nc.semaphore(f"sem_{e}")) for e in ENG}
        pb = PB(nc)
        cx = Ctx(nc, pb, st)
        make_consts(cx)
        blk = st.enter_context(nc.Block())
        emit_l0_attn(cx, dram)
        pb.realize(blk, sems)
    return nc


def prep_l0_attn_weights(inp):
    H = 16
    out = {}
    def pad32(W, b):
        Wp = np.zeros((D, 512), np.float32)
        bp = np.zeros((1, 512), np.float32)
        for h in range(H):
            Wp[:, 32 * h:32 * h + 16] = W[:, 16 * h:16 * h + 16]
            bp[0, 32 * h:32 * h + 16] = b[16 * h:16 * h + 16]
        return Wp, bp
    def aug65(W, b):
        Wp = np.zeros((D, 1040), np.float32)
        bp = np.zeros((1, 1040), np.float32)
        for h in range(H):
            Wp[:, 65 * h:65 * h + 64] = W[:, 64 * h:64 * h + 64]
            bp[0, 65 * h:65 * h + 64] = b[64 * h:64 * h + 64]
        return Wp, bp
    out["Wq"], out["bq"] = pad32(np.asarray(inp["L0_attn_Wq"]), np.asarray(inp["L0_attn_bq"]))
    out["Wqc"], out["bqc"] = pad32(np.asarray(inp["L0_attn_Wqc"]), np.asarray(inp["L0_attn_bqc"]))
    out["Wk"], out["bk"] = pad32(np.asarray(inp["L0_attn_Wk"]), np.asarray(inp["L0_attn_bk"]))
    out["Wkc"], out["bkc"] = pad32(np.asarray(inp["L0_attn_Wkc"]), np.asarray(inp["L0_attn_bkc"]))
    out["Wv"], out["bv"] = aug65(np.asarray(inp["L0_attn_Wv"]), np.asarray(inp["L0_attn_bv"]))
    out["Wvc"], out["bvc"] = aug65(np.asarray(inp["L0_attn_Wvc"]), np.asarray(inp["L0_attn_bvc"]))
    return out


def emit_l0_ffn(cx, dram):
    pb, nc = cx.pb, cx.nc

    moe = [cx.sb("moe", [128, T], F32) for c in range(KCH)]

    with ExitStack() as mid:
        xn = [cx.sb("xnf", [128, T], BF16, mid) for c in range(KCH)]
        wtm = [cx.sb("wtm", [128, 32], F32, mid) for tt in range(16)]
        idf = cx.sb("idf", [128, 128], F32, mid)
        pb.emit("sync", lambda h: h.dma_start(out=idf[:], in_=dram["ident_f"]), w=["idf"], dma=True)
        rbf = cx.sb("rbf", [128, T], F32, mid)
        abf = cx.sb("abf", [128, T], F32, mid)

        with ExitStack() as ph0:
            xs = XStream(cx, dram["x_in"], ph0, tag="xf0")
            rb_sb, ab_sb, breg = emit_ln_full(cx, xs.fetch, ph0, bf16_src=False,
                                              f32_out=(rbf, abf))
            emit_ln_apply(cx, xs.fetch, xn, lambda c: f"xnf{c}", rb_sb, ab_sb, breg=breg)
            dbg_dump(cx, dram, "d_xn", xn[0][:, 0:T], ["xnf0"])
            pb.barrier()

        with ExitStack() as ph1:
            rwtf = cx.sb("rwtf", [128, 8 * 32], F32, ph1)
            for c in range(KCH):
                pb.emit("sync", lambda h, c=c: h.dma_start(
                    out=rwtf[:, 32 * c:32 * c + 32], in_=dram["router_wT"][128 * c:128 * (c + 1), :]),
                    w=["rwtf"], dma=True)
            rbias = cx.sb("rbias", [1, 32], F32, ph1)
            pb.emit("sync", lambda h: h.dma_start(out=rbias[:], in_=dram["router_bias"]),
                    w=["rbias"], dma=True)
            lg = cx.sb("lgits", [32, T], F32, ph1)
            ps_l4 = [cx.ps("ps_l", [32, 512], ph1) for i in range(4)]
            xs1 = XStream(cx, dram["x_in"], ph1, tag="xr1")
            xn32 = cx.sb("xn32", [128, T], F32, ph1)
            for c in range(KCH):
                tile, reg = xs1.fetch(c)
                pb.emit("vector", lambda h, tile=tile: h.tensor_tensor(
                    out=xn32[:], in0=tile[:], in1=rbf[:], op=OP.mult),
                    r=[reg, "rbf"], w=["xn32"])
                pb.emit("vector", lambda h: h.tensor_tensor(
                    out=xn32[:], in0=xn32[:], in1=abf[:], op=OP.add),
                    r=["xn32", "abf"], w=["xn32"])
                for nch in range(4):
                    lo = nch * 512
                    lh = rwtf[:, 32 * c:32 * c + 32]
                    rh = xn32[:, lo:lo + 512]
                    pb.emit("tensor", lambda h, lh=lh, rh=rh, st=(c == 0), nch=nch: h.matmul(
                        ps_l4[nch][:], lh, rh, start=st, stop=False, skip_group_check=True),
                        r=["rwtf", "xn32"], w=[f"ps_l{nch}"])
            for nch in range(4):
                lo = nch * 512
                pb.emit("tensor", lambda h, nch=nch: h.matmul(
                    ps_l4[nch][:], rbias[0:1, 0:32], cx.ones_f[0:1, 0:512],
                    start=False, stop=True, skip_group_check=True),
                    r=["rbias", "ones_f"], w=[f"ps_l{nch}"])
                pb.emit("vector", lambda h, lo=lo, nch=nch: h.tensor_copy(
                    out=lg[:, lo:lo + 512], in_=ps_l4[nch][:]),
                    r=[f"ps_l{nch}"], w=["lg"])
            ps_t = cx.ps("ps_t", [128, 512], ph1)
            ptile = cx.sb("ptile", [128, 32], F32, ph1)
            sc1 = cx.sb("sc1", [128, 32], F32, ph1)
            sc1x = cx.sb("sc1x", [128, 32], F32, ph1)
            sc2 = cx.sb("sc2", [128, 8], F32, ph1)
            for tt in range(16):
                pb.emit("tensor", lambda h, tt=tt: h.transpose(
                    ps_t[:, 0:32], lg[:, 128 * tt:128 * tt + 128], idf[0:32, 0:32]),
                    r=["lg", "idf"], w=["ps_tt"])
                pb.emit("vector", lambda h: h.tensor_copy(out=ptile[:], in_=ps_t[:, 0:32]),
                        r=["ps_tt"], w=["ptile"])
                pb.emit("vector", lambda h: h.reduce_max(out=sc2[:, 1:2], in_=ptile[:], axis=AX.X,
                                                         negate=True),
                        r=["ptile"], w=["sc2"], risky=True)
                pb.emit("scalar", lambda h: h.activation(out=ptile[:], in_=ptile[:], func=AF.Exp,
                                                         bias=sc2[:, 1:2], accum_out=sc2[:, 2:3]),
                        r=["ptile", "sc2"], w=["ptile", "sc2"], risky=True)
                pb.emit("vector", lambda h: h.reciprocal(out=sc2[:, 3:4], in_=sc2[:, 2:3]),
                        r=["sc2"], w=["sc2"], risky=True)
                pb.emit("vector", lambda h: h.tensor_scalar(out=ptile[:], in0=ptile[:],
                                                            scalar1=sc2[:, 3:4], scalar2=None, op0=OP.mult),
                        r=["ptile", "sc2"], w=["ptile"])
                if tt == 0:
                    dbg_dump(cx, dram, "d_pt", ptile[:], ["ptile"])
                pb.emit("vector", lambda h: h.tensor_copy(out=sc1[:], in_=ptile[:]), r=["ptile"], w=["sc1"])
                for it in range(4):
                    pb.emit("vector", lambda h, it=it: h.reduce_max(out=sc2[:, 4:5], in_=sc1[:], axis=AX.X),
                            r=["sc1"], w=["sc2"], risky=True)
                    if it < 3:
                        pb.emit("vector", lambda h: h.tensor_scalar(out=sc1x[:], in0=sc1[:],
                                                                    scalar1=sc2[:, 4:5], scalar2=1e30,
                                                                    op0=OP.is_ge, op1=OP.mult),
                                r=["sc1", "sc2"], w=["sc1x"])
                        pb.emit("vector", lambda h: h.tensor_tensor(out=sc1[:], in0=sc1[:], in1=sc1x[:],
                                                                    op=OP.subtract),
                                r=["sc1", "sc1x"], w=["sc1"])
                pb.emit("vector", lambda h: h.tensor_scalar(out=sc1[:], in0=ptile[:],
                                                            scalar1=sc2[:, 4:5], scalar2=None, op0=OP.is_ge),
                        r=["ptile", "sc2"], w=["sc1"])
                pb.emit("vector", lambda h: h.tensor_tensor(out=sc1[:], in0=sc1[:], in1=ptile[:], op=OP.mult),
                        r=["sc1", "ptile"], w=["sc1"])
                pb.emit("vector", lambda h: h.reduce_sum(out=sc2[:, 5:6], in_=sc1[:], axis=AX.X),
                        r=["sc1"], w=["sc2"], risky=True)
                pb.emit("vector", lambda h: h.tensor_scalar(out=sc2[:, 6:7], in0=sc2[:, 5:6],
                                                            scalar1=1e-12, scalar2=None, op0=OP.max),
                        r=["sc2"], w=["sc2"])
                pb.emit("vector", lambda h: h.reciprocal(out=sc2[:, 7:8], in_=sc2[:, 6:7]),
                        r=["sc2"], w=["sc2"], risky=True)
                pb.emit("vector", lambda h, tt=tt: h.tensor_scalar(out=wtm[tt][:], in0=sc1[:],
                                                                   scalar1=sc2[:, 7:8], scalar2=None, op0=OP.mult),
                        r=["sc1", "sc2"], w=[f"wtm{tt}"])
            dbg_dump(cx, dram, "d_lg", lg[:], ["lg"])
            dbg_dump(cx, dram, "d_wtm", wtm[0][:], ["wtm0"])
            pb.barrier()

        with ExitStack() as ph2:
            wfm = cx.sb("wfm", [32, T], F32, ph2)
            with ExitStack() as tps:
                ps_tb = cx.ps("ps_tb", [32, 512], tps)
                for tt in range(16):
                    pb.emit("tensor", lambda h, tt=tt: h.transpose(
                        ps_tb[:, 0:128], wtm[tt][:], idf[:]),
                        r=[f"wtm{tt}", "idf"], w=["ps_tb"])
                    pb.emit("vector", lambda h, tt=tt: h.tensor_copy(out=wfm[:, 128 * tt:128 * tt + 128],
                                                                     in_=ps_tb[0:32, 0:128]),
                            r=["ps_tb"], w=["wfm"])
                pb.barrier()
            dbg_dump(cx, dram, "d_wfm", wfm[:], ["wfm"])
            wge = [cx.sb("wge", [128, 1024], BF16, ph2) for e in range(8)]
            wue = [cx.sb("wue", [128, 1024], BF16, ph2) for e in range(8)]
            wde = [cx.sb("wde", [128, 1024], BF16, ph2) for e in range(8)]
            wstage = [None, None]
            ht = [cx.sb("ht", [128, 512], BF16, ph2) for e in range(8)]
            gsb2v = [cx.sb("gsb", [128, 512], BF16, ph2) for i in range(2)]
            wbc = [cx.sb("wbc", [128, 512], BF16, ph2) for e in range(8)]
            wrowv = [cx.sb("wrow", [1, 512], F32, ph2) for i in range(2)]
            ps_g2 = [cx.ps("ps_g", [128, 512], ph2) for i in range(2)]
            ps_u2 = [cx.ps("ps_u", [128, 512], ph2) for i in range(2)]
            ps_wv = [cx.ps("ps_w", [128, 512], ph2)] * 2
            ps_rv = [cx.ps("ps_r", [1, 512], ph2)] * 2
            ps_d = [cx.ps("ps_d", [128, 512], ph2) for i in range(2)]

            for grp in range(4):
                for e in range(8):
                    ge = grp * 8 + e
                    emit_load_weight(cx, dram["gate_wT"][1024 * ge:1024 * (ge + 1), :], wge[e], f"wge{e}",
                                     KCH, 128, wstage[0], "wfstage0")
                    emit_load_weight(cx, dram["up_wT"][1024 * ge:1024 * (ge + 1), :], wue[e], f"wue{e}",
                                     KCH, 128, wstage[1], "wfstage1")
                    emit_load_weight(cx, dram["down_wT"][128 * ge:128 * (ge + 1), :], wde[e], f"wde{e}",
                                     1, 1024, wstage[0], "wfstage0")
                for nch in range(4):
                    lo = nch * 512
                    for e in range(8):
                        ge = grp * 8 + e
                        pe_ = e % 2
                        ps_re, rer = ps_rv[pe_], "ps_r"
                        ps_we, wer = ps_wv[pe_], "ps_w"
                        wre, wrr = wrowv[pe_], f"wrow{pe_}"
                        pb.emit("tensor", lambda h, ge=ge, lo=lo, ps_re=ps_re: h.matmul(
                            ps_re[:], idf[0:32, ge:ge + 1], wfm[:, lo:lo + 512],
                            start=True, stop=True, skip_group_check=True),
                            r=["idf", "wfm"], w=[rer])
                        pb.emit("vector", lambda h, wre=wre, ps_re=ps_re: h.tensor_copy(out=wre[:], in_=ps_re[:]),
                                r=[rer], w=[wrr])
                        pb.emit("tensor", lambda h, wre=wre, ps_we=ps_we: h.matmul(
                            ps_we[:], cx.ones_f[0:1, 0:128], wre[:],
                            start=True, stop=True, skip_group_check=True),
                            r=["ones_f", wrr], w=[wer])
                        pb.emit("vector", lambda h, e=e, ps_we=ps_we: h.tensor_copy(out=wbc[e][:], in_=ps_we[:]),
                                r=[wer], w=[f"wbc{e}"])
                    for e in range(8):
                        ge = grp * 8 + e
                        pe_ = e % 2
                        ps_g, greg = ps_g2[pe_], f"ps_g{pe_}"
                        ps_u, ureg = ps_u2[pe_], f"ps_u{pe_}" 
                        for c in range(KCH):
                            lh = wge[e][:, 128 * c:128 * c + 128]
                            rh = xn[c][:, lo:lo + 512]
                            pb.emit("tensor", lambda h, lh=lh, rh=rh, st=(c == 0), sp=(c == KCH - 1), ps_g=ps_g: h.matmul(
                                ps_g[:], lh, rh, start=st, stop=sp, skip_group_check=True),
                                r=[f"wge{e}", f"xnf{c}"], w=[greg])
                        gsb, gsreg = gsb2v[pe_], f"gsb{pe_}"
                        pb.emit("scalar", lambda h, ps_g=ps_g, gsb=gsb: h.activation(out=gsb[:], in_=ps_g[:], func=AF.Silu),
                                r=[greg], w=[gsreg])
                        for c in range(KCH):
                            lh = wue[e][:, 128 * c:128 * c + 128]
                            rh = xn[c][:, lo:lo + 512]
                            pb.emit("tensor", lambda h, lh=lh, rh=rh, st=(c == 0), sp=(c == KCH - 1), ps_u=ps_u: h.matmul(
                                ps_u[:], lh, rh, start=st, stop=sp, skip_group_check=True),
                                r=[f"wue{e}", f"xnf{c}"], w=[ureg])
                        pb.emit("vector", lambda h, e=e, ps_u=ps_u, gsb=gsb: h.tensor_tensor(out=ht[e][:], in0=ps_u[:],
                                                                         in1=gsb[:], op=OP.mult),
                                r=[ureg, gsreg], w=[f"ht{e}"])
                        pb.emit("vector", lambda h, e=e: h.tensor_tensor(out=ht[e][:], in0=ht[e][:],
                                                                         in1=wbc[e][:], op=OP.mult),
                                r=[f"ht{e}", f"wbc{e}"], w=[f"ht{e}"])
                        if grp == 0 and e == 0 and nch == 0:
                            dbg_dump(cx, dram, "d_ht", ht[0][:], ["ht0"])
                            dbg_dump(cx, dram, "d_gsb", gsb[:], [gsreg])
                    for mh in range(4):
                        for e in range(8):
                            for mt in range(2):
                                gmt = 2 * mh + mt
                                lh = wde[e][:, 128 * gmt:128 * gmt + 128]
                                pb.emit("tensor", lambda h, lh=lh, e=e, mt=mt: h.matmul(
                                    ps_d[mt][:], lh, ht[e][:], start=(e == 0), stop=(e == 7),
                                    skip_group_check=True),
                                    r=[f"wde{e}", f"ht{e}"], w=[f"ps_d{mt}"])
                        for mt in range(2):
                            gmt = 2 * mh + mt
                            if grp == 0:
                                pb.emit("vector", lambda h, gmt=gmt, mt=mt, lo=lo: h.tensor_copy(
                                    out=moe[gmt][:, lo:lo + 512], in_=ps_d[mt][:]),
                                    r=[f"ps_d{mt}"], w=[f"moe{gmt}"])
                            else:
                                pb.emit("vector", lambda h, gmt=gmt, mt=mt, lo=lo: h.tensor_tensor(
                                    out=moe[gmt][:, lo:lo + 512], in0=moe[gmt][:, lo:lo + 512],
                                    in1=ps_d[mt][:], op=OP.add),
                                    r=[f"ps_d{mt}", f"moe{gmt}"], w=[f"moe{gmt}"])
            dbg_dump(cx, dram, "d_moe0", moe[0][:], ["moe0"])
            pb.barrier()

        with ExitStack() as ph3:
            wsg = cx.sb("wsg", [128, 8 * 512], BF16, ph3)
            wsu = cx.sb("wsu", [128, 8 * 512], BF16, ph3)
            wsd = cx.sb("wsd", [128, 4 * 1024], BF16, ph3)
            wstage_d = [None, None]
            emit_load_weight(cx, dram["sgate_wT"], wsg, "wsg", KCH, 512, wstage_d[0], "wdstage0")
            emit_load_weight(cx, dram["sup_wT"], wsu, "wsu", KCH, 512, wstage_d[1], "wdstage1")
            emit_load_weight(cx, dram["sdown_wT"], wsd, "wsd", 4, 1024, wstage_d[0], "wdstage0")
            sdin = [cx.sb("sdin", [128, T], BF16, ph3) for m in range(4)]
            ps_dgv = [cx.ps("ps_sg", [128, 512], ph3) for i in range(2)]
            ps_duv = [cx.ps("ps_su", [128, 512], ph3) for i in range(2)]
            gsb_dv = [cx.sb("gsb2", [128, 512], BF16, ph3) for i in range(2)]
            itd = 0
            for mt in range(4):
                for nch in range(4):
                    lo = nch * 512
                    pp = itd % 2
                    itd += 1
                    ps_dg, dgr = ps_dgv[pp], f"ps_sg{pp}"
                    ps_du, dur = ps_duv[pp], f"ps_su{pp}"
                    gsb_d, gdr = gsb_dv[pp], f"gsb2{pp}"
                    for c in range(KCH):
                        lh = wsg[:, 512 * c + 128 * mt:512 * c + 128 * mt + 128]
                        rh = xn[c][:, lo:lo + 512]
                        pb.emit("tensor", lambda h, lh=lh, rh=rh, st=(c == 0), sp=(c == KCH - 1), ps_dg=ps_dg: h.matmul(
                            ps_dg[:], lh, rh, start=st, stop=sp, skip_group_check=True),
                            r=["wsg", f"xnf{c}"], w=[dgr])
                    pb.emit("scalar", lambda h, ps_dg=ps_dg, gsb_d=gsb_d: h.activation(out=gsb_d[:], in_=ps_dg[:], func=AF.Silu),
                            r=[dgr], w=[gdr])
                    for c in range(KCH):
                        lh = wsu[:, 512 * c + 128 * mt:512 * c + 128 * mt + 128]
                        rh = xn[c][:, lo:lo + 512]
                        pb.emit("tensor", lambda h, lh=lh, rh=rh, st=(c == 0), sp=(c == KCH - 1), ps_du=ps_du: h.matmul(
                            ps_du[:], lh, rh, start=st, stop=sp, skip_group_check=True),
                            r=["wsu", f"xnf{c}"], w=[dur])
                    pb.emit("vector", lambda h, mt=mt, lo=lo, ps_du=ps_du, gsb_d=gsb_d: h.tensor_tensor(
                        out=sdin[mt][:, lo:lo + 512], in0=ps_du[:], in1=gsb_d[:], op=OP.mult),
                        r=[dur, gdr], w=[f"sdin{mt}"])
            for mt in range(KCH):
                for nch in range(4):
                    lo = nch * 512
                    pp = itd % 2
                    itd += 1
                    ps_dg, dgr = ps_dgv[pp], f"ps_sg{pp}"
                    for c4 in range(4):
                        lh = wsd[:, 1024 * c4 + 128 * mt:1024 * c4 + 128 * mt + 128]
                        rh = sdin[c4][:, lo:lo + 512]
                        pb.emit("tensor", lambda h, lh=lh, rh=rh, st=(c4 == 0), sp=(c4 == 3), ps_dg=ps_dg: h.matmul(
                            ps_dg[:], lh, rh, start=st, stop=sp, skip_group_check=True),
                            r=["wsd", f"sdin{c4}"], w=[dgr])
                    pb.emit("vector", lambda h, mt=mt, lo=lo, ps_dg=ps_dg: h.tensor_tensor(
                        out=moe[mt][:, lo:lo + 512], in0=moe[mt][:, lo:lo + 512], in1=ps_dg[:], op=OP.add),
                        r=[dgr, f"moe{mt}"], w=[f"moe{mt}"])
            pb.barrier()
    pb.barrier()

    with ExitStack() as ph4:
        rb_sb, ab_sb, breg = emit_ln_full(cx, lambda c: (moe[c], f"moe{c}"), ph4, bf16_src=False)
        u = [cx.sb("uf", [128, T], BF16, ph4) for c in range(KCH)]
        xs4 = XStream(cx, dram["x_in"], ph4, tag="xf4")
        def store(c, xtile, xreg):
            pb.emit("scalar", lambda h, c=c, xtile=xtile: h.dma_start(
                out=dram["x_out"][128 * c:128 * (c + 1), :], in_=xtile[:]),
                r=[xreg], dma=True)
        emit_ln_apply(cx, lambda c: (moe[c], f"moe{c}"), u, lambda c: f"uf{c}", rb_sb, ab_sb,
                      residual_store=(xs4.fetch, store), breg=breg)
        pb.barrier()


def build_l0_ffn_prog(dbg=False):
    nc = bass.Bass("TRN2", target_bir_lowering=False)
    dram = {}
    dram["x_in"] = nc.dram_tensor("x_in", [D, T], F32, kind="ExternalInput").ap()
    dram["router_wT"] = nc.dram_tensor("router_wT", [D, 32], F32, kind="ExternalInput").ap()
    dram["router_bias"] = nc.dram_tensor("router_bias", [1, 32], F32, kind="ExternalInput").ap()
    dram["gate_wT"] = nc.dram_tensor("gate_wT", [32 * D, 128], F32, kind="ExternalInput").ap()
    dram["up_wT"] = nc.dram_tensor("up_wT", [32 * D, 128], F32, kind="ExternalInput").ap()
    dram["down_wT"] = nc.dram_tensor("down_wT", [32 * 128, 1024], F32, kind="ExternalInput").ap()
    dram["sgate_wT"] = nc.dram_tensor("sgate_wT", [D, 512], F32, kind="ExternalInput").ap()
    dram["sup_wT"] = nc.dram_tensor("sup_wT", [D, 512], F32, kind="ExternalInput").ap()
    dram["sdown_wT"] = nc.dram_tensor("sdown_wT", [512, 1024], F32, kind="ExternalInput").ap()
    dram["ident_f"] = nc.dram_tensor("ident_f", [128, 128], F32, kind="ExternalInput").ap()
    dram["x_out"] = nc.dram_tensor("x_out", [D, T], F32, kind="ExternalOutput").ap()
    if dbg:
        dram["d_xn"] = nc.dram_tensor("d_xn", [128, T], BF16, kind="ExternalOutput").ap()
        dram["d_lg"] = nc.dram_tensor("d_lg", [32, T], F32, kind="ExternalOutput").ap()
        dram["d_wtm"] = nc.dram_tensor("d_wtm", [128, 32], F32, kind="ExternalOutput").ap()
        dram["d_wfm"] = nc.dram_tensor("d_wfm", [32, T], F32, kind="ExternalOutput").ap()
        dram["d_moe0"] = nc.dram_tensor("d_moe0", [128, T], F32, kind="ExternalOutput").ap()
        dram["d_pt"] = nc.dram_tensor("d_pt", [128, 32], F32, kind="ExternalOutput").ap()
        dram["d_ht"] = nc.dram_tensor("d_ht", [128, 512], BF16, kind="ExternalOutput").ap()
        dram["d_wrow"] = nc.dram_tensor("d_wrow", [1, 512], F32, kind="ExternalOutput").ap()
        dram["d_gsb"] = nc.dram_tensor("d_gsb", [128, 512], BF16, kind="ExternalOutput").ap()

    with ExitStack() as st:
        sems = {e: st.enter_context(nc.semaphore(f"sem_{e}")) for e in ENG}
        pb = PB(nc)
        cx = Ctx(nc, pb, st)
        make_consts(cx)
        blk = st.enter_context(nc.Block())
        emit_l0_ffn(cx, dram)
        pb.realize(blk, sems)
    return nc


def prep_l0_ffn_weights(inp):
    out = {}
    out["router_wT"] = np.ascontiguousarray(np.asarray(inp["L0_ffn_router_w"]).T)
    out["router_bias"] = np.asarray(inp["L0_ffn_router_bias"]).reshape(1, 32)
    gw = np.asarray(inp["L0_ffn_gate_w"])
    uw = np.asarray(inp["L0_ffn_up_w"])
    dw = np.asarray(inp["L0_ffn_down_w"])
    out["gate_wT"] = np.ascontiguousarray(gw.transpose(0, 2, 1)).reshape(32 * 1024, 128)
    out["up_wT"] = np.ascontiguousarray(uw.transpose(0, 2, 1)).reshape(32 * 1024, 128)
    out["down_wT"] = np.ascontiguousarray(dw.transpose(0, 2, 1)).reshape(32 * 128, 1024)
    out["sgate_wT"] = np.ascontiguousarray(np.asarray(inp["L0_ffn_sgate_w"]).T)
    out["sup_wT"] = np.ascontiguousarray(np.asarray(inp["L0_ffn_sup_w"]).T)
    out["sdown_wT"] = np.ascontiguousarray(np.asarray(inp["L0_ffn_sdown_w"]).T)
    out["ident_f"] = np.eye(128, dtype=np.float32)
    return out


def emit_l1_attn(cx, dram):
    pb, nc = cx.pb, cx.nc
    C1 = 20.0

    ao = [cx.sb("a1o", [128, T], BF16) for c in range(KCH)]

    with ExitStack() as mid:
        with ExitStack() as xnst:
            xn = [cx.sb("x1n", [128, T], BF16, xnst) for c in range(KCH)]
            with ExitStack() as ph0:
                xs = XStream(cx, dram["x_in"], ph0, nbuf=2, tag="xa0")
                rb_sb, ab_sb, breg = emit_ln_full(cx, xs.fetch, ph0, bf16_src=False)
                emit_ln_apply(cx, xs.fetch, xn, lambda c: f"x1n{c}", rb_sb, ab_sb, breg=breg)
                pb.barrier()

            qf = [cx.sb("q1f", [128, T], BF16, mid, side="right") for m in range(KCH)]
            kf = [cx.sb("k1f", [128, T], BF16, mid, side="right") for m in range(KCH)]
            v = [cx.sb("v1t", [128, 1040], BF16, mid, side="right") for t in range(16)]
            cmask = cx.sb("cmask", [128, 4 * 512], BF16, mid, side="right")
            for d4 in range(4):
                pb.emit("sync", lambda h, d4=d4: h.dma_start(
                    out=cmask[:, 512 * d4:512 * (d4 + 1)], in_=dram["cmask"][128 * d4:128 * (d4 + 1), :]),
                    w=["cmask"], dma=True)

            with ExitStack() as ph1:
                wst = [None, None]
                ppAv1 = [cx.ps("p1A", [128, 512], ph1) for i in range(2)]
                ppBv1 = [cx.ps("p1B", [128, 512], ph1) for i in range(2)]
                for wnm, outs, outreg in [("Wq", qf, "q1f"), ("Wk", kf, "k1f")]:
                    with ExitStack() as ws:
                        wt = cx.sb("w1t", [128, 8 * 1024], BF16, ws)
                        emit_load_weight(cx, dram[wnm], wt, "w1t", KCH, 1024, wst[0], "w1stage0")
                        for mt in range(KCH):
                            for nch in range(4):
                                lo = nch * 512
                                pp = (4 * mt + nch) % 2
                                ppA, par_ = ppAv1[pp], f"p1A{pp}"
                                for c in range(KCH):
                                    lh = wt[:, 1024 * c + 128 * mt:1024 * c + 128 * mt + 128]
                                    rh = xn[c][:, lo:lo + 512]
                                    pb.emit("tensor", lambda h, lh=lh, rh=rh, st=(c == 0), sp=(c == KCH - 1), ppA=ppA: h.matmul(
                                        ppA[:], lh, rh, start=st, stop=sp, skip_group_check=True),
                                        r=["w1t", f"x1n{c}"], w=[par_])
                                pb.emit("vector", lambda h, mt=mt, lo=lo, outs=outs, outreg=outreg, ppA=ppA: h.tensor_copy(
                                    out=outs[mt][:, lo:lo + 512], in_=ppA[:]),
                                    r=[par_], w=[f"{outreg}{mt}"])
                        pb.barrier()
                with ExitStack() as ws:
                    wt = cx.sb("w1v", [128, 8 * 1040], BF16, ws)
                    emit_load_weight(cx, dram["Wv"], wt, "w1v", KCH, 1040, wst[1], "w1stage1")
                    itv = 0
                    for tt in range(16):
                        for lo in range(0, 1040, 512):
                            nn = min(512, 1040 - lo)
                            pp = itv % 2
                            itv += 1
                            ppB, pbr_ = ppBv1[pp], f"p1B{pp}"
                            for c in range(KCH):
                                lh = xn[c][:, 128 * tt:128 * tt + 128]
                                rh = wt[:, 1040 * c + lo:1040 * c + lo + nn]
                                pb.emit("tensor", lambda h, lh=lh, rh=rh, st=(c == 0), sp=(c == KCH - 1), nn=nn, ppB=ppB: h.matmul(
                                    ppB[:, 0:nn], lh, rh, start=st, stop=sp, skip_group_check=True),
                                    r=["w1v", f"x1n{c}"], w=[pbr_])
                            pb.emit("vector", lambda h, tt=tt, lo=lo, nn=nn, ppB=ppB: h.tensor_copy(
                                out=v[tt][:, lo:lo + nn], in_=ppB[:, 0:nn]),
                                r=[pbr_], w=[f"v1t{tt}"])
                        def ones_fn(h, tt=tt):
                            vv = v[tt].ap().rearrange("p (g w) -> p g w", w=65)[:, :, 64:65]
                            return h.memset(vv, 1.0)
                        pb.emit("vector", ones_fn, r=[f"v1t{tt}"], w=[f"v1t{tt}"])
                    pb.barrier()
            pb.barrier()

            with ExitStack() as ph2:
                ps_s = [cx.ps("p1s", [128, 1024], ph2) for i in range(2)]
                ps_ctx2 = [cx.ps("p1ctx", [65, 512], ph2) for i in range(2)]
                ps_bc = cx.ps("p1cb", [64, 512], ph2)
                probs = [cx.sb("p1rb", [128, 1024], BF16, ph2) for i in range(2)]
                ctxb2 = [cx.sb("c1b", [64, 512], BF16, ph2) for i in range(2)]
                rrow2 = [cx.sb("r1row", [1, 512], F32, ph2) for i in range(2)]
                negc = const_col(cx, -C1)

                for h_i in range(16):
                    mt, rb0 = h_i // 2, 64 * (h_i % 2)
                    qrows = slice(rb0, rb0 + 64)
                    for qc in range(4):
                        par = (4 * h_i + qc) % 2
                        ps_ctx, creg = ps_ctx2[par], f"p1ctx{par}"
                        ctxb, cbreg = ctxb2[par], f"c1b{par}"
                        rrow, rreg = rrow2[par], f"r1row{par}"
                        qlo = 512 * qc
                        nkt = 4 * qc + 4
                        pairs = [(2 * r_, 2 * r_ + 1) if 2 * r_ + 1 < nkt else (2 * r_, -1)
                                 for r_ in range((nkt + 1) // 2)]
                        n_pv = nkt
                        pv_i = 0
                        def emit_qk_round1(r_):
                            k0, k1 = pairs[r_]
                            pbuf = r_ % 2
                            sreg = f"p1s{pbuf}"
                            for gi, kt in enumerate([k0, k1]):
                                if kt < 0:
                                    continue
                                lh = kf[mt][qrows, 128 * kt:128 * kt + 128]
                                rh = qf[mt][qrows, qlo:qlo + 512]
                                pb.emit("tensor", lambda h, lh=lh, rh=rh, gi=gi, pbuf=pbuf, rb0=rb0: h.matmul(
                                    ps_s[pbuf][:, 512 * gi:512 * gi + 512], lh, rh,
                                    start=True, stop=True, skip_group_check=True, tile_position=(rb0, 0)),
                                    r=[f"k1f{mt}", f"q1f{mt}"], w=[sreg])
                        emit_qk_round1(0)
                        for r_, (k0, k1) in enumerate(pairs):
                            pbuf = r_ % 2
                            sreg, preg = f"p1s{pbuf}", f"p1rb{pbuf}"
                            if r_ + 1 < len(pairs):
                                emit_qk_round1(r_ + 1)
                            wid = 1024 if k1 >= 0 else 512
                            pb.emit("scalar", lambda h, pbuf=pbuf, wid=wid, negc=negc: h.activation(
                                out=probs[pbuf][:, 0:wid], in_=ps_s[pbuf][:, 0:wid],
                                func=AF.Exp, bias=negc, scale=0.125),
                                r=[sreg, "cc"], w=[preg])
                            for gi, kt in enumerate([k0, k1]):
                                if kt < 0:
                                    continue
                                d = kt - 4 * qc
                                if d >= 0:
                                    pb.emit("gpsimd", lambda h, gi=gi, d=d, pbuf=pbuf: h.tensor_tensor(
                                        out=probs[pbuf][:, 512 * gi:512 * gi + 512],
                                        in0=probs[pbuf][:, 512 * gi:512 * gi + 512],
                                        in1=cmask[:, 512 * d:512 * d + 512], op=OP.mult),
                                        r=[preg, "cmask"], w=[preg])
                                pv_i += 1
                                lh = v[kt][:, 65 * h_i:65 * h_i + 65]
                                rh = probs[pbuf][0:128, 512 * gi:512 * gi + 512]
                                pb.emit("tensor", lambda h, lh=lh, rh=rh, st=(pv_i == 1), sp=(pv_i == n_pv), ps_ctx=ps_ctx: h.matmul(
                                    ps_ctx[:], lh, rh, start=st, stop=sp, skip_group_check=True),
                                    r=[f"v1t{kt}", preg], w=[creg])
                        pb.emit("vector", lambda h, rrow=rrow, ps_ctx=ps_ctx: h.tensor_scalar(
                            out=rrow[:], in0=ps_ctx[64:65, 0:512], scalar1=1e-30, scalar2=None, op0=OP.max),
                            r=[creg], w=[rreg], risky=True)
                        pb.emit("vector", lambda h, rrow=rrow: h.reciprocal(out=rrow[:], in_=rrow[:]),
                                r=[rreg], w=[rreg], risky=True)
                        pb.emit("tensor", lambda h, rrow=rrow: h.matmul(ps_bc[:], cx.ones_f[0:1, 0:64], rrow[:],
                                                             start=True, stop=True, skip_group_check=True),
                                r=["ones_f", rreg], w=["p1cb"])
                        pb.emit("scalar", lambda h, ctxb=ctxb, ps_ctx=ps_ctx: h.activation(out=ctxb[:], in_=ps_ctx[0:64, 0:512],
                                                                 func=AF.Copy),
                                r=[creg], w=[cbreg])
                        pb.emit("vector", lambda h, mt=mt, rb0=rb0, qlo=qlo, ctxb=ctxb: h.tensor_tensor(
                            out=ao[mt][rb0:rb0 + 64, qlo:qlo + 512], in0=ctxb[:], in1=ps_bc[:], op=OP.mult),
                            r=[cbreg, "p1cb"], w=[f"a1o{mt}"])
                pb.barrier()
    pb.barrier()

    wout = [cx.sb("wo_out", [128, T], BF16) for c in range(KCH)]
    with ExitStack() as ph3:
        wst = None
        wt = cx.sb("w3t", [128, 8 * 1024], BF16, ph3)
        emit_load_weight(cx, dram["Wo"], wt, "w3t", KCH, 1024, wst, "w3stage")
        ppAv3 = [cx.ps("p3A", [128, 512], ph3) for i in range(2)]
        for mt in range(KCH):
            for nch in range(4):
                lo = nch * 512
                pp = (4 * mt + nch) % 2
                ppA, par_ = ppAv3[pp], f"p3A{pp}"
                for c in range(KCH):
                    lh = wt[:, 1024 * c + 128 * mt:1024 * c + 128 * mt + 128]
                    rh = ao[c][:, lo:lo + 512]
                    pb.emit("tensor", lambda h, lh=lh, rh=rh, st=(c == 0), sp=(c == KCH - 1), ppA=ppA: h.matmul(
                        ppA[:], lh, rh, start=st, stop=sp, skip_group_check=True),
                        r=["w3t", f"a1o{c}"], w=[par_])
                pb.emit("vector", lambda h, mt=mt, lo=lo, ppA=ppA: h.tensor_copy(
                    out=wout[mt][:, lo:lo + 512], in_=ppA[:]),
                    r=[par_], w=[f"wo_out{mt}"])
        pb.barrier()

    with ExitStack() as ph4:
        rb_sb, ab_sb, breg = emit_ln_full(cx, lambda c: (wout[c], f"wo_out{c}"), ph4, bf16_src=True)
        u = [cx.sb("u1", [128, T], BF16, ph4) for c in range(KCH)]
        xs4 = XStream(cx, dram["x_in"], ph4, tag="xa4")
        def store(c, xtile, xreg):
            pb.emit("scalar", lambda h, c=c, xtile=xtile: h.dma_start(
                out=dram["x_out"][128 * c:128 * (c + 1), :], in_=xtile[:]),
                r=[xreg], dma=True)
        emit_ln_apply(cx, lambda c: (wout[c], f"wo_out{c}"), u, lambda c: f"u1{c}", rb_sb, ab_sb,
                      residual_store=(xs4.fetch, store), breg=breg)
        pb.barrier()


def build_l1_attn_prog():
    nc = bass.Bass("TRN2", target_bir_lowering=False)
    dram = {}
    dram["x_in"] = nc.dram_tensor("x_in", [D, T], F32, kind="ExternalInput").ap()
    for nm, sh in [("Wq", [D, D]), ("Wk", [D, D]), ("Wv", [D, 1040]), ("Wo", [D, D])]:
        dram[nm] = nc.dram_tensor(nm, sh, F32, kind="ExternalInput").ap()
    dram["cmask"] = nc.dram_tensor("cmask", [4 * 128, 512], BF16, kind="ExternalInput").ap()
    dram["x_out"] = nc.dram_tensor("x_out", [D, T], F32, kind="ExternalOutput").ap()

    with ExitStack() as st:
        sems = {e: st.enter_context(nc.semaphore(f"sem_{e}")) for e in ENG}
        pb = PB(nc)
        cx = Ctx(nc, pb, st)
        make_consts(cx)
        blk = st.enter_context(nc.Block())
        emit_l1_attn(cx, dram)
        pb.realize(blk, sems)
    return nc


def prep_l1_attn_weights(inp):
    out = {}
    out["Wq"] = np.asarray(inp["L1_attn_Wq"])
    out["Wk"] = np.asarray(inp["L1_attn_Wk"])
    Wv = np.asarray(inp["L1_attn_Wv"])
    Wva = np.zeros((D, 1040), np.float32)
    for h in range(16):
        Wva[:, 65 * h:65 * h + 64] = Wv[:, 64 * h:64 * h + 64]
    out["Wv"] = Wva
    out["Wo"] = np.asarray(inp["L1_attn_Wo"])
    cm = np.zeros((4 * 128, 512), np.float32)
    for d in range(4):
        i = np.arange(128)[:, None]
        j = np.arange(512)[None, :]
        cm[128 * d:128 * (d + 1)] = (j >= 128 * d + i).astype(np.float32)
    out["cmask"] = cm.astype(BF)
    return out


def emit_l1_ffn(cx, dram):
    pb, nc = cx.pb, cx.nc
    TP = 2049

    xi = [cx.sb("xi", [128, TP], F32) for c in range(KCH)]
    for c in range(KCH):
        pb.emit("vector", lambda h, c=c: h.memset(xi[c][:, 0:1], 0.0), w=[f"xi{c}"])

    ind2b(cx)
    mle = cx.sb("mle", [128, 128], BF16)
    mge = cx.sb("mge", [128, 128], BF16)
    idf = cx.sb("lidf", [128, 128], F32)
    pb.emit("sync", lambda h: h.dma_start(out=mle[:], in_=dram["mle"]), w=["mle"], dma=True)
    pb.emit("sync", lambda h: h.dma_start(out=mge[:], in_=dram["mge"]), w=["mge"], dma=True)
    pb.emit("sync", lambda h: h.dma_start(out=idf[:], in_=dram["ident_f"]), w=["idf"], dma=True)

    with ExitStack() as ph0:
        xs = XStream(cx, dram["x_in"], ph0, tag="xl0")
        rb_sb, ab_sb, breg = emit_ln_full(cx, xs.fetch, ph0, bf16_src=False)
        emit_ln_apply(cx, xs.fetch, xi, lambda c: f"xi{c}", rb_sb, ab_sb, off_out=1, breg=breg)
        dbg_dump(cx, dram, "d_xi", xi[0][:, 1:TP], ["xi0"])
        pb.barrier()

    def _do_layer(layer):
        lay = ExitStack()
        xb = [cx.sb("xib", [128, TP], BF16, lay) for c in range(KCH)]
        tabs = {nm: cx.sb("tab" + nm, [128, T], BF16, lay) for nm in ["SQ", "TQ", "SK", "TK"]}
        with ExitStack() as tload:
            tstage = cx.sb("tstage", [128, T], F32, tload)
            for nm in ["SQ", "TQ", "SK", "TK"]:
                pb.emit("sync", lambda h, nm=nm: h.dma_start(out=tstage[:], in_=dram["tab_" + nm]),
                        w=["tstage"], dma=True)
                pb.emit("vector", lambda h, nm=nm: h.tensor_copy(out=tabs[nm][:], in_=tstage[:]),
                        r=["tstage"], w=["tab" + nm])
            pb.barrier()
        for c in range(KCH):
            pb.emit("vector", lambda h, c=c: h.memset(xb[c][:, 0:1], 0.0), w=[f"xib{c}"])
            pb.emit("vector", lambda h, c=c: h.tensor_copy(out=xb[c][:, 1:TP], in_=xi[c][:, 1:TP]),
                    r=[f"xi{c}"], w=[f"xib{c}"])

        def hview(c, lo, nn):
            o = 1 if c < 4 else 0
            return xb[c][:, o + lo:o + lo + nn]

        qf = [cx.sb("lqf", [128, T], BF16, lay) for m in range(4)]
        kf = [cx.sb("lkf", [128, T], BF16, lay) for m in range(4)]
        vt = [cx.sb("lvt", [128, 520], BF16, lay) for t in range(16)]
        ofm = [cx.sb("lofm", [128, T], BF16, lay) for m in range(4)]

        with ExitStack() as ph1:
            wst = [None, None]
            ppAvL = [cx.ps("lpA", [128, 512], ph1) for i in range(2)]
            for wnm, outs, outreg in [("wq", qf, "lqf"), ("wk", kf, "lkf")]:
                with ExitStack() as ws:
                    wt = cx.sb("lwt", [128, 8 * 512], BF16, ws)
                    emit_load_weight(cx, dram[wnm][D * layer:D * (layer + 1), :], wt,
                                     "lwt", KCH, 512, wst[0], "lwst0")
                    for mt in range(4):
                        for nch in range(4):
                            lo = nch * 512
                            pp = (4 * mt + nch) % 2
                            ppA, par_ = ppAvL[pp], f"lpA{pp}"
                            for c in range(KCH):
                                lh = wt[:, 512 * c + 128 * mt:512 * c + 128 * mt + 128]
                                rh = hview(c, lo, 512)
                                pb.emit("tensor", lambda h, lh=lh, rh=rh, st=(c == 0), sp=(c == KCH - 1), ppA=ppA: h.matmul(
                                    ppA[:], lh, rh, start=st, stop=sp, skip_group_check=True),
                                    r=["lwt", f"xib{c}"], w=[par_])
                            pb.emit("vector", lambda h, mt=mt, lo=lo, outs=outs, outreg=outreg, ppA=ppA: h.tensor_copy(
                                out=outs[mt][:, lo:lo + 512], in_=ppA[:]),
                                r=[par_], w=[f"{outreg}{mt}"])
                    pb.barrier()
            with ExitStack() as ws:
                wt = cx.sb("lwv", [128, 8 * 520], BF16, ws)
                emit_load_weight(cx, dram["wv"][D * layer:D * (layer + 1), :], wt,
                                 "lwv", KCH, 520, wst[1], "lwst1")
                itv = 0
                for tt in range(16):
                    for lo, nn in [(0, 512), (512, 8)]:
                        pp = itv % 2
                        itv += 1
                        ppA, par_ = ppAvL[pp], f"lpA{pp}"
                        for c in range(KCH):
                            lh = hview(c, 128 * tt, 128)
                            rh = wt[:, 520 * c + lo:520 * c + lo + nn]
                            pb.emit("tensor", lambda h, lh=lh, rh=rh, st=(c == 0), sp=(c == KCH - 1), nn=nn, ppA=ppA: h.matmul(
                                ppA[:, 0:nn], lh, rh, start=st, stop=sp, skip_group_check=True),
                                r=["lwv", f"xib{c}"], w=[par_])
                        pb.emit("vector", lambda h, tt=tt, lo=lo, nn=nn, ppA=ppA: h.tensor_copy(
                            out=vt[tt][:, lo:lo + nn], in_=ppA[:, 0:nn]),
                            r=[par_], w=[f"lvt{tt}"])
                    def ones_fn(h, tt=tt):
                        vv = vt[tt].ap().rearrange("p (g w) -> p g w", w=65)[:, :, 64:65]
                        return h.memset(vv, 1.0)
                    pb.emit("vector", ones_fn, r=[f"lvt{tt}"], w=[f"lvt{tt}"])
                pb.barrier()

        with ExitStack() as ph2:
            sqtv = [cx.sb("lsqt", [128, 512], BF16, ph2) for i in range(2)]
            nrowt = cx.sb("lnrow", [33, T], F32, ph2)
            swp = cx.sb("lswp", [128, T // 2], BF16, ph2)
            qsrow = cx.sb("lqs", [33, 64], F32, ph2)
            ph2ps = ExitStack()
            ps_nv = [cx.ps("lps_n", [1, 512], ph2ps) for i in range(2)]
            ps_ov = [cx.ps("lps_o", [64, 512], ph2ps) for i in range(2)]

            for qk_i, (ff, freg, snm, tnm, scnm) in enumerate([
                    (qf, "lqf", "SQ", "TQ", "q_scale"), (kf, "lkf", "SK", "TK", "k_scale")]):
                pb.emit("sync", lambda h, scnm=scnm: h.dma_start(
                    out=qsrow[0:1, :], in_=dram[scnm][layer:layer + 1, :]), w=["lqs"], dma=True)
                pb.emit("sync", lambda h, scnm=scnm: h.dma_start(
                    out=qsrow[32:33, :], in_=dram[scnm][layer:layer + 1, :]), w=["lqs"], dma=True)
                for mt in range(4):
                    for hh2 in range(2):
                        hh = 2 * mt + hh2
                        pn = hh % 2
                        ps_n, pnr = ps_nv[pn], f"lps_n{pn}"
                        nrow, nrr = nrowt[32 * pn:32 * pn + 1, :], f"lnrow{pn}"
                        for nch in range(4):
                            lo = nch * 512
                            pq = nch % 2
                            sqt, sqreg = sqtv[pq], f"lsqt{pq}"
                            pb.emit("scalar", lambda h, mt=mt, ff=ff, lo=lo, sqt=sqt: h.activation(
                                out=sqt[:], in_=ff[mt][:, lo:lo + 512], func=AF.Square),
                                r=[f"{freg}{mt}"], w=[sqreg])
                            ind_ap = ind2b(cx)[:, hh2:hh2 + 1]
                            pb.emit("tensor", lambda h, ind_ap=ind_ap, sqt=sqt, ps_n=ps_n: h.matmul(
                                ps_n[:], ind_ap, sqt[:],
                                start=True, stop=True, skip_group_check=True),
                                r=["lindb", sqreg], w=[pnr])
                            pb.emit("vector", lambda h, lo=lo, ps_n=ps_n, pn=pn: h.tensor_copy(
                                out=nrowt[32 * pn:32 * pn + 1, lo:lo + 512], in_=ps_n[:]),
                                r=[pnr], w=[nrr])
                        pb.emit("vector", lambda h, nrow=nrow: h.tensor_scalar(
                            out=nrow, in0=nrow, scalar1=1e-24, scalar2=None, op0=OP.max),
                            r=[nrr], w=[nrr])
                        pb.emit("scalar", lambda h, nrow=nrow: h.activation(out=nrow, in_=nrow, func=AF.Ln),
                                r=[nrr], w=[nrr])
                        pb.emit("scalar", lambda h, nrow=nrow: h.activation(out=nrow, in_=nrow, func=AF.Exp,
                                                                 scale=-0.5),
                                r=[nrr], w=[nrr])
                        for nch in range(4):
                            lo = nch * 512
                            po = (2 * hh + nch) % 2
                            ps_o, por = ps_ov[po], f"lps_o{po}"
                            pb.emit("tensor", lambda h, lo=lo, ps_o=ps_o, pn=pn: h.matmul(
                                ps_o[:], qsrow[32 * pn:32 * pn + 1, :], nrowt[32 * pn:32 * pn + 1, lo:lo + 512],
                                start=True, stop=True, skip_group_check=True),
                                r=["lqs", nrr], w=[por])
                            rb0 = 64 * hh2
                            ap = ff[mt][rb0:rb0 + 64, lo:lo + 512]
                            pb.emit("vector", lambda h, ap=ap, ps_o=ps_o: h.tensor_tensor(
                                out=ap, in0=ap, in1=ps_o[:], op=OP.mult),
                                r=[f"{freg}{mt}", por], w=[f"{freg}{mt}"])
                    stab, ttab = tabs[snm], tabs[tnm]
                    for half in range(2):
                        hlo = 1024 * half
                        for hh2 in range(2):
                            rb0 = 64 * hh2
                            pb.emit("gpsimd", lambda h, ff=ff, mt=mt, rb0=rb0, hlo=hlo: h.dma_start(
                                out=swp[rb0 + 32:rb0 + 64, :], in_=ff[mt][rb0:rb0 + 32, hlo:hlo + 1024]),
                                r=[f"{freg}{mt}"], w=["lswp"], dma=True)
                            pb.emit("gpsimd", lambda h, ff=ff, mt=mt, rb0=rb0, hlo=hlo: h.dma_start(
                                out=swp[rb0:rb0 + 32, :], in_=ff[mt][rb0 + 32:rb0 + 64, hlo:hlo + 1024]),
                                r=[f"{freg}{mt}"], w=["lswp"], dma=True)
                        pb.emit("vector", lambda h, ff=ff, mt=mt, stab=stab, hlo=hlo: h.tensor_tensor(
                            out=ff[mt][:, hlo:hlo + 1024], in0=ff[mt][:, hlo:hlo + 1024],
                            in1=stab[:, hlo:hlo + 1024], op=OP.mult),
                            r=[f"{freg}{mt}", "tab" + snm], w=[f"{freg}{mt}"])
                        pb.emit("vector", lambda h, ttab=ttab, hlo=hlo: h.tensor_tensor(
                            out=swp[:], in0=swp[:], in1=ttab[:, hlo:hlo + 1024], op=OP.mult),
                            r=["lswp", "tab" + tnm], w=["lswp"])
                        pb.emit("vector", lambda h, ff=ff, mt=mt, hlo=hlo: h.tensor_tensor(
                            out=ff[mt][:, hlo:hlo + 1024], in0=ff[mt][:, hlo:hlo + 1024],
                            in1=swp[:], op=OP.add),
                            r=[f"{freg}{mt}", "lswp"], w=[f"{freg}{mt}"])
                    pass
            pb.barrier()

            pb.barrier()
            ph2ps.close()
            with ExitStack() as ph3:
                ps_sc2 = [cx.ps("lps_sc", [128, 1024], ph3) for i in range(2)]
                p1ps = ExitStack()
                ps_tp = cx.ps("lps_tp", [1, 512], p1ps)
                ps_b2 = cx.ps("lps_b2", [128, 512], p1ps)
                prb2 = [cx.sb("lprb", [128, 640], BF16, ph3) for i in range(2)]
                ctxb2l = [cx.sb("lctxb", [64, 128], BF16, ph3) for i in range(2)]
                rrow2l = [cx.sb("lrrow", [1, 128], F32, ph3) for i in range(2)]
                small4 = cx.sb("lsm4", [128, 4], F32, ph3)
                mcol2 = [small4.ap()[:, 0:1], small4.ap()[:, 1:2]]
                mrow1 = cx.sb("lmrow1", [1, 128], F32, ph3)
                negBrow = cx.sb("lnegB", [1, 128], F32, ph3)
                Bbl = cx.sb("lBb", [128, 128], F32, ph3)

                its = [(hh, qc) for hh in range(8) for qc in range(16)]

                def qk_emit(idx):
                    hh, qc = its[idx]
                    mt, rb0 = hh // 2, 64 * (hh % 2)
                    qrows = slice(rb0, rb0 + 64)
                    par = idx % 2
                    ps_sc, screg = ps_sc2[par], f"lps_sc{par}"
                    r0 = max(0, qc - 4)
                    kts = list(range(r0, qc + 1))
                    for g, r_ in enumerate(kts):
                        lh = kf[mt][qrows, 128 * r_:128 * r_ + 128]
                        rh = qf[mt][qrows, 128 * qc:128 * qc + 128]
                        pb.emit("tensor", lambda h, lh=lh, rh=rh, g=g, rb0=rb0, ps_sc=ps_sc: h.matmul(
                            ps_sc[:, 128 * g:128 * g + 128], lh, rh,
                            start=True, stop=True, skip_group_check=True, tile_position=(rb0, 0)),
                            r=[f"lkf{mt}", f"lqf{mt}"], w=[screg])
                    return kts

                kts_cache = {}
                for idx in range(len(its) + 1):
                    if idx < len(its):
                        kts_cache[idx] = qk_emit(idx)
                    if idx >= 1:
                        i0 = idx - 1
                        hh, qc = its[i0]
                        par = i0 % 2
                        ps_sc, screg = ps_sc2[par], f"lps_sc{par}"
                        mcol, mcreg = mcol2[par], f"lmcol{par}"
                        wid = 128 * len(kts_cache[i0])
                        pb.emit("vector", lambda h, wid=wid, mcol=mcol, ps_sc=ps_sc: h.reduce_max(
                            out=mcol, in_=ps_sc[:, 0:wid], axis=AX.X),
                            r=[screg], w=[mcreg], risky=True)
                        pb.emit("tensor", lambda h, mcol=mcol: h.transpose(ps_tp[:, 0:128], mcol, idf[:]),
                                r=[mcreg, "idf"], w=["lps_tp"])
                        pb.emit("vector", lambda h: h.tensor_copy(out=mrow1[:], in_=ps_tp[0:1, 0:128]),
                                r=["lps_tp"], w=["lmrow1"])
                        col = 16 * hh + qc
                        pb.emit("vector", lambda h, col=col: h.reduce_max(
                            out=negBrow[0:1, col:col + 1], in_=mrow1[:], axis=AX.X, negate=True),
                            r=["lmrow1"], w=["lnegB"], risky=True)
                pb.emit("vector", lambda h: h.tensor_scalar(out=negBrow[:], in0=negBrow[:],
                                                            scalar1=8.0, scalar2=None, op0=OP.mult),
                        r=["lnegB"], w=["lnegB"])
                pb.emit("tensor", lambda h: h.matmul(ps_b2[:, 0:128], cx.ones_f[0:1, 0:128], negBrow[:],
                                                     start=True, stop=True, skip_group_check=True),
                        r=["ones_f", "lnegB"], w=["lps_b2"])
                pb.emit("vector", lambda h: h.tensor_copy(out=Bbl[:], in_=ps_b2[:, 0:128]),
                        r=["lps_b2"], w=["lBb"])
                pb.barrier()
                p1ps.close()
                ps_ctx2l = [cx.ps("lps_ctx", [65, 512], ph3) for i in range(2)]
                ps_cb = cx.ps("lps_cb", [64, 512], ph3)

                def rest_emit(i0):
                    hh, qc = its[i0]
                    mt, rb0 = hh // 2, 64 * (hh % 2)
                    par = i0 % 2
                    ps_sc, screg = ps_sc2[par], f"lps_sc{par}"
                    prb, preg2 = prb2[par], f"lprb{par}"
                    ps_ctx, creg = ps_ctx2l[par], f"lps_ctx{par}"
                    ctxb, cbreg = ctxb2l[par], f"lctxb{par}"
                    rrow, rreg = rrow2l[par], f"lrrow{par}"
                    kts = kts_cache[i0]
                    wid = 128 * len(kts)
                    col = 16 * hh + qc
                    pb.emit("scalar", lambda h, wid=wid, col=col, prb=prb, ps_sc=ps_sc: h.activation(
                        out=prb[:, 0:wid], in_=ps_sc[:, 0:wid], func=AF.Exp,
                        bias=Bbl[:, col:col + 1], scale=8.0),
                        r=[screg, "lBb"], w=[preg2])
                    for g, r_ in enumerate(kts):
                        if r_ == qc - 4:
                            pb.emit("gpsimd", lambda h, g=g, prb=prb: h.tensor_tensor(
                                out=prb[:, 128 * g:128 * g + 128], in0=prb[:, 128 * g:128 * g + 128],
                                in1=mle[:], op=OP.mult),
                                r=[preg2, "mle"], w=[preg2])
                        if r_ == qc:
                            pb.emit("gpsimd", lambda h, g=g, prb=prb: h.tensor_tensor(
                                out=prb[:, 128 * g:128 * g + 128], in0=prb[:, 128 * g:128 * g + 128],
                                in1=mge[:], op=OP.mult),
                                r=[preg2, "mge"], w=[preg2])
                    for g, r_ in enumerate(kts):
                        lh = vt[r_][:, 65 * hh:65 * hh + 65]
                        rh = prb[:, 128 * g:128 * g + 128]
                        pb.emit("tensor", lambda h, lh=lh, rh=rh, st=(g == 0), sp=(g == len(kts) - 1), ps_ctx=ps_ctx: h.matmul(
                            ps_ctx[:, 0:128], lh, rh, start=st, stop=sp, skip_group_check=True),
                            r=[f"lvt{r_}", preg2], w=[creg])
                    pb.emit("vector", lambda h, rrow=rrow, ps_ctx=ps_ctx: h.tensor_scalar(
                        out=rrow[:], in0=ps_ctx[64:65, 0:128], scalar1=1e-30, scalar2=None, op0=OP.max),
                        r=[creg], w=[rreg], risky=True)
                    pb.emit("vector", lambda h, rrow=rrow: h.reciprocal(out=rrow[:], in_=rrow[:]),
                            r=[rreg], w=[rreg], risky=True)
                    pb.emit("tensor", lambda h, rrow=rrow: h.matmul(ps_cb[:, 0:128], cx.ones_f[0:1, 0:64], rrow[:],
                                                         start=True, stop=True, skip_group_check=True),
                            r=["ones_f", rreg], w=["lps_cb"])
                    pb.emit("scalar", lambda h, ctxb=ctxb, ps_ctx=ps_ctx: h.activation(
                        out=ctxb[:], in_=ps_ctx[0:64, 0:128], func=AF.Copy),
                        r=[creg], w=[cbreg])
                    pb.emit("vector", lambda h, mt=mt, rb0=rb0, qc=qc, ctxb=ctxb: h.tensor_tensor(
                        out=ofm[mt][rb0:rb0 + 64, 128 * qc:128 * qc + 128], in0=ctxb[:],
                        in1=ps_cb[:, 0:128], op=OP.mult),
                        r=[cbreg, "lps_cb"], w=[f"lofm{mt}"])

                for idx in range(len(its) + 1):
                    if idx < len(its):
                        qk_emit(idx)
                    if idx >= 1:
                        rest_emit(idx - 1)
                pb.barrier()

        with ExitStack() as ph4:
            wst = None
            wt = cx.sb("lw4t", [128, 4 * 1024], BF16, ph4)
            emit_load_weight(cx, dram["wo"][512 * layer:512 * (layer + 1), :], wt, "lw4t",
                             4, 1024, wst, "lw4st")
            ppAv4 = [cx.ps("lp4A", [128, 512], ph4) for i in range(2)]
            for mt in range(KCH):
                for nch in range(4):
                    lo = nch * 512
                    pp = (4 * mt + nch) % 2
                    ppA, par_ = ppAv4[pp], f"lp4A{pp}"
                    for c4 in range(4):
                        lh = wt[:, 1024 * c4 + 128 * mt:1024 * c4 + 128 * mt + 128]
                        rh = ofm[c4][:, lo:lo + 512]
                        pb.emit("tensor", lambda h, lh=lh, rh=rh, st=(c4 == 0), sp=(c4 == 3), ppA=ppA: h.matmul(
                            ppA[:], lh, rh, start=st, stop=sp, skip_group_check=True),
                            r=["lw4t", f"lofm{c4}"], w=[par_])
                    pb.emit("vector", lambda h, mt=mt, lo=lo, ppA=ppA: h.tensor_tensor(
                        out=xi[mt][:, 1 + lo:1 + lo + 512], in0=xi[mt][:, 1 + lo:1 + lo + 512],
                        in1=ppA[:], op=OP.add),
                        r=[par_, f"xi{mt}"], w=[f"xi{mt}"])
            if layer == 0:
                dbg_dump(cx, dram, "d_o", ofm[0][:], ["lofm0"])
                dbg_dump(cx, dram, "d_x1", xi[0][:, 1:TP], ["xi0"])
            pb.barrier()
        lay.close()
        pb.barrier()


        with ExitStack() as ph5:
            lgc = cx.sb("llg", [128, 8], F32, ph5)
            lbc = cx.sb("llb", [128, 8], F32, ph5)
            for c in range(KCH):
                pb.emit("sync", lambda h, c=c: h.dma_start(
                    out=lgc[:, c:c + 1], in_=dram["ln_g"][layer:layer + 1, 128 * c:128 * (c + 1)].rearrange("a p -> p a")),
                    w=["llg"], dma=True)
                pb.emit("sync", lambda h, c=c: h.dma_start(
                    out=lbc[:, c:c + 1], in_=dram["ln_b"][layer:layer + 1, 128 * c:128 * (c + 1)].rearrange("a p -> p a")),
                    w=["llb"], dma=True)
            lnf = [cx.sb("llnf", [128, T], BF16, ph5) for c in range(KCH)]
            ph5a = ExitStack()
            rb_sb, ab_sb, breg = emit_ln_full(
                cx, lambda c: (xi[c], f"xi{c}"), ph5a, bf16_src=False,
                off=(lambda c: 1 if c < 4 else 0))
            for c in range(KCH):
                o = 1 if c < 4 else 0
                for nch in range(4):
                    lo = nch * 512
                    sap = xi[c][:, o + lo:o + lo + 512]
                    pb.emit("vector", lambda h, sap=sap, c=c, lo=lo: h.tensor_tensor(
                        out=lnf[c][:, lo:lo + 512], in0=sap, in1=rb_sb[:, lo:lo + 512], op=OP.mult),
                        r=[f"xi{c}", breg[0]], w=[f"llnf{c}"])
                    pb.emit("vector", lambda h, c=c, lo=lo: h.tensor_tensor(
                        out=lnf[c][:, lo:lo + 512], in0=lnf[c][:, lo:lo + 512],
                        in1=ab_sb[:, lo:lo + 512], op=OP.add),
                        r=[f"llnf{c}", breg[1]], w=[f"llnf{c}"])
                    pb.emit("vector", lambda h, c=c, lo=lo: h.tensor_scalar(
                        out=lnf[c][:, lo:lo + 512], in0=lnf[c][:, lo:lo + 512],
                        scalar1=lgc[:, c:c + 1], scalar2=lbc[:, c:c + 1],
                        op0=OP.mult, op1=OP.add),
                        r=[f"llnf{c}", "llg", "llb"], w=[f"llnf{c}"])

            pb.barrier()
            ph5a.close()
            b1ac = cx.sb("lb1a", [128, 22], F32, ph5)
            b1gc = cx.sb("lb1g", [128, 22], F32, ph5)
            b2c = cx.sb("lb2", [128, 8], F32, ph5)
            for hc in range(22):
                pb.emit("sync", lambda h, hc=hc: h.dma_start(
                    out=b1ac[:, hc:hc + 1], in_=dram["b1a"][layer:layer + 1, 128 * hc:128 * (hc + 1)].rearrange("a p -> p a")),
                    w=["lb1a"], dma=True)
                pb.emit("sync", lambda h, hc=hc: h.dma_start(
                    out=b1gc[:, hc:hc + 1], in_=dram["b1g"][layer:layer + 1, 128 * hc:128 * (hc + 1)].rearrange("a p -> p a")),
                    w=["lb1g"], dma=True)
            for c in range(KCH):
                pb.emit("sync", lambda h, c=c: h.dma_start(
                    out=b2c[:, c:c + 1], in_=dram["b2"][layer:layer + 1, 128 * c:128 * (c + 1)].rearrange("a p -> p a")),
                    w=["lb2"], dma=True)
            w1stage = [None, None]
            w1at = [cx.sb("lw1a", [128, 8 * 128], BF16, ph5) for i in range(4)]
            w1gt = [cx.sb("lw1g", [128, 8 * 128], BF16, ph5) for i in range(4)]
            w2t = [cx.sb("lw2", [128, 1024], BF16, ph5) for i in range(4)]
            hgrp = [cx.sb("lhg", [128, T], BF16, ph5) for i in range(4)]
            asb = cx.sb("lasb", [128, 512], BF16, ph5)
            gsb = cx.sb("lgsb", [128, 512], BF16, ph5)
            psAv = [cx.ps("lp5A", [128, 512], ph5) for i in range(2)]
            psGv = [cx.ps("lp5G", [128, 512], ph5) for i in range(2)]
            ps2v = [cx.ps("lp5W", [128, 512], ph5) for i in range(2)]
            asbv = [asb, cx.sb("lasb2", [128, 512], BF16, ph5)]
            gsbv = [gsb, cx.sb("lgsb2", [128, 512], BF16, ph5)]
            groups = [list(range(i, min(i + 4, 22))) for i in range(0, 22, 4)]
            for grp in groups:
                for i, hc in enumerate(grp):
                    emit_load_weight(cx, dram["w1a"][(22 * layer + hc) * 1024:(22 * layer + hc + 1) * 1024, :],
                                     w1at[i], f"lw1a{i}", KCH, 128, w1stage[0], "lw1st0")
                    emit_load_weight(cx, dram["w1g"][(22 * layer + hc) * 1024:(22 * layer + hc + 1) * 1024, :],
                                     w1gt[i], f"lw1g{i}", KCH, 128, w1stage[1], "lw1st1")
                    emit_load_weight(cx, dram["w2p"][(22 * layer + hc) * 128:(22 * layer + hc + 1) * 128, :],
                                     w2t[i], f"lw2{i}", 1, 1024, w1stage[0], "lw1st0")
                for i, hc in enumerate(grp):
                    for nch in range(4):
                        lo = nch * 512
                        pp = (4 * i + nch) % 2
                        psA, pAr = psAv[pp], f"lp5A{pp}"
                        psG, pGr = psGv[pp], f"lp5G{pp}"
                        asb_, asr = asbv[pp], f"lasb{pp}"
                        gsb_, gsr = gsbv[pp], f"lgsb{pp}"
                        for c in range(KCH):
                            lh = w1at[i][:, 128 * c:128 * c + 128]
                            pb.emit("tensor", lambda h, lh=lh, c=c, lo=lo, st=(c == 0), sp=(c == KCH - 1), psA=psA: h.matmul(
                                psA[:], lh, lnf[c][:, lo:lo + 512], start=st, stop=sp, skip_group_check=True),
                                r=[f"lw1a{i}", f"llnf{c}"], w=[pAr])
                        for c in range(KCH):
                            lh = w1gt[i][:, 128 * c:128 * c + 128]
                            pb.emit("tensor", lambda h, lh=lh, c=c, lo=lo, st=(c == 0), sp=(c == KCH - 1), psG=psG: h.matmul(
                                psG[:], lh, lnf[c][:, lo:lo + 512], start=st, stop=sp, skip_group_check=True),
                                r=[f"lw1g{i}", f"llnf{c}"], w=[pGr])
                        pb.emit("scalar", lambda h, hc=hc, psG=psG, gsb_=gsb_: h.activation(
                            out=gsb_[:], in_=psG[:], func=AF.Gelu, bias=b1gc[:, hc:hc + 1]),
                            r=[pGr, "lb1g"], w=[gsr])
                        pb.emit("vector", lambda h, hc=hc, psA=psA, asb_=asb_: h.tensor_scalar(
                            out=asb_[:], in0=psA[:], scalar1=b1ac[:, hc:hc + 1], scalar2=None,
                            op0=OP.add),
                            r=[pAr, "lb1a"], w=[asr])
                        pb.emit("vector", lambda h, i=i, lo=lo, asb_=asb_, gsb_=gsb_: h.tensor_tensor(
                            out=hgrp[i][:, lo:lo + 512], in0=asb_[:], in1=gsb_[:], op=OP.mult),
                            r=[asr, gsr], w=[f"lhg{i}"])
                for mt in range(KCH):
                    for nch in range(4):
                        lo = nch * 512
                        pp = (4 * mt + nch) % 2
                        ps2, p2r = ps2v[pp], f"lp5W{pp}"
                        for i, hc in enumerate(grp):
                            lh = w2t[i][:, 128 * mt:128 * mt + 128]
                            pb.emit("tensor", lambda h, lh=lh, i=i, lo=lo, st=(i == 0), sp=(i == len(grp) - 1), ps2=ps2: h.matmul(
                                ps2[:], lh, hgrp[i][:, lo:lo + 512], start=st, stop=sp, skip_group_check=True),
                                r=[f"lw2{i}", f"lhg{i}"], w=[p2r])
                        pb.emit("vector", lambda h, mt=mt, lo=lo, ps2=ps2: h.tensor_tensor(
                            out=xi[mt][:, 1 + lo:1 + lo + 512], in0=xi[mt][:, 1 + lo:1 + lo + 512],
                            in1=ps2[:], op=OP.add),
                            r=[p2r, f"xi{mt}"], w=[f"xi{mt}"])
            if layer == 1:
                dbg_dump(cx, dram, "d_lnf", lnf[0][:], ["llnf0"])
            if layer == 0:
                dbg_dump(cx, dram, "d_lnf0", lnf[0][:], ["llnf0"])

            for mt in range(KCH):
                pb.emit("vector", lambda h, mt=mt: h.tensor_scalar(
                    out=xi[mt][:, 1:TP], in0=xi[mt][:, 1:TP], scalar1=b2c[:, mt:mt + 1],
                    scalar2=None, op0=OP.add),
                    r=[f"xi{mt}", "lb2"], w=[f"xi{mt}"])
            pb.barrier()
        pb.barrier()
        if layer == 0:
            dbg_dump(cx, dram, "d_xiL0", xi[0][:, 1:TP], ["xi0"])

    for layer in range(2):
        _do_layer(layer)

    dbg_dump(cx, dram, "d_xi2", xi[0][:, 1:TP], ["xi0"])
    with ExitStack() as ph9:
        rb_sb, ab_sb, breg = emit_ln_full(cx, lambda c: (xi[c], f"xi{c}"), ph9, bf16_src=False, off=1)
        u = [cx.sb("u9", [128, T], BF16, ph9) for c in range(KCH)]
        xs9 = XStream(cx, dram["x_in"], ph9, tag="xl9")
        def store(c, xtile, xreg):
            pb.emit("scalar", lambda h, c=c, xtile=xtile: h.dma_start(
                out=dram["x_out"][128 * c:128 * (c + 1), :], in_=xtile[:]),
                r=[xreg], dma=True)
        emit_ln_apply(cx, lambda c: (xi[c], f"xi{c}"), u, lambda c: f"u9{c}", rb_sb, ab_sb,
                      off_src=1, residual_store=(xs9.fetch, store), breg=breg)
        pb.barrier()


def ind2b(cx):
    if not hasattr(cx, "_ind2b"):
        cx._ind2b = cx.sb("lindb", [128, 2], BF16)
        pb = cx.pb
        pb.emit("vector", lambda h: h.memset(cx._ind2b[:, 0:2], 0.0), w=["lindb"])
        pb.emit("vector", lambda h: h.memset(cx._ind2b[0:64, 0:1], 1.0), w=["lindb"])
        pb.emit("vector", lambda h: h.memset(cx._ind2b[64:128, 1:2], 1.0), w=["lindb"])
    return cx._ind2b


def build_l1_ffn_prog(dbg=False):
    nc = bass.Bass("TRN2", target_bir_lowering=False)
    dram = {}
    dram["x_in"] = nc.dram_tensor("x_in", [D, T], F32, kind="ExternalInput").ap()
    dram["wq"] = nc.dram_tensor("wq", [2 * D, 512], F32, kind="ExternalInput").ap()
    dram["wk"] = nc.dram_tensor("wk", [2 * D, 512], F32, kind="ExternalInput").ap()
    dram["wv"] = nc.dram_tensor("wv", [2 * D, 520], F32, kind="ExternalInput").ap()
    dram["wo"] = nc.dram_tensor("wo", [2 * 512, 1024], F32, kind="ExternalInput").ap()
    dram["q_scale"] = nc.dram_tensor("q_scale", [2, 64], F32, kind="ExternalInput").ap()
    dram["k_scale"] = nc.dram_tensor("k_scale", [2, 64], F32, kind="ExternalInput").ap()
    dram["ln_g"] = nc.dram_tensor("ln_g", [2, D], F32, kind="ExternalInput").ap()
    dram["ln_b"] = nc.dram_tensor("ln_b", [2, D], F32, kind="ExternalInput").ap()
    dram["w1a"] = nc.dram_tensor("w1a", [2 * 22 * D, 128], F32, kind="ExternalInput").ap()
    dram["w1g"] = nc.dram_tensor("w1g", [2 * 22 * D, 128], F32, kind="ExternalInput").ap()
    dram["b1a"] = nc.dram_tensor("b1a", [2, 22 * 128], F32, kind="ExternalInput").ap()
    dram["b1g"] = nc.dram_tensor("b1g", [2, 22 * 128], F32, kind="ExternalInput").ap()
    dram["w2p"] = nc.dram_tensor("w2p", [2 * 22 * 128, 1024], F32, kind="ExternalInput").ap()
    dram["b2"] = nc.dram_tensor("b2", [2, D], F32, kind="ExternalInput").ap()
    for nm in ["SQ", "TQ", "SK", "TK"]:
        dram["tab_" + nm] = nc.dram_tensor("tab_" + nm, [128, T], F32, kind="ExternalInput").ap()
    dram["mle"] = nc.dram_tensor("mle", [128, 128], BF16, kind="ExternalInput").ap()
    dram["mge"] = nc.dram_tensor("mge", [128, 128], BF16, kind="ExternalInput").ap()
    dram["ident_f"] = nc.dram_tensor("ident_f", [128, 128], F32, kind="ExternalInput").ap()
    dram["x_out"] = nc.dram_tensor("x_out", [D, T], F32, kind="ExternalOutput").ap()
    if dbg:
        dram["d_xi"] = nc.dram_tensor("d_xi", [128, T], F32, kind="ExternalOutput").ap()
        dram["d_q0"] = nc.dram_tensor("d_q0", [128, T], BF16, kind="ExternalOutput").ap()
        dram["d_k0"] = nc.dram_tensor("d_k0", [128, T], BF16, kind="ExternalOutput").ap()
        dram["d_v0"] = nc.dram_tensor("d_v0", [128, 520], BF16, kind="ExternalOutput").ap()
        dram["d_nb"] = nc.dram_tensor("d_nb", [1, 128], F32, kind="ExternalOutput").ap()
        dram["d_o"] = nc.dram_tensor("d_o", [128, T], BF16, kind="ExternalOutput").ap()
        dram["d_x1"] = nc.dram_tensor("d_x1", [128, T], F32, kind="ExternalOutput").ap()
        dram["d_lnf"] = nc.dram_tensor("d_lnf", [128, T], BF16, kind="ExternalOutput").ap()
        dram["d_lnf0"] = nc.dram_tensor("d_lnf0", [128, T], BF16, kind="ExternalOutput").ap()
        dram["d_xi2"] = nc.dram_tensor("d_xi2", [128, T], F32, kind="ExternalOutput").ap()
        dram["d_xiL0"] = nc.dram_tensor("d_xiL0", [128, T], F32, kind="ExternalOutput").ap()

    with ExitStack() as st:
        sems = {e: st.enter_context(nc.semaphore(f"sem_{e}")) for e in ENG}
        pb = PB(nc)
        cx = Ctx(nc, pb, st)
        make_consts(cx)
        blk = st.enter_context(nc.Block())
        emit_l1_ffn(cx, dram)
        pb.realize(blk, sems)
    return nc


def prep_l1_ffn_weights(inp):
    out = {}
    wqkv = np.asarray(inp["L1_ffn_wqkv"])
    out["wq"] = np.ascontiguousarray(wqkv[:, :, 0:512]).reshape(2 * D, 512)
    out["wk"] = np.ascontiguousarray(wqkv[:, :, 512:1024]).reshape(2 * D, 512)
    wv = wqkv[:, :, 1024:1536]
    wva = np.zeros((2, D, 520), np.float32)
    for hh in range(8):
        wva[:, :, 65 * hh:65 * hh + 64] = wv[:, :, 64 * hh:64 * hh + 64]
    out["wv"] = wva.reshape(2 * D, 520)
    out["wo"] = np.ascontiguousarray(np.asarray(inp["L1_ffn_wo"])).reshape(2 * 512, 1024)
    out["q_scale"] = np.asarray(inp["L1_ffn_q_scale"])
    out["k_scale"] = np.asarray(inp["L1_ffn_k_scale"])
    out["ln_g"] = np.asarray(inp["L1_ffn_ln_g"])
    out["ln_b"] = np.asarray(inp["L1_ffn_ln_b"])
    w1 = np.asarray(inp["L1_ffn_w1"])
    b1 = np.asarray(inp["L1_ffn_b1"])
    w2 = np.asarray(inp["L1_ffn_w2"])
    b2 = np.asarray(inp["L1_ffn_b2"])
    HID = 2730
    w1a = np.zeros((2, D, 2816), np.float32)
    w1g = np.zeros((2, D, 2816), np.float32)
    b1a = np.zeros((2, 2816), np.float32)
    b1g = np.zeros((2, 2816), np.float32)
    w2p = np.zeros((2, 2816, D), np.float32)
    w1a[:, :, 0:HID] = w1[:, :, 0:HID]
    w1g[:, :, 0:HID] = w1[:, :, HID:2 * HID]
    b1a[:, 0:HID] = b1[:, 0:HID]
    b1g[:, 0:HID] = b1[:, HID:2 * HID]
    w2p[:, 0:HID, :] = w2
    w1a_dev = np.zeros((2 * 22 * D, 128), np.float32)
    w1g_dev = np.zeros((2 * 22 * D, 128), np.float32)
    w2_dev = np.zeros((2 * 22 * 128, D), np.float32)
    for l in range(2):
        for hc in range(22):
            w1a_dev[(22 * l + hc) * D:(22 * l + hc + 1) * D, :] = w1a[l][:, 128 * hc:128 * (hc + 1)]
            w1g_dev[(22 * l + hc) * D:(22 * l + hc + 1) * D, :] = w1g[l][:, 128 * hc:128 * (hc + 1)]
            w2_dev[(22 * l + hc) * 128:(22 * l + hc + 1) * 128, :] = w2p[l][128 * hc:128 * (hc + 1), :]
    out["w1a"], out["w1g"], out["w2p"] = w1a_dev, w1g_dev, w2_dev
    out["b1a"] = b1a.reshape(2, 2816)
    out["b1g"] = b1g.reshape(2, 2816)
    out["b2"] = b2
    DIM_HEAD, WINDOW = 64, 512
    half = np.arange(0, DIM_HEAD, 2, dtype=np.float32)
    inv = (1.0 / 10000.0 ** (half / DIM_HEAD)).astype(np.float32)
    t = np.arange(T, dtype=np.float32)
    freqs = t[:, None] * inv[None, :]
    freqs = np.concatenate([freqs, freqs], -1)
    base = ((half + 0.4 * DIM_HEAD) / (1.4 * DIM_HEAD)).astype(np.float32)
    power = (t - T // 2) / float(WINDOW // 2)
    scale = base[None, :] ** power[:, None]
    scale = np.concatenate([scale, scale], -1).astype(np.float32)
    cos, sin = np.cos(freqs), np.sin(freqs)
    sgn = np.where(np.arange(DIM_HEAD) < 32, -1.0, 1.0).astype(np.float32)
    SQ = (cos * scale).T
    TQ = (sin * scale * sgn[None, :]).T
    SK = (cos / scale).T
    TK = (sin / scale * sgn[None, :]).T
    for nm, tab in [("SQ", SQ), ("TQ", TQ), ("SK", SK), ("TK", TK)]:
        out["tab_" + nm] = np.ascontiguousarray(np.tile(tab, (2, 1))).astype(np.float32)
    i = np.arange(128)[:, None]
    j = np.arange(128)[None, :]
    out["mle"] = (j <= i).astype(np.float32).astype(BF)
    out["mge"] = (j >= i).astype(np.float32).astype(BF)
    out["ident_f"] = np.eye(128, dtype=np.float32)
    return out




def _decl_l0_attn(nc, p):
    dram = {}
    for nm, sh in [("Wq", [D, 512]), ("Wqc", [D, 512]), ("Wk", [D, 512]), ("Wkc", [D, 512]),
                   ("Wv", [D, 1040]), ("Wvc", [D, 1040])]:
        dram[nm] = nc.dram_tensor(p + nm, sh, F32, kind="ExternalInput").ap()
    for nm, n in [("bq", 512), ("bqc", 512), ("bk", 512), ("bkc", 512), ("bv", 1040), ("bvc", 1040)]:
        dram[nm] = nc.dram_tensor(p + nm, [1, n], F32, kind="ExternalInput").ap()
    return dram


def _decl_l0_ffn(nc, p):
    dram = {}
    dram["router_wT"] = nc.dram_tensor(p + "router_wT", [D, 32], F32, kind="ExternalInput").ap()
    dram["router_bias"] = nc.dram_tensor(p + "router_bias", [1, 32], F32, kind="ExternalInput").ap()
    dram["gate_wT"] = nc.dram_tensor(p + "gate_wT", [32 * D, 128], F32, kind="ExternalInput").ap()
    dram["up_wT"] = nc.dram_tensor(p + "up_wT", [32 * D, 128], F32, kind="ExternalInput").ap()
    dram["down_wT"] = nc.dram_tensor(p + "down_wT", [32 * 128, 1024], F32, kind="ExternalInput").ap()
    dram["sgate_wT"] = nc.dram_tensor(p + "sgate_wT", [D, 512], F32, kind="ExternalInput").ap()
    dram["sup_wT"] = nc.dram_tensor(p + "sup_wT", [D, 512], F32, kind="ExternalInput").ap()
    dram["sdown_wT"] = nc.dram_tensor(p + "sdown_wT", [512, 1024], F32, kind="ExternalInput").ap()
    dram["ident_f"] = nc.dram_tensor(p + "ident_f", [128, 128], F32, kind="ExternalInput").ap()
    return dram


def _decl_l1_attn(nc, p):
    dram = {}
    for nm, sh in [("Wq", [D, D]), ("Wk", [D, D]), ("Wv", [D, 1040]), ("Wo", [D, D])]:
        dram[nm] = nc.dram_tensor(p + nm, sh, F32, kind="ExternalInput").ap()
    dram["cmask"] = nc.dram_tensor(p + "cmask", [4 * 128, 512], BF16, kind="ExternalInput").ap()
    return dram


def _decl_l1_ffn(nc, p):
    dram = {}
    dram["wq"] = nc.dram_tensor(p + "wq", [2 * D, 512], F32, kind="ExternalInput").ap()
    dram["wk"] = nc.dram_tensor(p + "wk", [2 * D, 512], F32, kind="ExternalInput").ap()
    dram["wv"] = nc.dram_tensor(p + "wv", [2 * D, 520], F32, kind="ExternalInput").ap()
    dram["wo"] = nc.dram_tensor(p + "wo", [2 * 512, 1024], F32, kind="ExternalInput").ap()
    dram["q_scale"] = nc.dram_tensor(p + "q_scale", [2, 64], F32, kind="ExternalInput").ap()
    dram["k_scale"] = nc.dram_tensor(p + "k_scale", [2, 64], F32, kind="ExternalInput").ap()
    dram["ln_g"] = nc.dram_tensor(p + "ln_g", [2, D], F32, kind="ExternalInput").ap()
    dram["ln_b"] = nc.dram_tensor(p + "ln_b", [2, D], F32, kind="ExternalInput").ap()
    dram["w1a"] = nc.dram_tensor(p + "w1a", [2 * 22 * D, 128], F32, kind="ExternalInput").ap()
    dram["w1g"] = nc.dram_tensor(p + "w1g", [2 * 22 * D, 128], F32, kind="ExternalInput").ap()
    dram["b1a"] = nc.dram_tensor(p + "b1a", [2, 22 * 128], F32, kind="ExternalInput").ap()
    dram["b1g"] = nc.dram_tensor(p + "b1g", [2, 22 * 128], F32, kind="ExternalInput").ap()
    dram["w2p"] = nc.dram_tensor(p + "w2p", [2 * 22 * 128, 1024], F32, kind="ExternalInput").ap()
    dram["b2"] = nc.dram_tensor(p + "b2", [2, D], F32, kind="ExternalInput").ap()
    for nm in ["SQ", "TQ", "SK", "TK"]:
        dram["tab_" + nm] = nc.dram_tensor(p + "tab_" + nm, [128, T], F32, kind="ExternalInput").ap()
    dram["mle"] = nc.dram_tensor(p + "mle", [128, 128], BF16, kind="ExternalInput").ap()
    dram["mge"] = nc.dram_tensor(p + "mge", [128, 128], BF16, kind="ExternalInput").ap()
    dram["ident_f"] = nc.dram_tensor(p + "ident_f2", [128, 128], F32, kind="ExternalInput").ap()
    return dram


def build_fused_prog():
    nc = bass.Bass("TRN2", target_bir_lowering=False)
    x_in = nc.dram_tensor("x_in", [D, T], F32, kind="ExternalInput").ap()
    x_out = nc.dram_tensor("x_out", [D, T], F32, kind="ExternalOutput").ap()
    xs0 = nc.dram_tensor("xsc0", [D, T], F32, kind="Internal").ap()
    xs1 = nc.dram_tensor("xsc1", [D, T], F32, kind="Internal").ap()
    d0 = _decl_l0_attn(nc, "a_"); d0["x_in"] = x_in; d0["x_out"] = xs0
    d1 = _decl_l0_ffn(nc, "f_"); d1["x_in"] = xs0; d1["x_out"] = xs1
    d2 = _decl_l1_attn(nc, "m_"); d2["x_in"] = xs1; d2["x_out"] = xs0
    d3 = _decl_l1_ffn(nc, "l_"); d3["x_in"] = xs0; d3["x_out"] = x_out

    with ExitStack() as st:
        sems = {e: st.enter_context(nc.semaphore(f"sem_{e}")) for e in ENG}
        pb = PB(nc)
        cx = Ctx(nc, pb, st)
        make_consts(cx)
        blk = st.enter_context(nc.Block())
        for emit_fn, dd in [(emit_l0_attn, d0), (emit_l0_ffn, d1),
                            (emit_l1_attn, d2), (emit_l1_ffn, d3)]:
            with ExitStack() as bs:
                old_st = cx.st
                cx.st = bs
                emit_fn(cx, dd)
                pb.barrier()
                cx.st = old_st
        pb.realize(blk, sems)
    return nc


_PROG_CACHE = {}


def _get_fused():
    if "p" not in _PROG_CACHE:
        _PROG_CACHE["p"] = build_fused_prog()
    return _PROG_CACHE["p"]


def kernel(**inputs):
    inputs = {k: np.asarray(v) for k, v in inputs.items()}
    wmap = {}
    for p, wm in [("a_", prep_l0_attn_weights(inputs)), ("f_", prep_l0_ffn_weights(inputs)),
                  ("m_", prep_l1_attn_weights(inputs)), ("l_", prep_l1_ffn_weights(inputs))]:
        for k, v in wm.items():
            if p == "l_" and k == "ident_f":
                wmap[p + "ident_f2"] = v
            else:
                wmap[p + k] = v
    x = np.asarray(inputs["L0_attn_x"], np.float32)
    nc = _get_fused()
    in_maps = [dict(wmap, x_in=np.ascontiguousarray(x[i].T)) for i in range(8)]
    res = run_bass_kernel_spmd(nc, in_maps, core_ids=list(range(8)))
    cur = [np.asarray(res.results[i]["x_out"]) for i in range(8)]
    return np.ascontiguousarray(np.stack([c.T for c in cur], 0))


# version 13
# speedup vs baseline: 1.1947x; 1.1947x over previous
import numpy as np
import ml_dtypes
from contextlib import ExitStack
import concourse.bass as bass
import concourse.mybir as mybir
from concourse.bass_utils import run_bass_kernel_spmd

F32 = mybir.dt.float32
BF16 = mybir.dt.bfloat16
AF = mybir.ActivationFunctionType
OP = mybir.AluOpType
AX = mybir.AxisListType
BF = ml_dtypes.bfloat16

ENG = ("sync", "scalar", "vector", "tensor", "gpsimd")
T = 2048
D = 1024
KCH = 8
EPS = 1e-5
SP = 2064


class PB:
    def __init__(self, nc):
        self.nc = nc
        self.items = {e: [] for e in ENG}
        self.cnt = {e: 0 for e in ENG}
        self.lw = {}
        self.rd = {}
        self.waited = {e: {o: 0 for o in ENG} for e in ENG}
        self.last_dma = {e: 0 for e in ENG}

    def emit(self, eng, fn, r=(), w=(), dma=False, risky=False):
        need = {}
        def add(tok):
            if tok is None:
                return
            te, tv, asy = tok
            if te == eng and not asy and eng == "tensor":
                return
            if self.waited[eng][te] >= tv:
                return
            if need.get(te, 0) < tv:
                need[te] = tv
        for reg in r:
            add(self.lw.get(reg))
        for reg in w:
            add(self.lw.get(reg))
            for t in self.rd.get(reg, ()):
                add(t)
        if dma and self.last_dma[eng] > 0 and self.waited[eng][eng] < self.last_dma[eng]:
            if need.get(eng, 0) < self.last_dma[eng]:
                need[eng] = self.last_dma[eng]
        inc = 16 if dma else 1
        self.cnt[eng] += inc
        tok = (eng, self.cnt[eng], dma or risky)
        if dma:
            self.last_dma[eng] = self.cnt[eng]
        for reg in w:
            self.lw[reg] = tok
            self.rd[reg] = []
        for reg in r:
            self.rd.setdefault(reg, []).append(tok)
        for te, tv in need.items():
            if self.waited[eng][te] < tv:
                self.waited[eng][te] = tv
        self.items[eng].append((need, fn, inc))

    def barrier(self):
        snap = {e: self.cnt[e] for e in ENG}
        for e in ENG:
            need = {o: snap[o] for o in ENG
                    if snap[o] > 0 and self.waited[e][o] < snap[o]}
            self.cnt[e] += 1
            self.items[e].append((need, lambda h: h.nop(), 1))
            for o, v in need.items():
                self.waited[e][o] = v

    def realize(self, blk, sems):
        self.barrier()
        for e in ENG:
            items = self.items[e]
            def mk(eng_name, items):
                def body(h):
                    for need, fn, inc in items:
                        for te in sorted(need):
                            h.wait_ge(sems[te], need[te])
                        ins = fn(h)
                        ins.then_inc(sems[eng_name], inc)
                return body
            getattr(blk, e)(mk(e, items))


class Ctx:
    def __init__(self, nc, pb, st):
        self.nc = nc
        self.pb = pb
        self.st = st
        self.uid = [0]

    def name(self, base):
        self.uid[0] += 1
        return f"{base}_{self.uid[0]}"

    def sb(self, base, shape, dtype, stack=None, side=None):
        kw = {} if side is None else {"side": side}
        return (stack or self.st).enter_context(
            self.nc.sbuf_tensor(self.name(base), shape, dtype, **kw))

    def ps(self, base, shape, stack=None, dtype=F32):
        return (stack or self.st).enter_context(
            self.nc.psum_tensor(self.name(base), shape, dtype))


def make_consts(cx):
    pb = cx.pb
    cx.ones_b = cx.sb("ones_b", [1, 512], BF16)
    cx.ones_f = cx.sb("ones_f", [1, 512], F32)
    cx.onec_b = cx.sb("onec_b", [128, 1], BF16)
    cx.cc = cx.sb("cc", [128, 8], F32)
    cx.wpool = [cx.sb("wpool", [128, 1040], F32) for i in range(2)]
    cx.wpool_i = 0
    cx.cr = cx.sb("cr", [1, 8], F32)
    cx.cc_map = {}
    cx.cr_map = {}
    pb.emit("vector", lambda h: h.memset(cx.ones_b[:], 1.0), w=["ones_b"])
    pb.emit("vector", lambda h: h.memset(cx.ones_f[:], 1.0), w=["ones_f"])
    pb.emit("vector", lambda h: h.memset(cx.onec_b[:], 1.0), w=["onec_b"])


def const_col(cx, val, rows=128):
    if val not in cx.cc_map:
        i = len(cx.cc_map)
        assert i < 8
        cx.cc_map[val] = i
        cx.pb.emit("vector", lambda h, i=i, val=val: h.memset(cx.cc[:, i:i + 1], float(val)), w=["cc"])
    i = cx.cc_map[val]
    return cx.cc[0:rows, i:i + 1]


def const_row(cx, val):
    if val not in cx.cr_map:
        i = len(cx.cr_map)
        assert i < 8
        cx.cr_map[val] = i
        cx.pb.emit("vector", lambda h, i=i, val=val: h.memset(cx.cr[0:1, i:i + 1], float(val)), w=["cr"])
    i = cx.cr_map[val]
    return cx.cr[0:1, i:i + 1]


class XStream:
    def __init__(self, cx, dram_ap, stack, nbuf=2, tag="xs"):
        self.cx = cx
        self.dram = dram_ap
        self.bufs = [cx.sb(tag, [128, T], F32, stack) for i in range(nbuf)]
        self.tag = tag
        self.nbuf = nbuf
        self.n = 0

    def fetch(self, c):
        i = self.n % self.nbuf
        self.n += 1
        buf = self.bufs[i]
        reg = f"{self.tag}{i}"
        self.cx.pb.emit("sync", lambda h, buf=buf, c=c: h.dma_start(
            out=buf[:], in_=self.dram[128 * c:128 * (c + 1), :]), w=[reg], dma=True)
        return buf, reg


def emit_ln_stats(cx, fetch, rstd_row, a_row, rows_reg, ps_stack, bf16_src, off=0):
    pb = cx.pb
    offf = off if callable(off) else (lambda c: off)
    R, A = rstd_row, a_row
    rr, ra = rows_reg + "_r", rows_reg + "_a"
    srow = cx.sb("lnsrow", [1, T], F32, ps_stack)
    sreg = "lnsrow"
    pb.emit("vector", lambda h: h.memset(R[:], 0.0), w=[rr])
    pb.emit("vector", lambda h: h.memset(A[:], 0.0), w=[ra])
    ps1v = [cx.ps("ps_st1", [1, 512], ps_stack) for i in range(2)]
    ps2v = [cx.ps("ps_st2", [1, 512], ps_stack) for i in range(2)]
    castv = [cx.sb("lncast", [128, 512], BF16, ps_stack) for i in range(2)]
    sqv = [cx.sb("lnsq", [128, 512], BF16, ps_stack) for i in range(2)]
    for c in range(KCH):
        tile, reg = fetch(c)
        for nch in range(4):
            lo = nch * 512
            pp = (4 * c + nch) % 2
            ps1, p1r = ps1v[pp], f"ps_st1{pp}"
            ps2, p2r = ps2v[pp], f"ps_st2{pp}"
            cast, creg_ = castv[pp], f"lncast{pp}"
            sq, sqreg = sqv[pp], f"lnsq{pp}"
            src_ap = tile[:, offf(c) + lo:offf(c) + lo + 512]
            if bf16_src:
                xcb, xreg = src_ap, reg
            else:
                pb.emit("vector", lambda h, a=src_ap, cast=cast: h.tensor_copy(out=cast[:], in_=a),
                        r=[reg], w=[creg_])
                xcb, xreg = cast[:], creg_
            pb.emit("scalar", lambda h, a=xcb, sq=sq: h.activation(out=sq[:], in_=a, func=AF.Square),
                    r=[xreg], w=[sqreg])
            pb.emit("tensor", lambda h, a=xcb, ps1=ps1: h.matmul(ps1[:], cx.onec_b[:], a,
                                                        start=True, stop=True, skip_group_check=True),
                    r=[xreg, "onec_b"], w=[p1r])
            pb.emit("tensor", lambda h, ps2=ps2, sq=sq: h.matmul(ps2[:], cx.onec_b[:], sq[:],
                                                 start=True, stop=True, skip_group_check=True),
                    r=[sqreg, "onec_b"], w=[p2r])
            pb.emit("vector", lambda h, lo=lo, ps1=ps1: h.tensor_tensor(
                out=R[0:1, lo:lo + 512], in0=R[0:1, lo:lo + 512], in1=ps1[:], op=OP.add),
                r=[rr, p1r], w=[rr])
            pb.emit("vector", lambda h, lo=lo, ps2=ps2: h.tensor_tensor(
                out=A[0:1, lo:lo + 512], in0=A[0:1, lo:lo + 512], in1=ps2[:], op=OP.add),
                r=[ra, p2r], w=[ra])
    pb.emit("vector", lambda h: h.tensor_scalar(out=R[:], in0=R[:], scalar1=1.0 / D,
                                                scalar2=None, op0=OP.mult), r=[rr], w=[rr])
    pb.emit("vector", lambda h: h.tensor_tensor(out=srow[:], in0=R[:], in1=R[:], op=OP.mult),
            r=[rr], w=[sreg])
    pb.emit("vector", lambda h: h.tensor_scalar(out=A[:], in0=A[:], scalar1=1.0 / D,
                                                scalar2=None, op0=OP.mult), r=[ra], w=[ra])
    pb.emit("vector", lambda h: h.tensor_tensor(out=A[:], in0=A[:], in1=srow[:], op=OP.subtract),
            r=[ra, sreg], w=[ra])
    epsap = const_row(cx, EPS)
    pb.emit("scalar", lambda h, epsap=epsap: h.activation(out=A[:], in_=A[:], func=AF.Ln, bias=epsap),
            r=[ra, "cr"], w=[ra])
    pb.emit("scalar", lambda h: h.activation(out=A[:], in_=A[:], func=AF.Exp, scale=-0.5),
            r=[ra], w=[ra])
    pb.emit("vector", lambda h: h.tensor_tensor(out=srow[:], in0=R[:], in1=A[:], op=OP.mult),
            r=[rr, ra], w=[sreg])
    pb.emit("vector", lambda h: h.tensor_copy(out=R[:], in_=A[:]), r=[ra], w=[rr])
    pb.emit("vector", lambda h: h.tensor_scalar(out=A[:], in0=srow[:], scalar1=-1.0,
                                                scalar2=None, op0=OP.mult),
            r=[sreg], w=[ra])


def emit_bcast_rows(cx, rstd_row, a_row, rows_reg, rb_sb, ab_sb, ps_stack):
    pb = cx.pb
    bcv = [cx.ps("ps_bcr", [128, 512], ps_stack) for i in range(2)]
    for ri, (row, sb_t, reg) in enumerate([(rstd_row, rb_sb, rows_reg + "rb"),
                                           (a_row, ab_sb, rows_reg + "ab")]):
        for nch in range(4):
            lo = nch * 512
            pp = (4 * ri + nch) % 2
            bc, bcr = bcv[pp], f"ps_bcr{pp}"
            rap = row[0:1, lo:lo + 512]
            pb.emit("tensor", lambda h, rap=rap, bc=bc: h.matmul(bc[:], cx.ones_f[0:1, 0:128], rap,
                                                          start=True, stop=True, skip_group_check=True),
                    r=["ones_f", rows_reg + "_r", rows_reg + "_a"], w=[bcr])
            pb.emit("scalar", lambda h, sb_t=sb_t, lo=lo, bc=bc: h.activation(
                out=sb_t[:, lo:lo + 512], in_=bc[:], func=AF.Copy),
                r=[bcr], w=[reg])


def emit_ln_full(cx, fetch, ps_stack, bf16_src, off=0, f32_out=None):
    rstd = cx.sb("rstd", [1, T], F32, ps_stack)
    arow = cx.sb("arow", [1, T], F32, ps_stack)
    rb_sb = cx.sb("rb_sb", [128, T], BF16, ps_stack)
    ab_sb = cx.sb("ab_sb", [128, T], BF16, ps_stack)
    tag = cx.name("ln")
    emit_ln_stats(cx, fetch, rstd, arow, tag, ps_stack, bf16_src, off)
    emit_bcast_rows(cx, rstd, arow, tag, rb_sb, ab_sb, ps_stack)
    if f32_out is not None:
        rbf, abf = f32_out
        pb = cx.pb
        bc = cx.ps("ps_bcf", [128, 512], ps_stack)
        for row, sb_t, reg in [(rstd, rbf, "rbf"), (arow, abf, "abf")]:
            for nch in range(4):
                lo = nch * 512
                rap = row[0:1, lo:lo + 512]
                pb.emit("tensor", lambda h, rap=rap: h.matmul(
                    bc[:], cx.ones_f[0:1, 0:128], rap, start=True, stop=True, skip_group_check=True),
                    r=["ones_f", tag + "_r", tag + "_a"], w=["ps_bcf"])
                pb.emit("scalar", lambda h, sb_t=sb_t, lo=lo: h.activation(
                    out=sb_t[:, lo:lo + 512], in_=bc[:], func=AF.Copy),
                    r=["ps_bcf"], w=[reg])
    return rb_sb, ab_sb, (tag + "rb", tag + "ab")


def emit_ln_apply(cx, fetch, out_tiles, out_reg, rb_sb, ab_sb, off_out=0, off_src=0,
                  residual_store=None, breg=("rb_sb", "ab_sb")):
    pb = cx.pb
    for c in range(KCH):
        tile, reg = fetch(c)
        oap_t = out_tiles[c]
        for nch in range(4):
            lo = nch * 512
            sap = tile[:, off_src + lo:off_src + lo + 512]
            oap = oap_t[:, off_out + lo:off_out + lo + 512]
            pb.emit("vector", lambda h, sap=sap, oap=oap, lo=lo: h.tensor_tensor(
                out=oap, in0=sap, in1=rb_sb[:, lo:lo + 512], op=OP.mult),
                r=[reg, breg[0]], w=[out_reg(c)])
            pb.emit("vector", lambda h, oap=oap, lo=lo: h.tensor_tensor(
                out=oap, in0=oap, in1=ab_sb[:, lo:lo + 512], op=OP.add),
                r=[out_reg(c), breg[1]], w=[out_reg(c)])
        if residual_store is not None:
            fetchx, store = residual_store
            xtile, xreg = fetchx(c)
            pb.emit("vector", lambda h, xtile=xtile, oap_t=oap_t: h.tensor_tensor(
                out=xtile[:], in0=xtile[:], in1=oap_t[:, off_out:off_out + T], op=OP.add),
                r=[xreg, out_reg(c)], w=[xreg])
            store(c, xtile, xreg)


def emit_load_weight(cx, dram_ap, dest, dest_reg, nch, cols, stage, stage_reg, dcols=None):
    pb = cx.pb
    dcols = dcols or cols
    grp = max(1, min(nch, 1040 // cols))
    c = 0
    while c < nch:
        g = min(grp, nch - c)
        i = cx.wpool_i % 2
        cx.wpool_i += 1
        st_t = cx.wpool[i]
        streg = f"wpool{i}"
        dap = dram_ap[128 * c:128 * (c + g), 0:cols].rearrange("(c p) n -> p c n", p=128)
        sap = st_t.ap()[:, 0:g * cols].rearrange("p (c n) -> p c n", n=cols)
        q = "sync" if i == 0 else "gpsimd"
        pb.emit(q, lambda h, dap=dap, sap=sap: h.dma_start(out=sap, in_=dap),
                w=[streg], dma=True)
        if dcols == cols:
            oap = dest[:, dcols * c:dcols * c + g * cols]
            pb.emit("vector", lambda h, st_t=st_t, oap=oap, g=g, cols=cols: h.tensor_copy(
                out=oap, in_=st_t[:, 0:g * cols]),
                r=[streg], w=[dest_reg])
        else:
            for j in range(g):
                oap = dest[:, dcols * (c + j):dcols * (c + j) + cols]
                pb.emit("vector", lambda h, st_t=st_t, oap=oap, j=j, cols=cols: h.tensor_copy(
                    out=oap, in_=st_t[:, j * cols:(j + 1) * cols]),
                    r=[streg], w=[dest_reg])
        c += g


def emit_load_row(cx, dram_ap, dest_ap, dest_reg, cols, stage, stage_reg):
    pb = cx.pb
    pb.emit("sync", lambda h: h.dma_start(out=stage[0:1, 0:cols], in_=dram_ap),
            w=[stage_reg], dma=True)
    pb.emit("vector", lambda h: h.tensor_copy(out=dest_ap, in_=stage[0:1, 0:cols]),
            r=[stage_reg], w=[dest_reg])


def dbg_dump(cx, dram, name, ap, regs):
    if name in dram:
        cx.pb.emit("sync", lambda h: h.dma_start(out=dram[name], in_=ap), r=list(regs), dma=True)


def emit_l0_attn(cx, dram):
    pb, nc = cx.pb, cx.nc
    C0 = 16.0

    ao = [cx.sb("ao", [128, T], BF16) for c in range(KCH)]

    with ExitStack() as mid:
        v = [cx.sb("vt", [128 if t < 16 else 16, 1040], BF16, mid) for t in range(17)]
        qf = [cx.sb("qf", [128, T], BF16, mid) for m in range(4)]
        kf = [cx.sb("kf", [128, SP], BF16, mid) for m in range(4)]

        with ExitStack() as xnst:
            xn = [cx.sb("xn", [128, SP], BF16, xnst) for c in range(KCH)]
            for c in range(KCH):
                pb.emit("vector", lambda h, c=c: h.memset(xn[c][:, T:SP], 0.0), w=[f"xn{c}"])

            with ExitStack() as ph0:
                xs = XStream(cx, dram["x_in"], ph0, tag="xs0")
                rb_sb, ab_sb, breg = emit_ln_full(cx, xs.fetch, ph0, bf16_src=False)
                emit_ln_apply(cx, xs.fetch, xn, lambda c: f"xn{c}", rb_sb, ab_sb, breg=breg)
                pb.barrier()

            with ExitStack() as ph1:
                _w0 = cx.sb("wstage", [128, 1040], F32, ph1)
                wst = [_w0, _w0]
                sig = cx.sb("sig", [128, SP], BF16, ph1)
                brows = cx.sb("brows", [1, 4 * 512 + 2 * 1040], BF16, ph1)
                boffs = {}
                off = 0
                for nm, cols in [("bq", 512), ("bqc", 512), ("bk", 512), ("bkc", 512),
                                 ("bv", 1040), ("bvc", 1040)]:
                    emit_load_row(cx, dram[nm], brows[0:1, off:off + cols], "brows", cols,
                                  wst[0], "wstage0")
                    boffs[nm] = off
                    off += cols
                ppAv = [cx.ps("ppA", [128, 512], ph1) for i in range(2)]
                ppBv = [cx.ps("ppB", [128, 512], ph1) for i in range(2)]
                sigv = [sig, cx.sb("sig2", [128, SP], BF16, ph1)]

                def qk_proj(wg_nm, bg_nm, wv_nm, bv_nm, outs, outreg, q_cols):
                    with ExitStack() as ws:
                        wg = cx.sb("wg", [128, 8 * 512], BF16, ws)
                        wv_ = cx.sb("wv_", [128, 8 * 512], BF16, ws)
                        emit_load_weight(cx, dram[wg_nm], wg, "wg_r", KCH, 512, wst[0], "wstage0")
                        emit_load_weight(cx, dram[wv_nm], wv_, "wv_r", KCH, 512, wst[1], "wstage1")
                        it = 0
                        for mt in range(4):
                            for lo in range(0, q_cols, 512):
                                nn = min(512, q_cols - lo)
                                pp = it % 2
                                it += 1
                                ppA, par_ = ppAv[pp], f"ppA{pp}"
                                ppB, pbr_ = ppBv[pp], f"ppB{pp}"
                                sg_, sgr_ = sigv[pp], f"sig{pp}"
                                for c in range(KCH):
                                    lh = wg[:, 512 * c + 128 * mt:512 * c + 128 * mt + 128]
                                    rh = xn[c][:, lo:lo + nn]
                                    pb.emit("tensor", lambda h, lh=lh, rh=rh, st=(c == 0), nn=nn, ppA=ppA: h.matmul(
                                        ppA[:, 0:nn], lh, rh, start=st, stop=False, skip_group_check=True),
                                        r=["wg_r", f"xn{c}"], w=[par_])
                                bh = brows[0:1, boffs[bg_nm] + 128 * mt: boffs[bg_nm] + 128 * mt + 128]
                                pb.emit("tensor", lambda h, bh=bh, nn=nn, ppA=ppA: h.matmul(
                                    ppA[:, 0:nn], bh, cx.ones_b[0:1, 0:nn], start=False, stop=True,
                                    skip_group_check=True), r=["brows", "ones_b"], w=[par_])
                                pb.emit("scalar", lambda h, lo=lo, nn=nn, ppA=ppA, sg_=sg_: h.activation(
                                    out=sg_[:, lo:lo + nn], in_=ppA[:, 0:nn], func=AF.Sigmoid),
                                    r=[par_], w=[sgr_])
                                for c in range(KCH):
                                    lh = wv_[:, 512 * c + 128 * mt:512 * c + 128 * mt + 128]
                                    rh = xn[c][:, lo:lo + nn]
                                    pb.emit("tensor", lambda h, lh=lh, rh=rh, st=(c == 0), nn=nn, ppB=ppB: h.matmul(
                                        ppB[:, 0:nn], lh, rh, start=st, stop=False, skip_group_check=True),
                                        r=["wv_r", f"xn{c}"], w=[pbr_])
                                bh = brows[0:1, boffs[bv_nm] + 128 * mt: boffs[bv_nm] + 128 * mt + 128]
                                pb.emit("tensor", lambda h, bh=bh, nn=nn, ppB=ppB: h.matmul(
                                    ppB[:, 0:nn], bh, cx.ones_b[0:1, 0:nn], start=False, stop=True,
                                    skip_group_check=True), r=["brows", "ones_b"], w=[pbr_])
                                oap = outs[mt][:, lo:lo + nn]
                                pb.emit("vector", lambda h, oap=oap, lo=lo, nn=nn, mt=mt, ppB=ppB, sg_=sg_: h.tensor_tensor(
                                    out=oap, in0=ppB[:, 0:nn], in1=sg_[:, lo:lo + nn], op=OP.mult),
                                    r=[pbr_, sgr_], w=[f"{outreg}{mt}"])
                        pb.barrier()

                qk_proj("Wqc", "bqc", "Wq", "bq", qf, "qf", T)
                qk_proj("Wkc", "bkc", "Wk", "bk", kf, "kf", SP)

                with ExitStack() as ws:
                    wvc = cx.sb("wvc", [128, 8 * 1040], BF16, ws)
                    wv = cx.sb("wv", [128, 8 * 1040], BF16, ws)
                    emit_load_weight(cx, dram["Wvc"], wvc, "wvc_r", KCH, 1040, wst[0], "wstage0")
                    emit_load_weight(cx, dram["Wv"], wv, "wv2_r", KCH, 1040, wst[1], "wstage1")
                    it2 = 0
                    for tt in range(17):
                        rows = 128 if tt < 16 else 16
                        for lo in range(0, 1040, 512):
                            nn = min(512, 1040 - lo)
                            pp = it2 % 2
                            it2 += 1
                            ppA, par_ = ppAv[pp], f"ppA{pp}"
                            ppB, pbr_ = ppBv[pp], f"ppB{pp}"
                            sg_, sgr_ = sigv[pp], f"sig{pp}"
                            for c in range(KCH):
                                lh = xn[c][:, 128 * tt:128 * tt + rows]
                                rh = wvc[:, 1040 * c + lo:1040 * c + lo + nn]
                                pb.emit("tensor", lambda h, lh=lh, rh=rh, st=(c == 0), rows=rows, nn=nn, ppA=ppA: h.matmul(
                                    ppA[0:rows, 0:nn], lh, rh, start=st, stop=False, skip_group_check=True),
                                    r=["wvc_r", f"xn{c}"], w=[par_])
                            bh = brows[0:1, boffs["bvc"] + lo:boffs["bvc"] + lo + nn]
                            pb.emit("tensor", lambda h, bh=bh, rows=rows, nn=nn, ppA=ppA: h.matmul(
                                ppA[0:rows, 0:nn], cx.ones_b[0:1, 0:rows], bh, start=False, stop=True,
                                skip_group_check=True), r=["brows", "ones_b"], w=[par_])
                            pb.emit("scalar", lambda h, rows=rows, lo=lo, nn=nn, ppA=ppA, sg_=sg_: h.activation(
                                out=sg_[0:rows, lo:lo + nn], in_=ppA[0:rows, 0:nn], func=AF.Sigmoid),
                                r=[par_], w=[sgr_])
                            for c in range(KCH):
                                lh = xn[c][:, 128 * tt:128 * tt + rows]
                                rh = wv[:, 1040 * c + lo:1040 * c + lo + nn]
                                pb.emit("tensor", lambda h, lh=lh, rh=rh, st=(c == 0), rows=rows, nn=nn, ppB=ppB: h.matmul(
                                    ppB[0:rows, 0:nn], lh, rh, start=st, stop=False, skip_group_check=True),
                                    r=["wv2_r", f"xn{c}"], w=[pbr_])
                            bh = brows[0:1, boffs["bv"] + lo:boffs["bv"] + lo + nn]
                            pb.emit("tensor", lambda h, bh=bh, rows=rows, nn=nn, ppB=ppB: h.matmul(
                                ppB[0:rows, 0:nn], cx.ones_b[0:1, 0:rows], bh, start=False, stop=True,
                                skip_group_check=True), r=["brows", "ones_b"], w=[pbr_])
                            pb.emit("vector", lambda h, tt=tt, rows=rows, lo=lo, nn=nn, ppB=ppB, sg_=sg_: h.tensor_tensor(
                                out=v[tt][0:rows, lo:lo + nn], in0=ppB[0:rows, 0:nn],
                                in1=sg_[0:rows, lo:lo + nn], op=OP.mult),
                                r=[pbr_, sgr_], w=[f"v{tt}"])
                        def ones_fn(h, tt=tt, rows=rows):
                            vv = v[tt].ap().rearrange("p (g w) -> p g w", w=65)[0:rows, :, 64:65]
                            return h.memset(vv, 1.0)
                        pb.emit("vector", ones_fn, r=[f"v{tt}"], w=[f"v{tt}"])
                    pb.barrier()
                pb.barrier()
        pb.barrier()

        with ExitStack() as ph2:
            ps_s = [cx.ps("ps_s", [128, 1024], ph2) for i in range(2)]
            ps_ctx2 = [cx.ps("ps_ctx", [65, 512], ph2) for i in range(2)]
            ps_bc = cx.ps("ps_cb", [64, 512], ph2)
            probs = [cx.sb("probs", [128, 1024], BF16, ph2) for i in range(2)]
            ctxb2 = [cx.sb("ctxb", [64, 512], BF16, ph2) for i in range(2)]
            rrow2 = [cx.sb("rrow", [1, 512], F32, ph2) for i in range(2)]

            rounds = []
            for r_ in range(9):
                pair = []
                for kt in (2 * r_, 2 * r_ + 1):
                    kn = min(128, SP - 128 * kt)
                    if kn > 0:
                        pair.append((kt, kn))
                rounds.append(pair)
            n_pv = sum(len(p) for p in rounds)

            for h_i in range(16):
                mt, g = h_i // 4, h_i % 4
                qrows = slice(32 * g, 32 * g + 16)
                tp = (32 * g, 0)
                for qc in range(4):
                    par = (4 * h_i + qc) % 2
                    ps_ctx, creg = ps_ctx2[par], f"ps_ctx{par}"
                    ctxb, cbreg = ctxb2[par], f"ctxb{par}"
                    rrow, rreg = rrow2[par], f"rrow{par}"
                    qlo = 512 * qc
                    pv_i = 0
                    def emit_qk_round(r_):
                        pair = rounds[r_]
                        pbuf = r_ % 2
                        sreg = f"ps_s{pbuf}"
                        for gi, (kt, kn) in enumerate(pair):
                            lh = kf[mt][qrows, 128 * kt:128 * kt + kn]
                            rh = qf[mt][qrows, qlo:qlo + 512]
                            pb.emit("tensor", lambda h, lh=lh, rh=rh, gi=gi, kn=kn, pbuf=pbuf, tp=tp: h.matmul(
                                ps_s[pbuf][0:kn, 512 * gi:512 * gi + 512], lh, rh,
                                start=True, stop=True, skip_group_check=True, tile_position=tp),
                                r=[f"kf{mt}", f"qf{mt}"], w=[sreg])
                    emit_qk_round(0)
                    for r_, pair in enumerate(rounds):
                        pbuf = r_ % 2
                        sreg, preg = f"ps_s{pbuf}", f"probs{pbuf}"
                        if r_ + 1 < len(rounds):
                            emit_qk_round(r_ + 1)
                        wid = 512 * len(pair)
                        kmax = pair[0][1]
                        negc = const_col(cx, -C0, rows=kmax)
                        pb.emit("scalar", lambda h, pbuf=pbuf, wid=wid, kmax=kmax, negc=negc: h.activation(
                            out=probs[pbuf][0:kmax, 0:wid], in_=ps_s[pbuf][0:kmax, 0:wid],
                            func=AF.Exp, bias=negc, scale=0.125),
                            r=[sreg, "cc"], w=[preg])
                        for gi, (kt, kn) in enumerate(pair):
                            pv_i += 1
                            lh = v[kt][0:kn, 65 * h_i:65 * h_i + 65]
                            rh = probs[pbuf][0:kn, 512 * gi:512 * gi + 512]
                            pb.emit("tensor", lambda h, lh=lh, rh=rh, st=(pv_i == 1), sp=(pv_i == n_pv), ps_ctx=ps_ctx: h.matmul(
                                ps_ctx[:], lh, rh, start=st, stop=sp, skip_group_check=True),
                                r=[f"v{kt}", preg], w=[creg])
                    pb.emit("vector", lambda h, rrow=rrow, ps_ctx=ps_ctx: h.tensor_scalar(
                        out=rrow[:], in0=ps_ctx[64:65, 0:512], scalar1=1e-30, scalar2=None, op0=OP.max),
                        r=[creg], w=[rreg], risky=True)
                    pb.emit("vector", lambda h, rrow=rrow: h.reciprocal(out=rrow[:], in_=rrow[:]),
                            r=[rreg], w=[rreg], risky=True)
                    pb.emit("tensor", lambda h, rrow=rrow: h.matmul(ps_bc[:], cx.ones_f[0:1, 0:64], rrow[:],
                                                         start=True, stop=True, skip_group_check=True),
                            r=["ones_f", rreg], w=["ps_cb"])
                    pb.emit("scalar", lambda h, ctxb=ctxb, ps_ctx=ps_ctx: h.activation(out=ctxb[:], in_=ps_ctx[0:64, 0:512],
                                                             func=AF.Copy),
                            r=[creg], w=[cbreg])
                    dt_, dr_ = h_i // 2, 64 * (h_i % 2)
                    pb.emit("vector", lambda h, dt_=dt_, dr_=dr_, qlo=qlo, ctxb=ctxb: h.tensor_tensor(
                        out=ao[dt_][dr_:dr_ + 64, qlo:qlo + 512], in0=ctxb[:], in1=ps_bc[:], op=OP.mult),
                        r=[cbreg, "ps_cb"], w=[f"ao{dt_}"])
            pb.barrier()
    pb.barrier()

    with ExitStack() as ph3:
        rb_sb, ab_sb, breg = emit_ln_full(cx, lambda c: (ao[c], f"ao{c}"), ph3, bf16_src=True)
        u = [cx.sb("u", [128, T], BF16, ph3) for c in range(KCH)]
        xs3 = XStream(cx, dram["x_in"], ph3, tag="xs3")
        def store(c, xtile, xreg):
            pb.emit("gpsimd", lambda h, c=c, xtile=xtile: h.dma_start(
                out=dram["x_out"][128 * c:128 * (c + 1), :], in_=xtile[:]),
                r=[xreg], dma=True)
        emit_ln_apply(cx, lambda c: (ao[c], f"ao{c}"), u, lambda c: f"u{c}", rb_sb, ab_sb,
                      residual_store=(xs3.fetch, store), breg=breg)
        pb.barrier()


def build_l0_attn_prog():
    nc = bass.Bass("TRN2", target_bir_lowering=False)
    dram = {}
    dram["x_in"] = nc.dram_tensor("x_in", [D, T], F32, kind="ExternalInput").ap()
    for nm, sh in [("Wq", [D, 512]), ("Wqc", [D, 512]), ("Wk", [D, 512]), ("Wkc", [D, 512]),
                   ("Wv", [D, 1040]), ("Wvc", [D, 1040])]:
        dram[nm] = nc.dram_tensor(nm, sh, F32, kind="ExternalInput").ap()
    for nm, n in [("bq", 512), ("bqc", 512), ("bk", 512), ("bkc", 512), ("bv", 1040), ("bvc", 1040)]:
        dram[nm] = nc.dram_tensor(nm, [1, n], F32, kind="ExternalInput").ap()
    dram["x_out"] = nc.dram_tensor("x_out", [D, T], F32, kind="ExternalOutput").ap()

    with ExitStack() as st:
        sems = {e: st.enter_context(nc.semaphore(f"sem_{e}")) for e in ENG}
        pb = PB(nc)
        cx = Ctx(nc, pb, st)
        make_consts(cx)
        blk = st.enter_context(nc.Block())
        emit_l0_attn(cx, dram)
        pb.realize(blk, sems)
    return nc


def prep_l0_attn_weights(inp):
    H = 16
    out = {}
    def pad32(W, b):
        Wp = np.zeros((D, 512), np.float32)
        bp = np.zeros((1, 512), np.float32)
        for h in range(H):
            Wp[:, 32 * h:32 * h + 16] = W[:, 16 * h:16 * h + 16]
            bp[0, 32 * h:32 * h + 16] = b[16 * h:16 * h + 16]
        return Wp, bp
    def aug65(W, b):
        Wp = np.zeros((D, 1040), np.float32)
        bp = np.zeros((1, 1040), np.float32)
        for h in range(H):
            Wp[:, 65 * h:65 * h + 64] = W[:, 64 * h:64 * h + 64]
            bp[0, 65 * h:65 * h + 64] = b[64 * h:64 * h + 64]
        return Wp, bp
    out["Wq"], out["bq"] = pad32(np.asarray(inp["L0_attn_Wq"]), np.asarray(inp["L0_attn_bq"]))
    out["Wqc"], out["bqc"] = pad32(np.asarray(inp["L0_attn_Wqc"]), np.asarray(inp["L0_attn_bqc"]))
    out["Wk"], out["bk"] = pad32(np.asarray(inp["L0_attn_Wk"]), np.asarray(inp["L0_attn_bk"]))
    out["Wkc"], out["bkc"] = pad32(np.asarray(inp["L0_attn_Wkc"]), np.asarray(inp["L0_attn_bkc"]))
    out["Wv"], out["bv"] = aug65(np.asarray(inp["L0_attn_Wv"]), np.asarray(inp["L0_attn_bv"]))
    out["Wvc"], out["bvc"] = aug65(np.asarray(inp["L0_attn_Wvc"]), np.asarray(inp["L0_attn_bvc"]))
    return out


def emit_l0_ffn(cx, dram):
    pb, nc = cx.pb, cx.nc

    moe = [cx.sb("moe", [128, T], F32) for c in range(KCH)]

    with ExitStack() as mid:
        xn = [cx.sb("xnf", [128, T], BF16, mid) for c in range(KCH)]
        wtm = [cx.sb("wtm", [128, 32], F32, mid) for tt in range(16)]
        idf = cx.sb("idf", [128, 128], F32, mid)
        pb.emit("sync", lambda h: h.dma_start(out=idf[:], in_=dram["ident_f"]), w=["idf"], dma=True)
        rbf = cx.sb("rbf", [128, T], F32, mid)
        abf = cx.sb("abf", [128, T], F32, mid)

        with ExitStack() as ph0:
            xs = XStream(cx, dram["x_in"], ph0, tag="xf0")
            rb_sb, ab_sb, breg = emit_ln_full(cx, xs.fetch, ph0, bf16_src=False,
                                              f32_out=(rbf, abf))
            emit_ln_apply(cx, xs.fetch, xn, lambda c: f"xnf{c}", rb_sb, ab_sb, breg=breg)
            dbg_dump(cx, dram, "d_xn", xn[0][:, 0:T], ["xnf0"])
            pb.barrier()

        with ExitStack() as ph1:
            rwtf = cx.sb("rwtf", [128, 8 * 32], F32, ph1)
            for c in range(KCH):
                pb.emit("sync", lambda h, c=c: h.dma_start(
                    out=rwtf[:, 32 * c:32 * c + 32], in_=dram["router_wT"][128 * c:128 * (c + 1), :]),
                    w=["rwtf"], dma=True)
            rbias = cx.sb("rbias", [1, 32], F32, ph1)
            pb.emit("sync", lambda h: h.dma_start(out=rbias[:], in_=dram["router_bias"]),
                    w=["rbias"], dma=True)
            lg = cx.sb("lgits", [32, T], F32, ph1)
            ps_l4 = [cx.ps("ps_l", [32, 512], ph1) for i in range(4)]
            xs1 = XStream(cx, dram["x_in"], ph1, tag="xr1")
            xn32 = cx.sb("xn32", [128, T], F32, ph1)
            for c in range(KCH):
                tile, reg = xs1.fetch(c)
                pb.emit("vector", lambda h, tile=tile: h.tensor_tensor(
                    out=xn32[:], in0=tile[:], in1=rbf[:], op=OP.mult),
                    r=[reg, "rbf"], w=["xn32"])
                pb.emit("vector", lambda h: h.tensor_tensor(
                    out=xn32[:], in0=xn32[:], in1=abf[:], op=OP.add),
                    r=["xn32", "abf"], w=["xn32"])
                for nch in range(4):
                    lo = nch * 512
                    lh = rwtf[:, 32 * c:32 * c + 32]
                    rh = xn32[:, lo:lo + 512]
                    pb.emit("tensor", lambda h, lh=lh, rh=rh, st=(c == 0), nch=nch: h.matmul(
                        ps_l4[nch][:], lh, rh, start=st, stop=False, skip_group_check=True),
                        r=["rwtf", "xn32"], w=[f"ps_l{nch}"])
            for nch in range(4):
                lo = nch * 512
                pb.emit("tensor", lambda h, nch=nch: h.matmul(
                    ps_l4[nch][:], rbias[0:1, 0:32], cx.ones_f[0:1, 0:512],
                    start=False, stop=True, skip_group_check=True),
                    r=["rbias", "ones_f"], w=[f"ps_l{nch}"])
                pb.emit("vector", lambda h, lo=lo, nch=nch: h.tensor_copy(
                    out=lg[:, lo:lo + 512], in_=ps_l4[nch][:]),
                    r=[f"ps_l{nch}"], w=["lg"])
            ps_t = cx.ps("ps_t", [128, 512], ph1)
            ptile = cx.sb("ptile", [128, 32], F32, ph1)
            sc1 = cx.sb("sc1", [128, 32], F32, ph1)
            sc1x = cx.sb("sc1x", [128, 32], F32, ph1)
            sc2 = cx.sb("sc2", [128, 8], F32, ph1)
            for tt in range(16):
                pb.emit("tensor", lambda h, tt=tt: h.transpose(
                    ps_t[:, 0:32], lg[:, 128 * tt:128 * tt + 128], idf[0:32, 0:32]),
                    r=["lg", "idf"], w=["ps_tt"])
                pb.emit("vector", lambda h: h.tensor_copy(out=ptile[:], in_=ps_t[:, 0:32]),
                        r=["ps_tt"], w=["ptile"])
                pb.emit("vector", lambda h: h.reduce_max(out=sc2[:, 1:2], in_=ptile[:], axis=AX.X,
                                                         negate=True),
                        r=["ptile"], w=["sc2"], risky=True)
                pb.emit("scalar", lambda h: h.activation(out=ptile[:], in_=ptile[:], func=AF.Exp,
                                                         bias=sc2[:, 1:2], accum_out=sc2[:, 2:3]),
                        r=["ptile", "sc2"], w=["ptile", "sc2"], risky=True)
                pb.emit("vector", lambda h: h.reciprocal(out=sc2[:, 3:4], in_=sc2[:, 2:3]),
                        r=["sc2"], w=["sc2"], risky=True)
                pb.emit("vector", lambda h: h.tensor_scalar(out=ptile[:], in0=ptile[:],
                                                            scalar1=sc2[:, 3:4], scalar2=None, op0=OP.mult),
                        r=["ptile", "sc2"], w=["ptile"])
                if tt == 0:
                    dbg_dump(cx, dram, "d_pt", ptile[:], ["ptile"])
                pb.emit("vector", lambda h: h.tensor_copy(out=sc1[:], in_=ptile[:]), r=["ptile"], w=["sc1"])
                for it in range(4):
                    pb.emit("vector", lambda h, it=it: h.reduce_max(out=sc2[:, 4:5], in_=sc1[:], axis=AX.X),
                            r=["sc1"], w=["sc2"], risky=True)
                    if it < 3:
                        pb.emit("vector", lambda h: h.tensor_scalar(out=sc1x[:], in0=sc1[:],
                                                                    scalar1=sc2[:, 4:5], scalar2=1e30,
                                                                    op0=OP.is_ge, op1=OP.mult),
                                r=["sc1", "sc2"], w=["sc1x"])
                        pb.emit("vector", lambda h: h.tensor_tensor(out=sc1[:], in0=sc1[:], in1=sc1x[:],
                                                                    op=OP.subtract),
                                r=["sc1", "sc1x"], w=["sc1"])
                pb.emit("vector", lambda h: h.tensor_scalar(out=sc1[:], in0=ptile[:],
                                                            scalar1=sc2[:, 4:5], scalar2=None, op0=OP.is_ge),
                        r=["ptile", "sc2"], w=["sc1"])
                pb.emit("vector", lambda h: h.tensor_tensor(out=sc1[:], in0=sc1[:], in1=ptile[:], op=OP.mult),
                        r=["sc1", "ptile"], w=["sc1"])
                pb.emit("vector", lambda h: h.reduce_sum(out=sc2[:, 5:6], in_=sc1[:], axis=AX.X),
                        r=["sc1"], w=["sc2"], risky=True)
                pb.emit("vector", lambda h: h.tensor_scalar(out=sc2[:, 6:7], in0=sc2[:, 5:6],
                                                            scalar1=1e-12, scalar2=None, op0=OP.max),
                        r=["sc2"], w=["sc2"])
                pb.emit("vector", lambda h: h.reciprocal(out=sc2[:, 7:8], in_=sc2[:, 6:7]),
                        r=["sc2"], w=["sc2"], risky=True)
                pb.emit("vector", lambda h, tt=tt: h.tensor_scalar(out=wtm[tt][:], in0=sc1[:],
                                                                   scalar1=sc2[:, 7:8], scalar2=None, op0=OP.mult),
                        r=["sc1", "sc2"], w=[f"wtm{tt}"])
            dbg_dump(cx, dram, "d_lg", lg[:], ["lg"])
            dbg_dump(cx, dram, "d_wtm", wtm[0][:], ["wtm0"])
            pb.barrier()

        with ExitStack() as ph2:
            wfm = cx.sb("wfm", [32, T], F32, ph2)
            with ExitStack() as tps:
                ps_tb = cx.ps("ps_tb", [32, 512], tps)
                for tt in range(16):
                    pb.emit("tensor", lambda h, tt=tt: h.transpose(
                        ps_tb[:, 0:128], wtm[tt][:], idf[:]),
                        r=[f"wtm{tt}", "idf"], w=["ps_tb"])
                    pb.emit("vector", lambda h, tt=tt: h.tensor_copy(out=wfm[:, 128 * tt:128 * tt + 128],
                                                                     in_=ps_tb[0:32, 0:128]),
                            r=["ps_tb"], w=["wfm"])
                pb.barrier()
            dbg_dump(cx, dram, "d_wfm", wfm[:], ["wfm"])
            wge = [cx.sb("wge", [128, 1024], BF16, ph2) for e in range(8)]
            wue = [cx.sb("wue", [128, 1024], BF16, ph2) for e in range(8)]
            wde = [cx.sb("wde", [128, 1024], BF16, ph2) for e in range(8)]
            wstage = [None, None]
            ht = [cx.sb("ht", [128, 512], BF16, ph2) for e in range(8)]
            gsb2v = [cx.sb("gsb", [128, 512], BF16, ph2) for i in range(2)]
            wbc = [cx.sb("wbc", [128, 512], BF16, ph2) for e in range(8)]
            wrowv = [cx.sb("wrow", [1, 512], F32, ph2) for i in range(2)]
            ps_g2 = [cx.ps("ps_g", [128, 512], ph2) for i in range(2)]
            ps_u2 = [cx.ps("ps_u", [128, 512], ph2) for i in range(2)]
            ps_wv = [cx.ps("ps_w", [128, 512], ph2)] * 2
            ps_rv = [cx.ps("ps_r", [1, 512], ph2)] * 2
            ps_d = [cx.ps("ps_d", [128, 512], ph2) for i in range(2)]

            for grp in range(4):
                for e in range(8):
                    ge = grp * 8 + e
                    emit_load_weight(cx, dram["gate_wT"][1024 * ge:1024 * (ge + 1), :], wge[e], f"wge{e}",
                                     KCH, 128, wstage[0], "wfstage0")
                    emit_load_weight(cx, dram["up_wT"][1024 * ge:1024 * (ge + 1), :], wue[e], f"wue{e}",
                                     KCH, 128, wstage[1], "wfstage1")
                    emit_load_weight(cx, dram["down_wT"][128 * ge:128 * (ge + 1), :], wde[e], f"wde{e}",
                                     1, 1024, wstage[0], "wfstage0")
                for nch in range(4):
                    lo = nch * 512
                    for e in range(8):
                        ge = grp * 8 + e
                        pe_ = e % 2
                        ps_re, rer = ps_rv[pe_], "ps_r"
                        ps_we, wer = ps_wv[pe_], "ps_w"
                        wre, wrr = wrowv[pe_], f"wrow{pe_}"
                        pb.emit("tensor", lambda h, ge=ge, lo=lo, ps_re=ps_re: h.matmul(
                            ps_re[:], idf[0:32, ge:ge + 1], wfm[:, lo:lo + 512],
                            start=True, stop=True, skip_group_check=True),
                            r=["idf", "wfm"], w=[rer])
                        pb.emit("vector", lambda h, wre=wre, ps_re=ps_re: h.tensor_copy(out=wre[:], in_=ps_re[:]),
                                r=[rer], w=[wrr])
                        pb.emit("tensor", lambda h, wre=wre, ps_we=ps_we: h.matmul(
                            ps_we[:], cx.ones_f[0:1, 0:128], wre[:],
                            start=True, stop=True, skip_group_check=True),
                            r=["ones_f", wrr], w=[wer])
                        pb.emit("vector", lambda h, e=e, ps_we=ps_we: h.tensor_copy(out=wbc[e][:], in_=ps_we[:]),
                                r=[wer], w=[f"wbc{e}"])
                    for e in range(8):
                        ge = grp * 8 + e
                        pe_ = e % 2
                        ps_g, greg = ps_g2[pe_], f"ps_g{pe_}"
                        ps_u, ureg = ps_u2[pe_], f"ps_u{pe_}" 
                        for c in range(KCH):
                            lh = wge[e][:, 128 * c:128 * c + 128]
                            rh = xn[c][:, lo:lo + 512]
                            pb.emit("tensor", lambda h, lh=lh, rh=rh, st=(c == 0), sp=(c == KCH - 1), ps_g=ps_g: h.matmul(
                                ps_g[:], lh, rh, start=st, stop=sp, skip_group_check=True),
                                r=[f"wge{e}", f"xnf{c}"], w=[greg])
                        gsb, gsreg = gsb2v[pe_], f"gsb{pe_}"
                        pb.emit("scalar", lambda h, ps_g=ps_g, gsb=gsb: h.activation(out=gsb[:], in_=ps_g[:], func=AF.Silu),
                                r=[greg], w=[gsreg])
                        for c in range(KCH):
                            lh = wue[e][:, 128 * c:128 * c + 128]
                            rh = xn[c][:, lo:lo + 512]
                            pb.emit("tensor", lambda h, lh=lh, rh=rh, st=(c == 0), sp=(c == KCH - 1), ps_u=ps_u: h.matmul(
                                ps_u[:], lh, rh, start=st, stop=sp, skip_group_check=True),
                                r=[f"wue{e}", f"xnf{c}"], w=[ureg])
                        pb.emit("vector", lambda h, e=e, ps_u=ps_u, gsb=gsb: h.tensor_tensor(out=ht[e][:], in0=ps_u[:],
                                                                         in1=gsb[:], op=OP.mult),
                                r=[ureg, gsreg], w=[f"ht{e}"])
                        pb.emit("vector", lambda h, e=e: h.tensor_tensor(out=ht[e][:], in0=ht[e][:],
                                                                         in1=wbc[e][:], op=OP.mult),
                                r=[f"ht{e}", f"wbc{e}"], w=[f"ht{e}"])
                        if grp == 0 and e == 0 and nch == 0:
                            dbg_dump(cx, dram, "d_ht", ht[0][:], ["ht0"])
                            dbg_dump(cx, dram, "d_gsb", gsb[:], [gsreg])
                    for mh in range(4):
                        for e in range(8):
                            for mt in range(2):
                                gmt = 2 * mh + mt
                                lh = wde[e][:, 128 * gmt:128 * gmt + 128]
                                pb.emit("tensor", lambda h, lh=lh, e=e, mt=mt: h.matmul(
                                    ps_d[mt][:], lh, ht[e][:], start=(e == 0), stop=(e == 7),
                                    skip_group_check=True),
                                    r=[f"wde{e}", f"ht{e}"], w=[f"ps_d{mt}"])
                        for mt in range(2):
                            gmt = 2 * mh + mt
                            if grp == 0:
                                pb.emit("vector", lambda h, gmt=gmt, mt=mt, lo=lo: h.tensor_copy(
                                    out=moe[gmt][:, lo:lo + 512], in_=ps_d[mt][:]),
                                    r=[f"ps_d{mt}"], w=[f"moe{gmt}"])
                            else:
                                pb.emit("vector", lambda h, gmt=gmt, mt=mt, lo=lo: h.tensor_tensor(
                                    out=moe[gmt][:, lo:lo + 512], in0=moe[gmt][:, lo:lo + 512],
                                    in1=ps_d[mt][:], op=OP.add),
                                    r=[f"ps_d{mt}", f"moe{gmt}"], w=[f"moe{gmt}"])
            dbg_dump(cx, dram, "d_moe0", moe[0][:], ["moe0"])
            pb.barrier()

        with ExitStack() as ph3:
            wsg = cx.sb("wsg", [128, 8 * 512], BF16, ph3)
            wsu = cx.sb("wsu", [128, 8 * 512], BF16, ph3)
            wsd = cx.sb("wsd", [128, 4 * 1024], BF16, ph3)
            wstage_d = [None, None]
            emit_load_weight(cx, dram["sgate_wT"], wsg, "wsg", KCH, 512, wstage_d[0], "wdstage0")
            emit_load_weight(cx, dram["sup_wT"], wsu, "wsu", KCH, 512, wstage_d[1], "wdstage1")
            emit_load_weight(cx, dram["sdown_wT"], wsd, "wsd", 4, 1024, wstage_d[0], "wdstage0")
            sdin = [cx.sb("sdin", [128, T], BF16, ph3) for m in range(4)]
            ps_dgv = [cx.ps("ps_sg", [128, 512], ph3) for i in range(2)]
            ps_duv = [cx.ps("ps_su", [128, 512], ph3) for i in range(2)]
            gsb_dv = [cx.sb("gsb2", [128, 512], BF16, ph3) for i in range(2)]
            itd = 0
            for mt in range(4):
                for nch in range(4):
                    lo = nch * 512
                    pp = itd % 2
                    itd += 1
                    ps_dg, dgr = ps_dgv[pp], f"ps_sg{pp}"
                    ps_du, dur = ps_duv[pp], f"ps_su{pp}"
                    gsb_d, gdr = gsb_dv[pp], f"gsb2{pp}"
                    for c in range(KCH):
                        lh = wsg[:, 512 * c + 128 * mt:512 * c + 128 * mt + 128]
                        rh = xn[c][:, lo:lo + 512]
                        pb.emit("tensor", lambda h, lh=lh, rh=rh, st=(c == 0), sp=(c == KCH - 1), ps_dg=ps_dg: h.matmul(
                            ps_dg[:], lh, rh, start=st, stop=sp, skip_group_check=True),
                            r=["wsg", f"xnf{c}"], w=[dgr])
                    pb.emit("scalar", lambda h, ps_dg=ps_dg, gsb_d=gsb_d: h.activation(out=gsb_d[:], in_=ps_dg[:], func=AF.Silu),
                            r=[dgr], w=[gdr])
                    for c in range(KCH):
                        lh = wsu[:, 512 * c + 128 * mt:512 * c + 128 * mt + 128]
                        rh = xn[c][:, lo:lo + 512]
                        pb.emit("tensor", lambda h, lh=lh, rh=rh, st=(c == 0), sp=(c == KCH - 1), ps_du=ps_du: h.matmul(
                            ps_du[:], lh, rh, start=st, stop=sp, skip_group_check=True),
                            r=["wsu", f"xnf{c}"], w=[dur])
                    pb.emit("vector", lambda h, mt=mt, lo=lo, ps_du=ps_du, gsb_d=gsb_d: h.tensor_tensor(
                        out=sdin[mt][:, lo:lo + 512], in0=ps_du[:], in1=gsb_d[:], op=OP.mult),
                        r=[dur, gdr], w=[f"sdin{mt}"])
            for mt in range(KCH):
                for nch in range(4):
                    lo = nch * 512
                    pp = itd % 2
                    itd += 1
                    ps_dg, dgr = ps_dgv[pp], f"ps_sg{pp}"
                    for c4 in range(4):
                        lh = wsd[:, 1024 * c4 + 128 * mt:1024 * c4 + 128 * mt + 128]
                        rh = sdin[c4][:, lo:lo + 512]
                        pb.emit("tensor", lambda h, lh=lh, rh=rh, st=(c4 == 0), sp=(c4 == 3), ps_dg=ps_dg: h.matmul(
                            ps_dg[:], lh, rh, start=st, stop=sp, skip_group_check=True),
                            r=["wsd", f"sdin{c4}"], w=[dgr])
                    pb.emit("vector", lambda h, mt=mt, lo=lo, ps_dg=ps_dg: h.tensor_tensor(
                        out=moe[mt][:, lo:lo + 512], in0=moe[mt][:, lo:lo + 512], in1=ps_dg[:], op=OP.add),
                        r=[dgr, f"moe{mt}"], w=[f"moe{mt}"])
            pb.barrier()
    pb.barrier()

    with ExitStack() as ph4:
        rb_sb, ab_sb, breg = emit_ln_full(cx, lambda c: (moe[c], f"moe{c}"), ph4, bf16_src=False)
        u = [cx.sb("uf", [128, T], BF16, ph4) for c in range(KCH)]
        xs4 = XStream(cx, dram["x_in"], ph4, tag="xf4")
        def store(c, xtile, xreg):
            pb.emit("gpsimd", lambda h, c=c, xtile=xtile: h.dma_start(
                out=dram["x_out"][128 * c:128 * (c + 1), :], in_=xtile[:]),
                r=[xreg], dma=True)
        emit_ln_apply(cx, lambda c: (moe[c], f"moe{c}"), u, lambda c: f"uf{c}", rb_sb, ab_sb,
                      residual_store=(xs4.fetch, store), breg=breg)
        pb.barrier()


def build_l0_ffn_prog(dbg=False):
    nc = bass.Bass("TRN2", target_bir_lowering=False)
    dram = {}
    dram["x_in"] = nc.dram_tensor("x_in", [D, T], F32, kind="ExternalInput").ap()
    dram["router_wT"] = nc.dram_tensor("router_wT", [D, 32], F32, kind="ExternalInput").ap()
    dram["router_bias"] = nc.dram_tensor("router_bias", [1, 32], F32, kind="ExternalInput").ap()
    dram["gate_wT"] = nc.dram_tensor("gate_wT", [32 * D, 128], F32, kind="ExternalInput").ap()
    dram["up_wT"] = nc.dram_tensor("up_wT", [32 * D, 128], F32, kind="ExternalInput").ap()
    dram["down_wT"] = nc.dram_tensor("down_wT", [32 * 128, 1024], F32, kind="ExternalInput").ap()
    dram["sgate_wT"] = nc.dram_tensor("sgate_wT", [D, 512], F32, kind="ExternalInput").ap()
    dram["sup_wT"] = nc.dram_tensor("sup_wT", [D, 512], F32, kind="ExternalInput").ap()
    dram["sdown_wT"] = nc.dram_tensor("sdown_wT", [512, 1024], F32, kind="ExternalInput").ap()
    dram["ident_f"] = nc.dram_tensor("ident_f", [128, 128], F32, kind="ExternalInput").ap()
    dram["x_out"] = nc.dram_tensor("x_out", [D, T], F32, kind="ExternalOutput").ap()
    if dbg:
        dram["d_xn"] = nc.dram_tensor("d_xn", [128, T], BF16, kind="ExternalOutput").ap()
        dram["d_lg"] = nc.dram_tensor("d_lg", [32, T], F32, kind="ExternalOutput").ap()
        dram["d_wtm"] = nc.dram_tensor("d_wtm", [128, 32], F32, kind="ExternalOutput").ap()
        dram["d_wfm"] = nc.dram_tensor("d_wfm", [32, T], F32, kind="ExternalOutput").ap()
        dram["d_moe0"] = nc.dram_tensor("d_moe0", [128, T], F32, kind="ExternalOutput").ap()
        dram["d_pt"] = nc.dram_tensor("d_pt", [128, 32], F32, kind="ExternalOutput").ap()
        dram["d_ht"] = nc.dram_tensor("d_ht", [128, 512], BF16, kind="ExternalOutput").ap()
        dram["d_wrow"] = nc.dram_tensor("d_wrow", [1, 512], F32, kind="ExternalOutput").ap()
        dram["d_gsb"] = nc.dram_tensor("d_gsb", [128, 512], BF16, kind="ExternalOutput").ap()

    with ExitStack() as st:
        sems = {e: st.enter_context(nc.semaphore(f"sem_{e}")) for e in ENG}
        pb = PB(nc)
        cx = Ctx(nc, pb, st)
        make_consts(cx)
        blk = st.enter_context(nc.Block())
        emit_l0_ffn(cx, dram)
        pb.realize(blk, sems)
    return nc


def prep_l0_ffn_weights(inp):
    out = {}
    out["router_wT"] = np.ascontiguousarray(np.asarray(inp["L0_ffn_router_w"]).T)
    out["router_bias"] = np.asarray(inp["L0_ffn_router_bias"]).reshape(1, 32)
    gw = np.asarray(inp["L0_ffn_gate_w"])
    uw = np.asarray(inp["L0_ffn_up_w"])
    dw = np.asarray(inp["L0_ffn_down_w"])
    out["gate_wT"] = np.ascontiguousarray(gw.transpose(0, 2, 1)).reshape(32 * 1024, 128)
    out["up_wT"] = np.ascontiguousarray(uw.transpose(0, 2, 1)).reshape(32 * 1024, 128)
    out["down_wT"] = np.ascontiguousarray(dw.transpose(0, 2, 1)).reshape(32 * 128, 1024)
    out["sgate_wT"] = np.ascontiguousarray(np.asarray(inp["L0_ffn_sgate_w"]).T)
    out["sup_wT"] = np.ascontiguousarray(np.asarray(inp["L0_ffn_sup_w"]).T)
    out["sdown_wT"] = np.ascontiguousarray(np.asarray(inp["L0_ffn_sdown_w"]).T)
    out["ident_f"] = np.eye(128, dtype=np.float32)
    return out


def emit_l1_attn(cx, dram):
    pb, nc = cx.pb, cx.nc
    C1 = 20.0

    ao = [cx.sb("a1o", [128, T], BF16) for c in range(KCH)]

    with ExitStack() as mid:
        with ExitStack() as xnst:
            xn = [cx.sb("x1n", [128, T], BF16, xnst) for c in range(KCH)]
            with ExitStack() as ph0:
                xs = XStream(cx, dram["x_in"], ph0, nbuf=2, tag="xa0")
                rb_sb, ab_sb, breg = emit_ln_full(cx, xs.fetch, ph0, bf16_src=False)
                emit_ln_apply(cx, xs.fetch, xn, lambda c: f"x1n{c}", rb_sb, ab_sb, breg=breg)
                pb.barrier()

            qf = [cx.sb("q1f", [128, T], BF16, mid, side="right") for m in range(KCH)]
            kf = [cx.sb("k1f", [128, T], BF16, mid, side="right") for m in range(KCH)]
            v = [cx.sb("v1t", [128, 1040], BF16, mid, side="right") for t in range(16)]
            cmask = cx.sb("cmask", [128, 4 * 512], BF16, mid, side="right")
            for d4 in range(4):
                pb.emit("sync", lambda h, d4=d4: h.dma_start(
                    out=cmask[:, 512 * d4:512 * (d4 + 1)], in_=dram["cmask"][128 * d4:128 * (d4 + 1), :]),
                    w=["cmask"], dma=True)

            with ExitStack() as ph1:
                wst = [None, None]
                ppAv1 = [cx.ps("p1A", [128, 512], ph1) for i in range(2)]
                ppBv1 = [cx.ps("p1B", [128, 512], ph1) for i in range(2)]
                for wnm, outs, outreg in [("Wq", qf, "q1f"), ("Wk", kf, "k1f")]:
                    with ExitStack() as ws:
                        wt = cx.sb("w1t", [128, 8 * 1024], BF16, ws)
                        emit_load_weight(cx, dram[wnm], wt, "w1t", KCH, 1024, wst[0], "w1stage0")
                        for mt in range(KCH):
                            for nch in range(4):
                                lo = nch * 512
                                pp = (4 * mt + nch) % 2
                                ppA, par_ = ppAv1[pp], f"p1A{pp}"
                                for c in range(KCH):
                                    lh = wt[:, 1024 * c + 128 * mt:1024 * c + 128 * mt + 128]
                                    rh = xn[c][:, lo:lo + 512]
                                    pb.emit("tensor", lambda h, lh=lh, rh=rh, st=(c == 0), sp=(c == KCH - 1), ppA=ppA: h.matmul(
                                        ppA[:], lh, rh, start=st, stop=sp, skip_group_check=True),
                                        r=["w1t", f"x1n{c}"], w=[par_])
                                pb.emit("vector", lambda h, mt=mt, lo=lo, outs=outs, outreg=outreg, ppA=ppA: h.tensor_copy(
                                    out=outs[mt][:, lo:lo + 512], in_=ppA[:]),
                                    r=[par_], w=[f"{outreg}{mt}"])
                        pb.barrier()
                with ExitStack() as ws:
                    wt = cx.sb("w1v", [128, 8 * 1040], BF16, ws)
                    emit_load_weight(cx, dram["Wv"], wt, "w1v", KCH, 1040, wst[1], "w1stage1")
                    itv = 0
                    for tt in range(16):
                        for lo in range(0, 1040, 512):
                            nn = min(512, 1040 - lo)
                            pp = itv % 2
                            itv += 1
                            ppB, pbr_ = ppBv1[pp], f"p1B{pp}"
                            for c in range(KCH):
                                lh = xn[c][:, 128 * tt:128 * tt + 128]
                                rh = wt[:, 1040 * c + lo:1040 * c + lo + nn]
                                pb.emit("tensor", lambda h, lh=lh, rh=rh, st=(c == 0), sp=(c == KCH - 1), nn=nn, ppB=ppB: h.matmul(
                                    ppB[:, 0:nn], lh, rh, start=st, stop=sp, skip_group_check=True),
                                    r=["w1v", f"x1n{c}"], w=[pbr_])
                            pb.emit("vector", lambda h, tt=tt, lo=lo, nn=nn, ppB=ppB: h.tensor_copy(
                                out=v[tt][:, lo:lo + nn], in_=ppB[:, 0:nn]),
                                r=[pbr_], w=[f"v1t{tt}"])
                        def ones_fn(h, tt=tt):
                            vv = v[tt].ap().rearrange("p (g w) -> p g w", w=65)[:, :, 64:65]
                            return h.memset(vv, 1.0)
                        pb.emit("vector", ones_fn, r=[f"v1t{tt}"], w=[f"v1t{tt}"])
                    pb.barrier()
            pb.barrier()

            with ExitStack() as ph2:
                ps_s = [cx.ps("p1s", [128, 1024], ph2) for i in range(2)]
                ps_ctx2 = [cx.ps("p1ctx", [65, 512], ph2) for i in range(2)]
                ps_bc = cx.ps("p1cb", [64, 512], ph2)
                probs = [cx.sb("p1rb", [128, 1024], BF16, ph2) for i in range(2)]
                ctxb2 = [cx.sb("c1b", [64, 512], BF16, ph2) for i in range(2)]
                rrow2 = [cx.sb("r1row", [1, 512], F32, ph2) for i in range(2)]
                negc = const_col(cx, -C1)

                for h_i in range(16):
                    mt, rb0 = h_i // 2, 64 * (h_i % 2)
                    qrows = slice(rb0, rb0 + 64)
                    for qc in range(4):
                        par = (4 * h_i + qc) % 2
                        ps_ctx, creg = ps_ctx2[par], f"p1ctx{par}"
                        ctxb, cbreg = ctxb2[par], f"c1b{par}"
                        rrow, rreg = rrow2[par], f"r1row{par}"
                        qlo = 512 * qc
                        nkt = 4 * qc + 4
                        pairs = [(2 * r_, 2 * r_ + 1) if 2 * r_ + 1 < nkt else (2 * r_, -1)
                                 for r_ in range((nkt + 1) // 2)]
                        n_pv = nkt
                        pv_i = 0
                        def emit_qk_round1(r_):
                            k0, k1 = pairs[r_]
                            pbuf = r_ % 2
                            sreg = f"p1s{pbuf}"
                            for gi, kt in enumerate([k0, k1]):
                                if kt < 0:
                                    continue
                                lh = kf[mt][qrows, 128 * kt:128 * kt + 128]
                                rh = qf[mt][qrows, qlo:qlo + 512]
                                pb.emit("tensor", lambda h, lh=lh, rh=rh, gi=gi, pbuf=pbuf, rb0=rb0: h.matmul(
                                    ps_s[pbuf][:, 512 * gi:512 * gi + 512], lh, rh,
                                    start=True, stop=True, skip_group_check=True, tile_position=(rb0, 0)),
                                    r=[f"k1f{mt}", f"q1f{mt}"], w=[sreg])
                        emit_qk_round1(0)
                        for r_, (k0, k1) in enumerate(pairs):
                            pbuf = r_ % 2
                            sreg, preg = f"p1s{pbuf}", f"p1rb{pbuf}"
                            if r_ + 1 < len(pairs):
                                emit_qk_round1(r_ + 1)
                            wid = 1024 if k1 >= 0 else 512
                            pb.emit("scalar", lambda h, pbuf=pbuf, wid=wid, negc=negc: h.activation(
                                out=probs[pbuf][:, 0:wid], in_=ps_s[pbuf][:, 0:wid],
                                func=AF.Exp, bias=negc, scale=0.125),
                                r=[sreg, "cc"], w=[preg])
                            for gi, kt in enumerate([k0, k1]):
                                if kt < 0:
                                    continue
                                d = kt - 4 * qc
                                if d >= 0:
                                    pb.emit("gpsimd", lambda h, gi=gi, d=d, pbuf=pbuf: h.tensor_tensor(
                                        out=probs[pbuf][:, 512 * gi:512 * gi + 512],
                                        in0=probs[pbuf][:, 512 * gi:512 * gi + 512],
                                        in1=cmask[:, 512 * d:512 * d + 512], op=OP.mult),
                                        r=[preg, "cmask"], w=[preg])
                                pv_i += 1
                                lh = v[kt][:, 65 * h_i:65 * h_i + 65]
                                rh = probs[pbuf][0:128, 512 * gi:512 * gi + 512]
                                pb.emit("tensor", lambda h, lh=lh, rh=rh, st=(pv_i == 1), sp=(pv_i == n_pv), ps_ctx=ps_ctx: h.matmul(
                                    ps_ctx[:], lh, rh, start=st, stop=sp, skip_group_check=True),
                                    r=[f"v1t{kt}", preg], w=[creg])
                        pb.emit("vector", lambda h, rrow=rrow, ps_ctx=ps_ctx: h.tensor_scalar(
                            out=rrow[:], in0=ps_ctx[64:65, 0:512], scalar1=1e-30, scalar2=None, op0=OP.max),
                            r=[creg], w=[rreg], risky=True)
                        pb.emit("vector", lambda h, rrow=rrow: h.reciprocal(out=rrow[:], in_=rrow[:]),
                                r=[rreg], w=[rreg], risky=True)
                        pb.emit("tensor", lambda h, rrow=rrow: h.matmul(ps_bc[:], cx.ones_f[0:1, 0:64], rrow[:],
                                                             start=True, stop=True, skip_group_check=True),
                                r=["ones_f", rreg], w=["p1cb"])
                        pb.emit("scalar", lambda h, ctxb=ctxb, ps_ctx=ps_ctx: h.activation(out=ctxb[:], in_=ps_ctx[0:64, 0:512],
                                                                 func=AF.Copy),
                                r=[creg], w=[cbreg])
                        pb.emit("vector", lambda h, mt=mt, rb0=rb0, qlo=qlo, ctxb=ctxb: h.tensor_tensor(
                            out=ao[mt][rb0:rb0 + 64, qlo:qlo + 512], in0=ctxb[:], in1=ps_bc[:], op=OP.mult),
                            r=[cbreg, "p1cb"], w=[f"a1o{mt}"])
                pb.barrier()
    pb.barrier()

    wout = [cx.sb("wo_out", [128, T], BF16) for c in range(KCH)]
    with ExitStack() as ph3:
        wst = None
        wt = cx.sb("w3t", [128, 8 * 1024], BF16, ph3)
        emit_load_weight(cx, dram["Wo"], wt, "w3t", KCH, 1024, wst, "w3stage")
        ppAv3 = [cx.ps("p3A", [128, 512], ph3) for i in range(2)]
        for mt in range(KCH):
            for nch in range(4):
                lo = nch * 512
                pp = (4 * mt + nch) % 2
                ppA, par_ = ppAv3[pp], f"p3A{pp}"
                for c in range(KCH):
                    lh = wt[:, 1024 * c + 128 * mt:1024 * c + 128 * mt + 128]
                    rh = ao[c][:, lo:lo + 512]
                    pb.emit("tensor", lambda h, lh=lh, rh=rh, st=(c == 0), sp=(c == KCH - 1), ppA=ppA: h.matmul(
                        ppA[:], lh, rh, start=st, stop=sp, skip_group_check=True),
                        r=["w3t", f"a1o{c}"], w=[par_])
                pb.emit("vector", lambda h, mt=mt, lo=lo, ppA=ppA: h.tensor_copy(
                    out=wout[mt][:, lo:lo + 512], in_=ppA[:]),
                    r=[par_], w=[f"wo_out{mt}"])
        pb.barrier()

    with ExitStack() as ph4:
        rb_sb, ab_sb, breg = emit_ln_full(cx, lambda c: (wout[c], f"wo_out{c}"), ph4, bf16_src=True)
        u = [cx.sb("u1", [128, T], BF16, ph4) for c in range(KCH)]
        xs4 = XStream(cx, dram["x_in"], ph4, tag="xa4")
        def store(c, xtile, xreg):
            pb.emit("gpsimd", lambda h, c=c, xtile=xtile: h.dma_start(
                out=dram["x_out"][128 * c:128 * (c + 1), :], in_=xtile[:]),
                r=[xreg], dma=True)
        emit_ln_apply(cx, lambda c: (wout[c], f"wo_out{c}"), u, lambda c: f"u1{c}", rb_sb, ab_sb,
                      residual_store=(xs4.fetch, store), breg=breg)
        pb.barrier()


def build_l1_attn_prog():
    nc = bass.Bass("TRN2", target_bir_lowering=False)
    dram = {}
    dram["x_in"] = nc.dram_tensor("x_in", [D, T], F32, kind="ExternalInput").ap()
    for nm, sh in [("Wq", [D, D]), ("Wk", [D, D]), ("Wv", [D, 1040]), ("Wo", [D, D])]:
        dram[nm] = nc.dram_tensor(nm, sh, F32, kind="ExternalInput").ap()
    dram["cmask"] = nc.dram_tensor("cmask", [4 * 128, 512], BF16, kind="ExternalInput").ap()
    dram["x_out"] = nc.dram_tensor("x_out", [D, T], F32, kind="ExternalOutput").ap()

    with ExitStack() as st:
        sems = {e: st.enter_context(nc.semaphore(f"sem_{e}")) for e in ENG}
        pb = PB(nc)
        cx = Ctx(nc, pb, st)
        make_consts(cx)
        blk = st.enter_context(nc.Block())
        emit_l1_attn(cx, dram)
        pb.realize(blk, sems)
    return nc


def prep_l1_attn_weights(inp):
    out = {}
    out["Wq"] = np.asarray(inp["L1_attn_Wq"])
    out["Wk"] = np.asarray(inp["L1_attn_Wk"])
    Wv = np.asarray(inp["L1_attn_Wv"])
    Wva = np.zeros((D, 1040), np.float32)
    for h in range(16):
        Wva[:, 65 * h:65 * h + 64] = Wv[:, 64 * h:64 * h + 64]
    out["Wv"] = Wva
    out["Wo"] = np.asarray(inp["L1_attn_Wo"])
    cm = np.zeros((4 * 128, 512), np.float32)
    for d in range(4):
        i = np.arange(128)[:, None]
        j = np.arange(512)[None, :]
        cm[128 * d:128 * (d + 1)] = (j >= 128 * d + i).astype(np.float32)
    out["cmask"] = cm.astype(BF)
    return out


def emit_l1_ffn(cx, dram):
    pb, nc = cx.pb, cx.nc
    TP = 2049

    xi = [cx.sb("xi", [128, TP], F32) for c in range(KCH)]
    for c in range(KCH):
        pb.emit("vector", lambda h, c=c: h.memset(xi[c][:, 0:1], 0.0), w=[f"xi{c}"])

    ind2b(cx)
    mle = cx.sb("mle", [128, 128], BF16)
    mge = cx.sb("mge", [128, 128], BF16)
    idf = cx.sb("lidf", [128, 128], F32)
    pb.emit("sync", lambda h: h.dma_start(out=mle[:], in_=dram["mle"]), w=["mle"], dma=True)
    pb.emit("sync", lambda h: h.dma_start(out=mge[:], in_=dram["mge"]), w=["mge"], dma=True)
    pb.emit("sync", lambda h: h.dma_start(out=idf[:], in_=dram["ident_f"]), w=["idf"], dma=True)

    with ExitStack() as ph0:
        xs = XStream(cx, dram["x_in"], ph0, tag="xl0")
        rb_sb, ab_sb, breg = emit_ln_full(cx, xs.fetch, ph0, bf16_src=False)
        emit_ln_apply(cx, xs.fetch, xi, lambda c: f"xi{c}", rb_sb, ab_sb, off_out=1, breg=breg)
        dbg_dump(cx, dram, "d_xi", xi[0][:, 1:TP], ["xi0"])
        pb.barrier()

    def _do_layer(layer):
        lay = ExitStack()
        xb = [cx.sb("xib", [128, TP], BF16, lay) for c in range(KCH)]
        tabs = {nm: cx.sb("tab" + nm, [128, T], BF16, lay) for nm in ["SQ", "TQ", "SK", "TK"]}
        with ExitStack() as tload:
            tstage = cx.sb("tstage", [128, T], F32, tload)
            for nm in ["SQ", "TQ", "SK", "TK"]:
                pb.emit("sync", lambda h, nm=nm: h.dma_start(out=tstage[:], in_=dram["tab_" + nm]),
                        w=["tstage"], dma=True)
                pb.emit("vector", lambda h, nm=nm: h.tensor_copy(out=tabs[nm][:], in_=tstage[:]),
                        r=["tstage"], w=["tab" + nm])
            pb.barrier()
        for c in range(KCH):
            pb.emit("vector", lambda h, c=c: h.memset(xb[c][:, 0:1], 0.0), w=[f"xib{c}"])
            pb.emit("vector", lambda h, c=c: h.tensor_copy(out=xb[c][:, 1:TP], in_=xi[c][:, 1:TP]),
                    r=[f"xi{c}"], w=[f"xib{c}"])

        def hview(c, lo, nn):
            o = 1 if c < 4 else 0
            return xb[c][:, o + lo:o + lo + nn]

        qf = [cx.sb("lqf", [128, T], BF16, lay) for m in range(4)]
        kf = [cx.sb("lkf", [128, T], BF16, lay) for m in range(4)]
        vt = [cx.sb("lvt", [128, 520], BF16, lay) for t in range(16)]
        ofm = [cx.sb("lofm", [128, T], BF16, lay) for m in range(4)]

        with ExitStack() as ph1:
            wst = [None, None]
            ppAvL = [cx.ps("lpA", [128, 512], ph1) for i in range(2)]
            for wnm, outs, outreg in [("wq", qf, "lqf"), ("wk", kf, "lkf")]:
                with ExitStack() as ws:
                    wt = cx.sb("lwt", [128, 8 * 512], BF16, ws)
                    emit_load_weight(cx, dram[wnm][D * layer:D * (layer + 1), :], wt,
                                     "lwt", KCH, 512, wst[0], "lwst0")
                    for mt in range(4):
                        for nch in range(4):
                            lo = nch * 512
                            pp = (4 * mt + nch) % 2
                            ppA, par_ = ppAvL[pp], f"lpA{pp}"
                            for c in range(KCH):
                                lh = wt[:, 512 * c + 128 * mt:512 * c + 128 * mt + 128]
                                rh = hview(c, lo, 512)
                                pb.emit("tensor", lambda h, lh=lh, rh=rh, st=(c == 0), sp=(c == KCH - 1), ppA=ppA: h.matmul(
                                    ppA[:], lh, rh, start=st, stop=sp, skip_group_check=True),
                                    r=["lwt", f"xib{c}"], w=[par_])
                            pb.emit("vector", lambda h, mt=mt, lo=lo, outs=outs, outreg=outreg, ppA=ppA: h.tensor_copy(
                                out=outs[mt][:, lo:lo + 512], in_=ppA[:]),
                                r=[par_], w=[f"{outreg}{mt}"])
                    pb.barrier()
            with ExitStack() as ws:
                wt = cx.sb("lwv", [128, 8 * 520], BF16, ws)
                emit_load_weight(cx, dram["wv"][D * layer:D * (layer + 1), :], wt,
                                 "lwv", KCH, 520, wst[1], "lwst1")
                itv = 0
                for tt in range(16):
                    for lo, nn in [(0, 512), (512, 8)]:
                        pp = itv % 2
                        itv += 1
                        ppA, par_ = ppAvL[pp], f"lpA{pp}"
                        for c in range(KCH):
                            lh = hview(c, 128 * tt, 128)
                            rh = wt[:, 520 * c + lo:520 * c + lo + nn]
                            pb.emit("tensor", lambda h, lh=lh, rh=rh, st=(c == 0), sp=(c == KCH - 1), nn=nn, ppA=ppA: h.matmul(
                                ppA[:, 0:nn], lh, rh, start=st, stop=sp, skip_group_check=True),
                                r=["lwv", f"xib{c}"], w=[par_])
                        pb.emit("vector", lambda h, tt=tt, lo=lo, nn=nn, ppA=ppA: h.tensor_copy(
                            out=vt[tt][:, lo:lo + nn], in_=ppA[:, 0:nn]),
                            r=[par_], w=[f"lvt{tt}"])
                    def ones_fn(h, tt=tt):
                        vv = vt[tt].ap().rearrange("p (g w) -> p g w", w=65)[:, :, 64:65]
                        return h.memset(vv, 1.0)
                    pb.emit("vector", ones_fn, r=[f"lvt{tt}"], w=[f"lvt{tt}"])
                pb.barrier()

        with ExitStack() as ph2:
            sqtv = [cx.sb("lsqt", [128, 512], BF16, ph2) for i in range(2)]
            nrowt = cx.sb("lnrow", [33, T], F32, ph2)
            swp = cx.sb("lswp", [128, T // 2], BF16, ph2)
            qsrow = cx.sb("lqs", [33, 64], F32, ph2)
            ph2ps = ExitStack()
            ps_nv = [cx.ps("lps_n", [1, 512], ph2ps) for i in range(2)]
            ps_ov = [cx.ps("lps_o", [64, 512], ph2ps) for i in range(2)]

            for qk_i, (ff, freg, snm, tnm, scnm) in enumerate([
                    (qf, "lqf", "SQ", "TQ", "q_scale"), (kf, "lkf", "SK", "TK", "k_scale")]):
                pb.emit("sync", lambda h, scnm=scnm: h.dma_start(
                    out=qsrow[0:1, :], in_=dram[scnm][layer:layer + 1, :]), w=["lqs"], dma=True)
                pb.emit("sync", lambda h, scnm=scnm: h.dma_start(
                    out=qsrow[32:33, :], in_=dram[scnm][layer:layer + 1, :]), w=["lqs"], dma=True)
                for mt in range(4):
                    for hh2 in range(2):
                        hh = 2 * mt + hh2
                        pn = hh % 2
                        ps_n, pnr = ps_nv[pn], f"lps_n{pn}"
                        nrow, nrr = nrowt[32 * pn:32 * pn + 1, :], f"lnrow{pn}"
                        for nch in range(4):
                            lo = nch * 512
                            pq = nch % 2
                            sqt, sqreg = sqtv[pq], f"lsqt{pq}"
                            pb.emit("scalar", lambda h, mt=mt, ff=ff, lo=lo, sqt=sqt: h.activation(
                                out=sqt[:], in_=ff[mt][:, lo:lo + 512], func=AF.Square),
                                r=[f"{freg}{mt}"], w=[sqreg])
                            ind_ap = ind2b(cx)[:, hh2:hh2 + 1]
                            pb.emit("tensor", lambda h, ind_ap=ind_ap, sqt=sqt, ps_n=ps_n: h.matmul(
                                ps_n[:], ind_ap, sqt[:],
                                start=True, stop=True, skip_group_check=True),
                                r=["lindb", sqreg], w=[pnr])
                            pb.emit("vector", lambda h, lo=lo, ps_n=ps_n, pn=pn: h.tensor_copy(
                                out=nrowt[32 * pn:32 * pn + 1, lo:lo + 512], in_=ps_n[:]),
                                r=[pnr], w=[nrr])
                        pb.emit("vector", lambda h, nrow=nrow: h.tensor_scalar(
                            out=nrow, in0=nrow, scalar1=1e-24, scalar2=None, op0=OP.max),
                            r=[nrr], w=[nrr])
                        pb.emit("scalar", lambda h, nrow=nrow: h.activation(out=nrow, in_=nrow, func=AF.Ln),
                                r=[nrr], w=[nrr])
                        pb.emit("scalar", lambda h, nrow=nrow: h.activation(out=nrow, in_=nrow, func=AF.Exp,
                                                                 scale=-0.5),
                                r=[nrr], w=[nrr])
                        for nch in range(4):
                            lo = nch * 512
                            po = (2 * hh + nch) % 2
                            ps_o, por = ps_ov[po], f"lps_o{po}"
                            pb.emit("tensor", lambda h, lo=lo, ps_o=ps_o, pn=pn: h.matmul(
                                ps_o[:], qsrow[32 * pn:32 * pn + 1, :], nrowt[32 * pn:32 * pn + 1, lo:lo + 512],
                                start=True, stop=True, skip_group_check=True),
                                r=["lqs", nrr], w=[por])
                            rb0 = 64 * hh2
                            ap = ff[mt][rb0:rb0 + 64, lo:lo + 512]
                            pb.emit("vector", lambda h, ap=ap, ps_o=ps_o: h.tensor_tensor(
                                out=ap, in0=ap, in1=ps_o[:], op=OP.mult),
                                r=[f"{freg}{mt}", por], w=[f"{freg}{mt}"])
                    stab, ttab = tabs[snm], tabs[tnm]
                    for half in range(2):
                        hlo = 1024 * half
                        for hh2 in range(2):
                            rb0 = 64 * hh2
                            pb.emit("gpsimd", lambda h, ff=ff, mt=mt, rb0=rb0, hlo=hlo: h.dma_start(
                                out=swp[rb0 + 32:rb0 + 64, :], in_=ff[mt][rb0:rb0 + 32, hlo:hlo + 1024]),
                                r=[f"{freg}{mt}"], w=["lswp"], dma=True)
                            pb.emit("gpsimd", lambda h, ff=ff, mt=mt, rb0=rb0, hlo=hlo: h.dma_start(
                                out=swp[rb0:rb0 + 32, :], in_=ff[mt][rb0 + 32:rb0 + 64, hlo:hlo + 1024]),
                                r=[f"{freg}{mt}"], w=["lswp"], dma=True)
                        pb.emit("vector", lambda h, ff=ff, mt=mt, stab=stab, hlo=hlo: h.tensor_tensor(
                            out=ff[mt][:, hlo:hlo + 1024], in0=ff[mt][:, hlo:hlo + 1024],
                            in1=stab[:, hlo:hlo + 1024], op=OP.mult),
                            r=[f"{freg}{mt}", "tab" + snm], w=[f"{freg}{mt}"])
                        pb.emit("vector", lambda h, ttab=ttab, hlo=hlo: h.tensor_tensor(
                            out=swp[:], in0=swp[:], in1=ttab[:, hlo:hlo + 1024], op=OP.mult),
                            r=["lswp", "tab" + tnm], w=["lswp"])
                        pb.emit("vector", lambda h, ff=ff, mt=mt, hlo=hlo: h.tensor_tensor(
                            out=ff[mt][:, hlo:hlo + 1024], in0=ff[mt][:, hlo:hlo + 1024],
                            in1=swp[:], op=OP.add),
                            r=[f"{freg}{mt}", "lswp"], w=[f"{freg}{mt}"])
                    pass
            pb.barrier()

            pb.barrier()
            ph2ps.close()
            with ExitStack() as ph3:
                ps_sc2 = [cx.ps("lps_sc", [128, 1024], ph3) for i in range(2)]
                p1ps = ExitStack()
                ps_tp = cx.ps("lps_tp", [1, 512], p1ps)
                ps_b2 = cx.ps("lps_b2", [128, 512], p1ps)
                prb2 = [cx.sb("lprb", [128, 640], BF16, ph3) for i in range(2)]
                ctxb2l = [cx.sb("lctxb", [64, 128], BF16, ph3) for i in range(2)]
                rrow2l = [cx.sb("lrrow", [1, 128], F32, ph3) for i in range(2)]
                small4 = cx.sb("lsm4", [128, 4], F32, ph3)
                mcol2 = [small4.ap()[:, 0:1], small4.ap()[:, 1:2]]
                mrow1 = cx.sb("lmrow1", [1, 128], F32, ph3)
                negBrow = cx.sb("lnegB", [1, 128], F32, ph3)
                Bbl = cx.sb("lBb", [128, 128], F32, ph3)

                its = [(hh, qc) for hh in range(8) for qc in range(16)]

                def qk_emit(idx):
                    hh, qc = its[idx]
                    mt, rb0 = hh // 2, 64 * (hh % 2)
                    qrows = slice(rb0, rb0 + 64)
                    par = idx % 2
                    ps_sc, screg = ps_sc2[par], f"lps_sc{par}"
                    r0 = max(0, qc - 4)
                    kts = list(range(r0, qc + 1))
                    for g, r_ in enumerate(kts):
                        lh = kf[mt][qrows, 128 * r_:128 * r_ + 128]
                        rh = qf[mt][qrows, 128 * qc:128 * qc + 128]
                        pb.emit("tensor", lambda h, lh=lh, rh=rh, g=g, rb0=rb0, ps_sc=ps_sc: h.matmul(
                            ps_sc[:, 128 * g:128 * g + 128], lh, rh,
                            start=True, stop=True, skip_group_check=True, tile_position=(rb0, 0)),
                            r=[f"lkf{mt}", f"lqf{mt}"], w=[screg])
                    return kts

                kts_cache = {}
                for idx in range(len(its) + 1):
                    if idx < len(its):
                        kts_cache[idx] = qk_emit(idx)
                    if idx >= 1:
                        i0 = idx - 1
                        hh, qc = its[i0]
                        par = i0 % 2
                        ps_sc, screg = ps_sc2[par], f"lps_sc{par}"
                        mcol, mcreg = mcol2[par], f"lmcol{par}"
                        wid = 128 * len(kts_cache[i0])
                        pb.emit("vector", lambda h, wid=wid, mcol=mcol, ps_sc=ps_sc: h.reduce_max(
                            out=mcol, in_=ps_sc[:, 0:wid], axis=AX.X),
                            r=[screg], w=[mcreg], risky=True)
                        pb.emit("tensor", lambda h, mcol=mcol: h.transpose(ps_tp[:, 0:128], mcol, idf[:]),
                                r=[mcreg, "idf"], w=["lps_tp"])
                        pb.emit("vector", lambda h: h.tensor_copy(out=mrow1[:], in_=ps_tp[0:1, 0:128]),
                                r=["lps_tp"], w=["lmrow1"])
                        col = 16 * hh + qc
                        pb.emit("vector", lambda h, col=col: h.reduce_max(
                            out=negBrow[0:1, col:col + 1], in_=mrow1[:], axis=AX.X, negate=True),
                            r=["lmrow1"], w=["lnegB"], risky=True)
                pb.emit("vector", lambda h: h.tensor_scalar(out=negBrow[:], in0=negBrow[:],
                                                            scalar1=8.0, scalar2=None, op0=OP.mult),
                        r=["lnegB"], w=["lnegB"])
                pb.emit("tensor", lambda h: h.matmul(ps_b2[:, 0:128], cx.ones_f[0:1, 0:128], negBrow[:],
                                                     start=True, stop=True, skip_group_check=True),
                        r=["ones_f", "lnegB"], w=["lps_b2"])
                pb.emit("vector", lambda h: h.tensor_copy(out=Bbl[:], in_=ps_b2[:, 0:128]),
                        r=["lps_b2"], w=["lBb"])
                pb.barrier()
                p1ps.close()
                ps_ctx2l = [cx.ps("lps_ctx", [65, 512], ph3) for i in range(2)]
                ps_cb = cx.ps("lps_cb", [64, 512], ph3)

                def rest_emit(i0):
                    hh, qc = its[i0]
                    mt, rb0 = hh // 2, 64 * (hh % 2)
                    par = i0 % 2
                    ps_sc, screg = ps_sc2[par], f"lps_sc{par}"
                    prb, preg2 = prb2[par], f"lprb{par}"
                    ps_ctx, creg = ps_ctx2l[par], f"lps_ctx{par}"
                    ctxb, cbreg = ctxb2l[par], f"lctxb{par}"
                    rrow, rreg = rrow2l[par], f"lrrow{par}"
                    kts = kts_cache[i0]
                    wid = 128 * len(kts)
                    col = 16 * hh + qc
                    pb.emit("scalar", lambda h, wid=wid, col=col, prb=prb, ps_sc=ps_sc: h.activation(
                        out=prb[:, 0:wid], in_=ps_sc[:, 0:wid], func=AF.Exp,
                        bias=Bbl[:, col:col + 1], scale=8.0),
                        r=[screg, "lBb"], w=[preg2])
                    for g, r_ in enumerate(kts):
                        if r_ == qc - 4:
                            pb.emit("gpsimd", lambda h, g=g, prb=prb: h.tensor_tensor(
                                out=prb[:, 128 * g:128 * g + 128], in0=prb[:, 128 * g:128 * g + 128],
                                in1=mle[:], op=OP.mult),
                                r=[preg2, "mle"], w=[preg2])
                        if r_ == qc:
                            pb.emit("gpsimd", lambda h, g=g, prb=prb: h.tensor_tensor(
                                out=prb[:, 128 * g:128 * g + 128], in0=prb[:, 128 * g:128 * g + 128],
                                in1=mge[:], op=OP.mult),
                                r=[preg2, "mge"], w=[preg2])
                    for g, r_ in enumerate(kts):
                        lh = vt[r_][:, 65 * hh:65 * hh + 65]
                        rh = prb[:, 128 * g:128 * g + 128]
                        pb.emit("tensor", lambda h, lh=lh, rh=rh, st=(g == 0), sp=(g == len(kts) - 1), ps_ctx=ps_ctx: h.matmul(
                            ps_ctx[:, 0:128], lh, rh, start=st, stop=sp, skip_group_check=True),
                            r=[f"lvt{r_}", preg2], w=[creg])
                    pb.emit("vector", lambda h, rrow=rrow, ps_ctx=ps_ctx: h.tensor_scalar(
                        out=rrow[:], in0=ps_ctx[64:65, 0:128], scalar1=1e-30, scalar2=None, op0=OP.max),
                        r=[creg], w=[rreg], risky=True)
                    pb.emit("vector", lambda h, rrow=rrow: h.reciprocal(out=rrow[:], in_=rrow[:]),
                            r=[rreg], w=[rreg], risky=True)
                    pb.emit("tensor", lambda h, rrow=rrow: h.matmul(ps_cb[:, 0:128], cx.ones_f[0:1, 0:64], rrow[:],
                                                         start=True, stop=True, skip_group_check=True),
                            r=["ones_f", rreg], w=["lps_cb"])
                    pb.emit("scalar", lambda h, ctxb=ctxb, ps_ctx=ps_ctx: h.activation(
                        out=ctxb[:], in_=ps_ctx[0:64, 0:128], func=AF.Copy),
                        r=[creg], w=[cbreg])
                    pb.emit("vector", lambda h, mt=mt, rb0=rb0, qc=qc, ctxb=ctxb: h.tensor_tensor(
                        out=ofm[mt][rb0:rb0 + 64, 128 * qc:128 * qc + 128], in0=ctxb[:],
                        in1=ps_cb[:, 0:128], op=OP.mult),
                        r=[cbreg, "lps_cb"], w=[f"lofm{mt}"])

                for idx in range(len(its) + 1):
                    if idx < len(its):
                        qk_emit(idx)
                    if idx >= 1:
                        rest_emit(idx - 1)
                pb.barrier()

        with ExitStack() as ph4:
            wst = None
            wt = cx.sb("lw4t", [128, 4 * 1024], BF16, ph4)
            emit_load_weight(cx, dram["wo"][512 * layer:512 * (layer + 1), :], wt, "lw4t",
                             4, 1024, wst, "lw4st")
            ppAv4 = [cx.ps("lp4A", [128, 512], ph4) for i in range(2)]
            for mt in range(KCH):
                for nch in range(4):
                    lo = nch * 512
                    pp = (4 * mt + nch) % 2
                    ppA, par_ = ppAv4[pp], f"lp4A{pp}"
                    for c4 in range(4):
                        lh = wt[:, 1024 * c4 + 128 * mt:1024 * c4 + 128 * mt + 128]
                        rh = ofm[c4][:, lo:lo + 512]
                        pb.emit("tensor", lambda h, lh=lh, rh=rh, st=(c4 == 0), sp=(c4 == 3), ppA=ppA: h.matmul(
                            ppA[:], lh, rh, start=st, stop=sp, skip_group_check=True),
                            r=["lw4t", f"lofm{c4}"], w=[par_])
                    pb.emit("vector", lambda h, mt=mt, lo=lo, ppA=ppA: h.tensor_tensor(
                        out=xi[mt][:, 1 + lo:1 + lo + 512], in0=xi[mt][:, 1 + lo:1 + lo + 512],
                        in1=ppA[:], op=OP.add),
                        r=[par_, f"xi{mt}"], w=[f"xi{mt}"])
            if layer == 0:
                dbg_dump(cx, dram, "d_o", ofm[0][:], ["lofm0"])
                dbg_dump(cx, dram, "d_x1", xi[0][:, 1:TP], ["xi0"])
            pb.barrier()
        lay.close()
        pb.barrier()


        with ExitStack() as ph5:
            lgc = cx.sb("llg", [128, 8], F32, ph5)
            lbc = cx.sb("llb", [128, 8], F32, ph5)
            for c in range(KCH):
                pb.emit("sync", lambda h, c=c: h.dma_start(
                    out=lgc[:, c:c + 1], in_=dram["ln_g"][layer:layer + 1, 128 * c:128 * (c + 1)].rearrange("a p -> p a")),
                    w=["llg"], dma=True)
                pb.emit("sync", lambda h, c=c: h.dma_start(
                    out=lbc[:, c:c + 1], in_=dram["ln_b"][layer:layer + 1, 128 * c:128 * (c + 1)].rearrange("a p -> p a")),
                    w=["llb"], dma=True)
            lnf = [cx.sb("llnf", [128, T], BF16, ph5) for c in range(KCH)]
            ph5a = ExitStack()
            rb_sb, ab_sb, breg = emit_ln_full(
                cx, lambda c: (xi[c], f"xi{c}"), ph5a, bf16_src=False,
                off=(lambda c: 1 if c < 4 else 0))
            for c in range(KCH):
                o = 1 if c < 4 else 0
                for nch in range(4):
                    lo = nch * 512
                    sap = xi[c][:, o + lo:o + lo + 512]
                    pb.emit("vector", lambda h, sap=sap, c=c, lo=lo: h.tensor_tensor(
                        out=lnf[c][:, lo:lo + 512], in0=sap, in1=rb_sb[:, lo:lo + 512], op=OP.mult),
                        r=[f"xi{c}", breg[0]], w=[f"llnf{c}"])
                    pb.emit("vector", lambda h, c=c, lo=lo: h.tensor_tensor(
                        out=lnf[c][:, lo:lo + 512], in0=lnf[c][:, lo:lo + 512],
                        in1=ab_sb[:, lo:lo + 512], op=OP.add),
                        r=[f"llnf{c}", breg[1]], w=[f"llnf{c}"])
                    pb.emit("vector", lambda h, c=c, lo=lo: h.tensor_scalar(
                        out=lnf[c][:, lo:lo + 512], in0=lnf[c][:, lo:lo + 512],
                        scalar1=lgc[:, c:c + 1], scalar2=lbc[:, c:c + 1],
                        op0=OP.mult, op1=OP.add),
                        r=[f"llnf{c}", "llg", "llb"], w=[f"llnf{c}"])

            pb.barrier()
            ph5a.close()
            b1ac = cx.sb("lb1a", [128, 22], F32, ph5)
            b1gc = cx.sb("lb1g", [128, 22], F32, ph5)
            b2c = cx.sb("lb2", [128, 8], F32, ph5)
            for hc in range(22):
                pb.emit("sync", lambda h, hc=hc: h.dma_start(
                    out=b1ac[:, hc:hc + 1], in_=dram["b1a"][layer:layer + 1, 128 * hc:128 * (hc + 1)].rearrange("a p -> p a")),
                    w=["lb1a"], dma=True)
                pb.emit("sync", lambda h, hc=hc: h.dma_start(
                    out=b1gc[:, hc:hc + 1], in_=dram["b1g"][layer:layer + 1, 128 * hc:128 * (hc + 1)].rearrange("a p -> p a")),
                    w=["lb1g"], dma=True)
            for c in range(KCH):
                pb.emit("sync", lambda h, c=c: h.dma_start(
                    out=b2c[:, c:c + 1], in_=dram["b2"][layer:layer + 1, 128 * c:128 * (c + 1)].rearrange("a p -> p a")),
                    w=["lb2"], dma=True)
            w1stage = [None, None]
            w1at = [cx.sb("lw1a", [128, 8 * 128], BF16, ph5) for i in range(4)]
            w1gt = [cx.sb("lw1g", [128, 8 * 128], BF16, ph5) for i in range(4)]
            w2t = [cx.sb("lw2", [128, 1024], BF16, ph5) for i in range(4)]
            hgrp = [cx.sb("lhg", [128, T], BF16, ph5) for i in range(4)]
            asb = cx.sb("lasb", [128, 512], BF16, ph5)
            gsb = cx.sb("lgsb", [128, 512], BF16, ph5)
            psAv = [cx.ps("lp5A", [128, 512], ph5) for i in range(2)]
            psGv = [cx.ps("lp5G", [128, 512], ph5) for i in range(2)]
            ps2v = [cx.ps("lp5W", [128, 512], ph5) for i in range(2)]
            asbv = [asb, cx.sb("lasb2", [128, 512], BF16, ph5)]
            gsbv = [gsb, cx.sb("lgsb2", [128, 512], BF16, ph5)]
            groups = [list(range(i, min(i + 4, 22))) for i in range(0, 22, 4)]
            for grp in groups:
                for i, hc in enumerate(grp):
                    emit_load_weight(cx, dram["w1a"][(22 * layer + hc) * 1024:(22 * layer + hc + 1) * 1024, :],
                                     w1at[i], f"lw1a{i}", KCH, 128, w1stage[0], "lw1st0")
                    emit_load_weight(cx, dram["w1g"][(22 * layer + hc) * 1024:(22 * layer + hc + 1) * 1024, :],
                                     w1gt[i], f"lw1g{i}", KCH, 128, w1stage[1], "lw1st1")
                    emit_load_weight(cx, dram["w2p"][(22 * layer + hc) * 128:(22 * layer + hc + 1) * 128, :],
                                     w2t[i], f"lw2{i}", 1, 1024, w1stage[0], "lw1st0")
                for i, hc in enumerate(grp):
                    for nch in range(4):
                        lo = nch * 512
                        pp = (4 * i + nch) % 2
                        psA, pAr = psAv[pp], f"lp5A{pp}"
                        psG, pGr = psGv[pp], f"lp5G{pp}"
                        asb_, asr = asbv[pp], f"lasb{pp}"
                        gsb_, gsr = gsbv[pp], f"lgsb{pp}"
                        for c in range(KCH):
                            lh = w1at[i][:, 128 * c:128 * c + 128]
                            pb.emit("tensor", lambda h, lh=lh, c=c, lo=lo, st=(c == 0), sp=(c == KCH - 1), psA=psA: h.matmul(
                                psA[:], lh, lnf[c][:, lo:lo + 512], start=st, stop=sp, skip_group_check=True),
                                r=[f"lw1a{i}", f"llnf{c}"], w=[pAr])
                        for c in range(KCH):
                            lh = w1gt[i][:, 128 * c:128 * c + 128]
                            pb.emit("tensor", lambda h, lh=lh, c=c, lo=lo, st=(c == 0), sp=(c == KCH - 1), psG=psG: h.matmul(
                                psG[:], lh, lnf[c][:, lo:lo + 512], start=st, stop=sp, skip_group_check=True),
                                r=[f"lw1g{i}", f"llnf{c}"], w=[pGr])
                        pb.emit("scalar", lambda h, hc=hc, psG=psG, gsb_=gsb_: h.activation(
                            out=gsb_[:], in_=psG[:], func=AF.Gelu, bias=b1gc[:, hc:hc + 1]),
                            r=[pGr, "lb1g"], w=[gsr])
                        pb.emit("vector", lambda h, hc=hc, psA=psA, asb_=asb_: h.tensor_scalar(
                            out=asb_[:], in0=psA[:], scalar1=b1ac[:, hc:hc + 1], scalar2=None,
                            op0=OP.add),
                            r=[pAr, "lb1a"], w=[asr])
                        pb.emit("vector", lambda h, i=i, lo=lo, asb_=asb_, gsb_=gsb_: h.tensor_tensor(
                            out=hgrp[i][:, lo:lo + 512], in0=asb_[:], in1=gsb_[:], op=OP.mult),
                            r=[asr, gsr], w=[f"lhg{i}"])
                for mt in range(KCH):
                    for nch in range(4):
                        lo = nch * 512
                        pp = (4 * mt + nch) % 2
                        ps2, p2r = ps2v[pp], f"lp5W{pp}"
                        for i, hc in enumerate(grp):
                            lh = w2t[i][:, 128 * mt:128 * mt + 128]
                            pb.emit("tensor", lambda h, lh=lh, i=i, lo=lo, st=(i == 0), sp=(i == len(grp) - 1), ps2=ps2: h.matmul(
                                ps2[:], lh, hgrp[i][:, lo:lo + 512], start=st, stop=sp, skip_group_check=True),
                                r=[f"lw2{i}", f"lhg{i}"], w=[p2r])
                        pb.emit("vector", lambda h, mt=mt, lo=lo, ps2=ps2: h.tensor_tensor(
                            out=xi[mt][:, 1 + lo:1 + lo + 512], in0=xi[mt][:, 1 + lo:1 + lo + 512],
                            in1=ps2[:], op=OP.add),
                            r=[p2r, f"xi{mt}"], w=[f"xi{mt}"])
            if layer == 1:
                dbg_dump(cx, dram, "d_lnf", lnf[0][:], ["llnf0"])
            if layer == 0:
                dbg_dump(cx, dram, "d_lnf0", lnf[0][:], ["llnf0"])

            for mt in range(KCH):
                pb.emit("vector", lambda h, mt=mt: h.tensor_scalar(
                    out=xi[mt][:, 1:TP], in0=xi[mt][:, 1:TP], scalar1=b2c[:, mt:mt + 1],
                    scalar2=None, op0=OP.add),
                    r=[f"xi{mt}", "lb2"], w=[f"xi{mt}"])
            pb.barrier()
        pb.barrier()
        if layer == 0:
            dbg_dump(cx, dram, "d_xiL0", xi[0][:, 1:TP], ["xi0"])

    for layer in range(2):
        _do_layer(layer)

    dbg_dump(cx, dram, "d_xi2", xi[0][:, 1:TP], ["xi0"])
    with ExitStack() as ph9:
        rb_sb, ab_sb, breg = emit_ln_full(cx, lambda c: (xi[c], f"xi{c}"), ph9, bf16_src=False, off=1)
        u = [cx.sb("u9", [128, T], BF16, ph9) for c in range(KCH)]
        xs9 = XStream(cx, dram["x_in"], ph9, tag="xl9")
        def store(c, xtile, xreg):
            pb.emit("gpsimd", lambda h, c=c, xtile=xtile: h.dma_start(
                out=dram["x_out"][128 * c:128 * (c + 1), :], in_=xtile[:]),
                r=[xreg], dma=True)
        emit_ln_apply(cx, lambda c: (xi[c], f"xi{c}"), u, lambda c: f"u9{c}", rb_sb, ab_sb,
                      off_src=1, residual_store=(xs9.fetch, store), breg=breg)
        pb.barrier()


def ind2b(cx):
    if not hasattr(cx, "_ind2b"):
        cx._ind2b = cx.sb("lindb", [128, 2], BF16)
        pb = cx.pb
        pb.emit("vector", lambda h: h.memset(cx._ind2b[:, 0:2], 0.0), w=["lindb"])
        pb.emit("vector", lambda h: h.memset(cx._ind2b[0:64, 0:1], 1.0), w=["lindb"])
        pb.emit("vector", lambda h: h.memset(cx._ind2b[64:128, 1:2], 1.0), w=["lindb"])
    return cx._ind2b


def build_l1_ffn_prog(dbg=False):
    nc = bass.Bass("TRN2", target_bir_lowering=False)
    dram = {}
    dram["x_in"] = nc.dram_tensor("x_in", [D, T], F32, kind="ExternalInput").ap()
    dram["wq"] = nc.dram_tensor("wq", [2 * D, 512], F32, kind="ExternalInput").ap()
    dram["wk"] = nc.dram_tensor("wk", [2 * D, 512], F32, kind="ExternalInput").ap()
    dram["wv"] = nc.dram_tensor("wv", [2 * D, 520], F32, kind="ExternalInput").ap()
    dram["wo"] = nc.dram_tensor("wo", [2 * 512, 1024], F32, kind="ExternalInput").ap()
    dram["q_scale"] = nc.dram_tensor("q_scale", [2, 64], F32, kind="ExternalInput").ap()
    dram["k_scale"] = nc.dram_tensor("k_scale", [2, 64], F32, kind="ExternalInput").ap()
    dram["ln_g"] = nc.dram_tensor("ln_g", [2, D], F32, kind="ExternalInput").ap()
    dram["ln_b"] = nc.dram_tensor("ln_b", [2, D], F32, kind="ExternalInput").ap()
    dram["w1a"] = nc.dram_tensor("w1a", [2 * 22 * D, 128], F32, kind="ExternalInput").ap()
    dram["w1g"] = nc.dram_tensor("w1g", [2 * 22 * D, 128], F32, kind="ExternalInput").ap()
    dram["b1a"] = nc.dram_tensor("b1a", [2, 22 * 128], F32, kind="ExternalInput").ap()
    dram["b1g"] = nc.dram_tensor("b1g", [2, 22 * 128], F32, kind="ExternalInput").ap()
    dram["w2p"] = nc.dram_tensor("w2p", [2 * 22 * 128, 1024], F32, kind="ExternalInput").ap()
    dram["b2"] = nc.dram_tensor("b2", [2, D], F32, kind="ExternalInput").ap()
    for nm in ["SQ", "TQ", "SK", "TK"]:
        dram["tab_" + nm] = nc.dram_tensor("tab_" + nm, [128, T], F32, kind="ExternalInput").ap()
    dram["mle"] = nc.dram_tensor("mle", [128, 128], BF16, kind="ExternalInput").ap()
    dram["mge"] = nc.dram_tensor("mge", [128, 128], BF16, kind="ExternalInput").ap()
    dram["ident_f"] = nc.dram_tensor("ident_f", [128, 128], F32, kind="ExternalInput").ap()
    dram["x_out"] = nc.dram_tensor("x_out", [D, T], F32, kind="ExternalOutput").ap()
    if dbg:
        dram["d_xi"] = nc.dram_tensor("d_xi", [128, T], F32, kind="ExternalOutput").ap()
        dram["d_q0"] = nc.dram_tensor("d_q0", [128, T], BF16, kind="ExternalOutput").ap()
        dram["d_k0"] = nc.dram_tensor("d_k0", [128, T], BF16, kind="ExternalOutput").ap()
        dram["d_v0"] = nc.dram_tensor("d_v0", [128, 520], BF16, kind="ExternalOutput").ap()
        dram["d_nb"] = nc.dram_tensor("d_nb", [1, 128], F32, kind="ExternalOutput").ap()
        dram["d_o"] = nc.dram_tensor("d_o", [128, T], BF16, kind="ExternalOutput").ap()
        dram["d_x1"] = nc.dram_tensor("d_x1", [128, T], F32, kind="ExternalOutput").ap()
        dram["d_lnf"] = nc.dram_tensor("d_lnf", [128, T], BF16, kind="ExternalOutput").ap()
        dram["d_lnf0"] = nc.dram_tensor("d_lnf0", [128, T], BF16, kind="ExternalOutput").ap()
        dram["d_xi2"] = nc.dram_tensor("d_xi2", [128, T], F32, kind="ExternalOutput").ap()
        dram["d_xiL0"] = nc.dram_tensor("d_xiL0", [128, T], F32, kind="ExternalOutput").ap()

    with ExitStack() as st:
        sems = {e: st.enter_context(nc.semaphore(f"sem_{e}")) for e in ENG}
        pb = PB(nc)
        cx = Ctx(nc, pb, st)
        make_consts(cx)
        blk = st.enter_context(nc.Block())
        emit_l1_ffn(cx, dram)
        pb.realize(blk, sems)
    return nc


def prep_l1_ffn_weights(inp):
    out = {}
    wqkv = np.asarray(inp["L1_ffn_wqkv"])
    out["wq"] = np.ascontiguousarray(wqkv[:, :, 0:512]).reshape(2 * D, 512)
    out["wk"] = np.ascontiguousarray(wqkv[:, :, 512:1024]).reshape(2 * D, 512)
    wv = wqkv[:, :, 1024:1536]
    wva = np.zeros((2, D, 520), np.float32)
    for hh in range(8):
        wva[:, :, 65 * hh:65 * hh + 64] = wv[:, :, 64 * hh:64 * hh + 64]
    out["wv"] = wva.reshape(2 * D, 520)
    out["wo"] = np.ascontiguousarray(np.asarray(inp["L1_ffn_wo"])).reshape(2 * 512, 1024)
    out["q_scale"] = np.asarray(inp["L1_ffn_q_scale"])
    out["k_scale"] = np.asarray(inp["L1_ffn_k_scale"])
    out["ln_g"] = np.asarray(inp["L1_ffn_ln_g"])
    out["ln_b"] = np.asarray(inp["L1_ffn_ln_b"])
    w1 = np.asarray(inp["L1_ffn_w1"])
    b1 = np.asarray(inp["L1_ffn_b1"])
    w2 = np.asarray(inp["L1_ffn_w2"])
    b2 = np.asarray(inp["L1_ffn_b2"])
    HID = 2730
    w1a = np.zeros((2, D, 2816), np.float32)
    w1g = np.zeros((2, D, 2816), np.float32)
    b1a = np.zeros((2, 2816), np.float32)
    b1g = np.zeros((2, 2816), np.float32)
    w2p = np.zeros((2, 2816, D), np.float32)
    w1a[:, :, 0:HID] = w1[:, :, 0:HID]
    w1g[:, :, 0:HID] = w1[:, :, HID:2 * HID]
    b1a[:, 0:HID] = b1[:, 0:HID]
    b1g[:, 0:HID] = b1[:, HID:2 * HID]
    w2p[:, 0:HID, :] = w2
    w1a_dev = np.zeros((2 * 22 * D, 128), np.float32)
    w1g_dev = np.zeros((2 * 22 * D, 128), np.float32)
    w2_dev = np.zeros((2 * 22 * 128, D), np.float32)
    for l in range(2):
        for hc in range(22):
            w1a_dev[(22 * l + hc) * D:(22 * l + hc + 1) * D, :] = w1a[l][:, 128 * hc:128 * (hc + 1)]
            w1g_dev[(22 * l + hc) * D:(22 * l + hc + 1) * D, :] = w1g[l][:, 128 * hc:128 * (hc + 1)]
            w2_dev[(22 * l + hc) * 128:(22 * l + hc + 1) * 128, :] = w2p[l][128 * hc:128 * (hc + 1), :]
    out["w1a"], out["w1g"], out["w2p"] = w1a_dev, w1g_dev, w2_dev
    out["b1a"] = b1a.reshape(2, 2816)
    out["b1g"] = b1g.reshape(2, 2816)
    out["b2"] = b2
    DIM_HEAD, WINDOW = 64, 512
    half = np.arange(0, DIM_HEAD, 2, dtype=np.float32)
    inv = (1.0 / 10000.0 ** (half / DIM_HEAD)).astype(np.float32)
    t = np.arange(T, dtype=np.float32)
    freqs = t[:, None] * inv[None, :]
    freqs = np.concatenate([freqs, freqs], -1)
    base = ((half + 0.4 * DIM_HEAD) / (1.4 * DIM_HEAD)).astype(np.float32)
    power = (t - T // 2) / float(WINDOW // 2)
    scale = base[None, :] ** power[:, None]
    scale = np.concatenate([scale, scale], -1).astype(np.float32)
    cos, sin = np.cos(freqs), np.sin(freqs)
    sgn = np.where(np.arange(DIM_HEAD) < 32, -1.0, 1.0).astype(np.float32)
    SQ = (cos * scale).T
    TQ = (sin * scale * sgn[None, :]).T
    SK = (cos / scale).T
    TK = (sin / scale * sgn[None, :]).T
    for nm, tab in [("SQ", SQ), ("TQ", TQ), ("SK", SK), ("TK", TK)]:
        out["tab_" + nm] = np.ascontiguousarray(np.tile(tab, (2, 1))).astype(np.float32)
    i = np.arange(128)[:, None]
    j = np.arange(128)[None, :]
    out["mle"] = (j <= i).astype(np.float32).astype(BF)
    out["mge"] = (j >= i).astype(np.float32).astype(BF)
    out["ident_f"] = np.eye(128, dtype=np.float32)
    return out




def _decl_l0_attn(nc, p):
    dram = {}
    for nm, sh in [("Wq", [D, 512]), ("Wqc", [D, 512]), ("Wk", [D, 512]), ("Wkc", [D, 512]),
                   ("Wv", [D, 1040]), ("Wvc", [D, 1040])]:
        dram[nm] = nc.dram_tensor(p + nm, sh, F32, kind="ExternalInput").ap()
    for nm, n in [("bq", 512), ("bqc", 512), ("bk", 512), ("bkc", 512), ("bv", 1040), ("bvc", 1040)]:
        dram[nm] = nc.dram_tensor(p + nm, [1, n], F32, kind="ExternalInput").ap()
    return dram


def _decl_l0_ffn(nc, p):
    dram = {}
    dram["router_wT"] = nc.dram_tensor(p + "router_wT", [D, 32], F32, kind="ExternalInput").ap()
    dram["router_bias"] = nc.dram_tensor(p + "router_bias", [1, 32], F32, kind="ExternalInput").ap()
    dram["gate_wT"] = nc.dram_tensor(p + "gate_wT", [32 * D, 128], F32, kind="ExternalInput").ap()
    dram["up_wT"] = nc.dram_tensor(p + "up_wT", [32 * D, 128], F32, kind="ExternalInput").ap()
    dram["down_wT"] = nc.dram_tensor(p + "down_wT", [32 * 128, 1024], F32, kind="ExternalInput").ap()
    dram["sgate_wT"] = nc.dram_tensor(p + "sgate_wT", [D, 512], F32, kind="ExternalInput").ap()
    dram["sup_wT"] = nc.dram_tensor(p + "sup_wT", [D, 512], F32, kind="ExternalInput").ap()
    dram["sdown_wT"] = nc.dram_tensor(p + "sdown_wT", [512, 1024], F32, kind="ExternalInput").ap()
    dram["ident_f"] = nc.dram_tensor(p + "ident_f", [128, 128], F32, kind="ExternalInput").ap()
    return dram


def _decl_l1_attn(nc, p):
    dram = {}
    for nm, sh in [("Wq", [D, D]), ("Wk", [D, D]), ("Wv", [D, 1040]), ("Wo", [D, D])]:
        dram[nm] = nc.dram_tensor(p + nm, sh, F32, kind="ExternalInput").ap()
    dram["cmask"] = nc.dram_tensor(p + "cmask", [4 * 128, 512], BF16, kind="ExternalInput").ap()
    return dram


def _decl_l1_ffn(nc, p):
    dram = {}
    dram["wq"] = nc.dram_tensor(p + "wq", [2 * D, 512], F32, kind="ExternalInput").ap()
    dram["wk"] = nc.dram_tensor(p + "wk", [2 * D, 512], F32, kind="ExternalInput").ap()
    dram["wv"] = nc.dram_tensor(p + "wv", [2 * D, 520], F32, kind="ExternalInput").ap()
    dram["wo"] = nc.dram_tensor(p + "wo", [2 * 512, 1024], F32, kind="ExternalInput").ap()
    dram["q_scale"] = nc.dram_tensor(p + "q_scale", [2, 64], F32, kind="ExternalInput").ap()
    dram["k_scale"] = nc.dram_tensor(p + "k_scale", [2, 64], F32, kind="ExternalInput").ap()
    dram["ln_g"] = nc.dram_tensor(p + "ln_g", [2, D], F32, kind="ExternalInput").ap()
    dram["ln_b"] = nc.dram_tensor(p + "ln_b", [2, D], F32, kind="ExternalInput").ap()
    dram["w1a"] = nc.dram_tensor(p + "w1a", [2 * 22 * D, 128], F32, kind="ExternalInput").ap()
    dram["w1g"] = nc.dram_tensor(p + "w1g", [2 * 22 * D, 128], F32, kind="ExternalInput").ap()
    dram["b1a"] = nc.dram_tensor(p + "b1a", [2, 22 * 128], F32, kind="ExternalInput").ap()
    dram["b1g"] = nc.dram_tensor(p + "b1g", [2, 22 * 128], F32, kind="ExternalInput").ap()
    dram["w2p"] = nc.dram_tensor(p + "w2p", [2 * 22 * 128, 1024], F32, kind="ExternalInput").ap()
    dram["b2"] = nc.dram_tensor(p + "b2", [2, D], F32, kind="ExternalInput").ap()
    for nm in ["SQ", "TQ", "SK", "TK"]:
        dram["tab_" + nm] = nc.dram_tensor(p + "tab_" + nm, [128, T], F32, kind="ExternalInput").ap()
    dram["mle"] = nc.dram_tensor(p + "mle", [128, 128], BF16, kind="ExternalInput").ap()
    dram["mge"] = nc.dram_tensor(p + "mge", [128, 128], BF16, kind="ExternalInput").ap()
    dram["ident_f"] = nc.dram_tensor(p + "ident_f2", [128, 128], F32, kind="ExternalInput").ap()
    return dram


def build_fused_prog():
    nc = bass.Bass("TRN2", target_bir_lowering=False)
    x_in = nc.dram_tensor("x_in", [D, T], F32, kind="ExternalInput").ap()
    x_out = nc.dram_tensor("x_out", [D, T], F32, kind="ExternalOutput").ap()
    xs0 = nc.dram_tensor("xsc0", [D, T], F32, kind="Internal").ap()
    xs1 = nc.dram_tensor("xsc1", [D, T], F32, kind="Internal").ap()
    d0 = _decl_l0_attn(nc, "a_"); d0["x_in"] = x_in; d0["x_out"] = xs0
    d1 = _decl_l0_ffn(nc, "f_"); d1["x_in"] = xs0; d1["x_out"] = xs1
    d2 = _decl_l1_attn(nc, "m_"); d2["x_in"] = xs1; d2["x_out"] = xs0
    d3 = _decl_l1_ffn(nc, "l_"); d3["x_in"] = xs0; d3["x_out"] = x_out

    with ExitStack() as st:
        sems = {e: st.enter_context(nc.semaphore(f"sem_{e}")) for e in ENG}
        pb = PB(nc)
        cx = Ctx(nc, pb, st)
        make_consts(cx)
        blk = st.enter_context(nc.Block())
        for emit_fn, dd in [(emit_l0_attn, d0), (emit_l0_ffn, d1),
                            (emit_l1_attn, d2), (emit_l1_ffn, d3)]:
            with ExitStack() as bs:
                old_st = cx.st
                cx.st = bs
                emit_fn(cx, dd)
                pb.barrier()
                cx.st = old_st
        pb.realize(blk, sems)
    return nc


_PROG_CACHE = {}


def _get_fused():
    if "p" not in _PROG_CACHE:
        _PROG_CACHE["p"] = build_fused_prog()
    return _PROG_CACHE["p"]


def kernel(**inputs):
    inputs = {k: np.asarray(v) for k, v in inputs.items()}
    wmap = {}
    for p, wm in [("a_", prep_l0_attn_weights(inputs)), ("f_", prep_l0_ffn_weights(inputs)),
                  ("m_", prep_l1_attn_weights(inputs)), ("l_", prep_l1_ffn_weights(inputs))]:
        for k, v in wm.items():
            if p == "l_" and k == "ident_f":
                wmap[p + "ident_f2"] = v
            else:
                wmap[p + k] = v
    x = np.asarray(inputs["L0_attn_x"], np.float32)
    nc = _get_fused()
    in_maps = [dict(wmap, x_in=np.ascontiguousarray(x[i].T)) for i in range(8)]
    res = run_bass_kernel_spmd(nc, in_maps, core_ids=list(range(8)))
    cur = [np.asarray(res.results[i]["x_out"]) for i in range(8)]
    return np.ascontiguousarray(np.stack([c.T for c in cur], 0))
